# version 28
# baseline (speedup 1.0000x reference)
import numpy as np
from contextlib import ExitStack

import concourse.bass as bass
import concourse.mybir as mybir
from concourse.bass_utils import run_bass_kernel_spmd

F32 = mybir.dt.float32
BF16 = mybir.dt.bfloat16
AF = mybir.ActivationFunctionType
ALU = mybir.AluOpType
AX = mybir.AxisListType

D = 1024
T = 2048
CH = 512
NCH = T // CH
KC = D // 128
E = 4096
HEADS = 4
EPS = 1e-5
CLAMP = 1e6
MARGIN = 1.0

ENGS = ["tensor", "vector", "scalar", "gpsimd", "sync"]
ESEM = {"tensor": "pe", "vector": "dve", "scalar": "act", "gpsimd": "pool", "sync": "sp"}


class Prog:

    def __init__(self, nc, ctx):
        self.nc = nc
        self.ctx = ctx
        self.block = ctx.enter_context(nc.Block())
        self.sems = {}
        for s in ["pe", "dve", "act", "pool", "sp"]:
            self.sems[s] = ctx.enter_context(nc.semaphore(s))
        self.cnt = {s: 0 for s in self.sems}
        self.last_w = {}
        self.readers = {}
        self.waited = {e: {s: 0 for s in self.sems} for e in ENGS}
        self.pending = {e: [] for e in ENGS}

    def dma_sem(self, stream):
        name = "dq_" + stream
        if name not in self.sems:
            self.sems[name] = self.ctx.enter_context(self.nc.semaphore(name))
            self.cnt[name] = 0
            for e in ENGS:
                self.waited[e][name] = 0
        return name

    def flush(self, engs=None):
        for e in (ENGS if engs is None else engs):
            items = self.pending[e]
            if not items:
                continue
            self.pending[e] = []

            def section(handle, items=items):
                for waits, build, sem, k, drain, new_cnt in items:
                    for s_, c_ in waits:
                        handle.wait_ge(self.sems[s_], c_)
                    ins = build(handle)
                    assert ins is not None
                    if isinstance(ins, list):
                        assert 16 * len(ins) == k
                        for i_ in ins:
                            i_.then_inc(self.sems[sem], 16)
                    else:
                        ins.then_inc(self.sems[sem], k)
                    if drain:
                        handle.wait_ge(self.sems[sem], new_cnt)
            getattr(self.block, e)(section)

    def _need(self, eng, reads, writes):
        need = {}

        def add(tok):
            if tok is None:
                return
            s, c, weng = tok
            if weng == eng and not s.startswith("dq_"):
                return
            need[s] = max(need.get(s, 0), c)

        for r in reads:
            add(self.last_w.get(r))
        for w in writes:
            add(self.last_w.get(w))
            for s, c in self.readers.get(w, {}).items():
                need[s] = max(need.get(s, 0), c)
        need.pop(ESEM[eng], None)
        out = []
        for s, c in sorted(need.items()):
            if c > self.waited[eng][s]:
                out.append((s, c))
                self.waited[eng][s] = c
        return out

    def op(self, eng, build, reads=(), writes=(), dma=False, drain=False, stream=None,
           pre_drain=False):
        if dma:
            sem = self.dma_sem(stream or "misc")
        else:
            sem = ESEM[eng]
        waits = self._need(eng, reads, writes)
        if (pre_drain or eng in ("vector", "scalar", "gpsimd")) and not dma \
                and self.cnt[ESEM[eng]] > self.waited[eng][ESEM[eng]]:
            waits.insert(0, (ESEM[eng], self.cnt[ESEM[eng]]))
            self.waited[eng][ESEM[eng]] = self.cnt[ESEM[eng]]
        if dma:
            k = 16 * (build.n_dma if hasattr(build, "n_dma") else 1)
        else:
            k = 1
        new_cnt = self.cnt[sem] + k
        self.pending[eng].append((waits, build, sem, k, drain, new_cnt))
        self.cnt[sem] = new_cnt
        if drain:
            self.waited[eng][sem] = max(self.waited[eng][sem], new_cnt)
        tok = (sem, new_cnt, eng)
        for w in writes:
            self.last_w[w] = tok
            self.readers[w] = {}
        for r in reads:
            d = self.readers.setdefault(r, {})
            d[sem] = max(d.get(sem, 0), new_cnt)
        return tok

    def barrier(self):
        self.flush()
        snap = dict(self.cnt)
        for e in ENGS:
            todo = [(s, c) for s, c in sorted(snap.items()) if c > self.waited[e][s]]

            def f(eng, todo=todo):
                for s, c in todo:
                    eng.wait_ge(self.sems[s], c)
            getattr(self.block, e)(f)
            for s, c in snap.items():
                self.waited[e][s] = max(self.waited[e][s], c)


def build_nc(debug=False):
    nc = bass.Bass("TRN2", target_bir_lowering=False, dynamic_dma_scratch_size=16384)
    dt = lambda n, s, d_, k: nc.dram_tensor(n, s, d_, kind=k).ap()

    u_own = dt("u_own", [D, T], F32, "ExternalInput")
    u_pre = dt("u_pre", [D, T], F32, "ExternalInput")
    selv = dt("selv", [128, 1], F32, "ExternalInput")
    h0v2 = dt("h0v2", [128, KC], F32, "ExternalInput")
    afmv = dt("afmv", [128, KC], F32, "ExternalInput")
    h0fm = dt("h0fm", [128, KC], F32, "ExternalInput")
    gfm = dt("gfm", [128, KC], F32, "ExternalInput")
    bfm = dt("bfm", [128, KC], F32, "ExternalInput")
    bmt = dt("bmt", [D, D], F32, "ExternalInput")
    cmt = dt("cmt", [D, D], F32, "ExternalInput")
    dmt = dt("dmt", [D, D], F32, "ExternalInput")
    wqt = dt("wqt", [D, 512], F32, "ExternalInput")
    keyspad = dt("keyspad", [128, 512], F32, "ExternalInput")
    det = dt("det", [D, E], F32, "ExternalInput")
    uem = dt("uem", [E, D], F32, "ExternalInput")
    wupt = dt("wupt", [D, E], F32, "ExternalInput")
    wdnt = dt("wdnt", [E, D], F32, "ExternalInput")
    identm = dt("identm", [128, 128], F32, "ExternalInput")
    out_h = dt("out", [D, T], F32, "ExternalOutput")

    det_b = dt("det_b", [D, E], BF16, "Internal")
    uem_b = dt("uem_b", [E, D], BF16, "Internal")
    wupt_b = dt("wupt_b", [D, E], BF16, "Internal")
    wdnt_b = dt("wdnt_b", [E, D], BF16, "Internal")
    x1_h = dt("x1_h", [D, T], BF16, "ExternalOutput" if debug else "Internal")
    if debug:
        dbg_vec = dt("dbg_vec", [128, 6 * KC], F32, "ExternalOutput")
        dbg_bmt = dt("dbg_bmt", [128, KC * D], BF16, "ExternalOutput")
        dbg_uop = dt("dbg_uop", [128, KC * CH], BF16, "ExternalOutput")
        dbg_xs = dt("dbg_xs", [128, KC * CH], BF16, "ExternalOutput")
        dbg_act = dt("dbg_act", [128, KC * CH], F32, "ExternalOutput")
        dbg_carry = dt("dbg_carry", [128, KC], F32, "ExternalOutput")
        dbg_bu = dt("dbg_bu", [128, CH], F32, "ExternalOutput")
        dbg_cc = dt("dbg_cc", [128, 4 * KC], F32, "ExternalOutput")
        dbg_cc2 = dt("dbg_cc2", [128, KC * KC], F32, "ExternalOutput")
        dbg_sc1 = dt("dbg_sc1", [128, CH], BF16, "ExternalOutput")

    kview = lambda ap: ap.rearrange("(k p) t -> p k t", p=128)

    ctx = ExitStack()
    sb = lambda n, s, d_: ctx.enter_context(nc.sbuf_tensor(n, s, d_))
    P = Prog(nc, ctx)

    vec = sb("vec", [128, 6 * KC], F32)
    a_fm = vec[:, 0:KC]
    h0_fm = vec[:, KC:2 * KC]
    g_fm = vec[:, 2 * KC:3 * KC]
    b_fm = vec[:, 3 * KC:4 * KC]
    h02_fm = vec[:, 4 * KC:5 * KC]
    spa = vec[:, 5 * KC:6 * KC]
    selv_sb = sb("selv_sb", [128, 1], F32)
    ones_b = sb("ones_b", [128, 1], BF16)
    ones_p = sb("ones_p", [128, 1], F32)
    ones_r = sb("ones_r", [1, 128], F32)
    id_b = sb("id_b", [128, 128], BF16)
    carry = sb("carry", [128, KC], F32)
    init2 = sb("init2", [128, KC], F32)
    keys_sb = sb("keys_sb", [128, 512], BF16)

    def f(sy):
        return [sy.dma_start(vec[:, 0:KC], afmv),
                sy.dma_start(vec[:, KC:2 * KC], h0fm),
                sy.dma_start(vec[:, 2 * KC:3 * KC], gfm),
                sy.dma_start(vec[:, 3 * KC:4 * KC], bfm),
                sy.dma_start(vec[:, 4 * KC:5 * KC], h0v2),
                sy.dma_start(selv_sb[:], selv)]
    f.n_dma = 6
    P.op("sync", f, writes=["vec", "selv"], dma=True, stream="init")

    def f(g):
        return [g.dma_start(id_b[:], identm),
                g.dma_start(keys_sb[:], keyspad)]
    f.n_dma = 2
    P.op("gpsimd", f, writes=["idb", "keys"], dma=True, stream="consts")

    def f(v):
        v.memset(ones_b[:], 1.0)
        v.memset(ones_p[:], 1.0)
        return v.memset(ones_r[:], 1.0)
    P.op("vector", f, writes=["ones"])

    P.op("scalar", lambda sc: sc.activation(spa, vec[:, 0:KC], AF.Exp),
         reads=["vec"], writes=["spa"])
    P.op("vector", lambda v: v.tensor_scalar(spa, spa, 1.0, None, ALU.add),
         reads=["spa"], writes=["spa"])
    P.op("scalar", lambda sc: sc.activation(spa, spa, AF.Ln),
         reads=["spa"], writes=["spa"])
    P.op("vector", lambda v: v.tensor_scalar(a_fm, spa, -1.0, None, ALU.mult),
         reads=["spa"], writes=["vec_a"])
    if debug:
        P.op("sync", lambda sy: sy.dma_start(dbg_vec, vec[:]), reads=["vec", "vec_a"], dma=True)

    with ExitStack() as actx:
        asb = lambda n, s, d_: actx.enter_context(nc.sbuf_tensor(n, s, d_))
        aps = lambda n, s, d_: actx.enter_context(nc.psum_tensor(n, s, d_))
        bmt_s = asb("bmt_s", [128, KC * D], BF16)
        cmt_s = asb("cmt_s", [128, KC * D], BF16)
        dmt_s = asb("dmt_s", [128, KC * D], BF16)
        uop_s = asb("uop_s", [128, 2 * KC * CH], BF16)
        ufc_s = asb("ufc_s", [128, 2 * KC * CH], F32)
        xs_s = asb("xs_s", [128, 2 * KC * CH], BF16)
        scr1 = asb("scr1", [128, CH], BF16)
        asq8 = asb("asq8", [128, KC * CH], BF16)
        zf = asb("zf", [128, KC * CH], F32)
        zq8 = asb("zq8", [128, KC * CH], BF16)
        x1c = asb("x1c", [128, KC * CH], BF16)
        tmp = asb("tmp", [128, 2 * CH], F32)
        tm8 = asb("tm8", [128, KC * CH], F32)
        stat = asb("stat", [1, 6 * CH], F32)
        pbu = [aps("pbu%d" % i, [128, CH], F32) for i in range(2)]
        py = [aps("py%d" % i, [128, CH], F32) for i in range(2)]
        pb4 = [aps("pb4_%d" % i, [128, CH], F32) for i in range(4)]

        wtile = lambda buf, kc, j: buf[:, kc * D + j * 128: kc * D + (j + 1) * 128]

        def f(g):
            return [g.dma_start(bmt_s[:].rearrange("p (k c) -> p k c", k=KC), kview(bmt)),
                    g.dma_start(cmt_s[:].rearrange("p (k c) -> p k c", k=KC), kview(cmt)),
                    g.dma_start(dmt_s[:].rearrange("p (k c) -> p k c", k=KC), kview(dmt))]
        f.n_dma = 3
        P.op("gpsimd", f, writes=["bmt", "cmt", "dmt"], dma=True, stream="bcd")
        if debug:
            P.op("sync", lambda sy: sy.dma_start(dbg_bmt, bmt_s[:]), reads=["bmt"], dma=True)

        def ln_smalls(st, extra_reads):
            o = st * 3 * CH
            b0, b1 = pb4[2 * st], pb4[2 * st + 1]
            sres = "stat%d" % st
            r0, r1 = "pbc%d0" % st, "pbc%d1" % st

            def f(v):
                v.tensor_scalar(stat[:, o:o + CH], b0[0:1, :], 1.0 / D, None, ALU.mult)
                return v.tensor_scalar(stat[:, o + CH:o + 2 * CH], b1[0:1, :], 1.0 / D, None, ALU.mult)
            P.op("vector", f, reads=list(extra_reads) + [sres, r0, r1], writes=[sres])
            P.op("vector", lambda v: v.tensor_tensor(stat[:, o + 2 * CH:o + 3 * CH],
                                                     stat[:, o:o + CH], stat[:, o:o + CH], ALU.mult),
                 reads=[sres], writes=[sres])
            P.op("vector", lambda v: v.scalar_tensor_tensor(stat[:, o + CH:o + 2 * CH],
                                                            stat[:, o + CH:o + 2 * CH], EPS,
                                                            stat[:, o + 2 * CH:o + 3 * CH],
                                                            ALU.add, ALU.subtract),
                 reads=[sres], writes=[sres])
            P.op("vector", lambda v: v.reciprocal(stat[:, o + 2 * CH:o + 3 * CH],
                                                  stat[:, o + CH:o + 2 * CH]),
                 reads=[sres], writes=[sres])
            P.op("scalar", lambda sc: sc.activation(stat[:, o + CH:o + 2 * CH],
                                                    stat[:, o + 2 * CH:o + 3 * CH], AF.Sqrt),
                 reads=[sres], writes=[sres])
            P.op("vector", lambda v: v.tensor_tensor(stat[:, o:o + CH], stat[:, o:o + CH],
                                                     stat[:, o + CH:o + 2 * CH], ALU.mult),
                 reads=[sres], writes=[sres])

            def f(t):
                t.matmul(b0[:], ones_r[:], stat[:, o + CH:o + 2 * CH], start=True, stop=True)
                return t.matmul(b1[:], ones_r[:], stat[:, o:o + CH], start=True, stop=True)
            P.op("tensor", f, reads=[sres, "ones", r0, r1], writes=[r0, r1])

        for c in range(NCH):
            ub1 = c % 2
            uf1 = "ufc%d" % ub1
            up1 = "uop%d" % ub1

            def f(sy, c=c, ub1=ub1):
                return sy.dma_start(
                    ufc_s[:, ub1 * KC * CH:(ub1 + 1) * KC * CH].rearrange("p (k t) -> p k t", k=KC),
                    kview(u_pre[:, c * CH:(c + 1) * CH]))
            P.op("sync", f, reads=[uf1], writes=[uf1], dma=True, stream=uf1)
            def f(sc, ub1=ub1):
                r = None
                for kc in range(KC):
                    r = sc.activation(uop_s[:, ub1 * KC * CH + kc * CH: ub1 * KC * CH + (kc + 1) * CH],
                                      ufc_s[:, ub1 * KC * CH + kc * CH: ub1 * KC * CH + (kc + 1) * CH],
                                      AF.Copy)
                return r
            P.op("scalar", f, reads=[uf1, up1], writes=[up1])
            for ncn in range(KC):
                bank = [pbu[0], pbu[1], py[0], py[1]][ncn % 4]
                pb = ["pbu0", "pbu1", "py0", "py1"][ncn % 4]

                def f(t, ncn=ncn, bank=bank, ub1=ub1):
                    r = None
                    for kc in range(KC):
                        r = t.matmul(bank[:], wtile(bmt_s, kc, ncn),
                                     uop_s[:, ub1 * KC * CH + kc * CH: ub1 * KC * CH + (kc + 1) * CH],
                                     start=(kc == 0), stop=(kc == KC - 1))
                    return r
                P.op("tensor", f, reads=["bmt", up1], writes=[pb])

                if debug and c == 2 and ncn == 0:
                    def fd(v):
                        return v.tensor_copy(zf[:, 0:CH], pbu[0][:])
                    P.op("vector", fd, reads=["pbu0"], writes=["zf0"])
                    P.op("sync", lambda sy: sy.dma_start(dbg_bu, zf[:, 0:CH]), reads=["zf0"], dma=True)
                def f(v, c=c, ncn=ncn, bank=bank):
                    initial = h0_fm[:, ncn:ncn + 1] if c == 0 else carry[:, ncn:ncn + 1]
                    return v.tensor_tensor_scan(scr1[:], a_fm[:, ncn:ncn + 1].to_broadcast([128, CH]),
                                                bank[:], initial, ALU.mult, ALU.add)
                P.op("vector", f, reads=[pb, "vec_a", "vec", "carry", "scr1"],
                     writes=[pb, "scr1"], pre_drain=True)
                def f(v, ncn=ncn):
                    return v.tensor_copy(carry[:, ncn:ncn + 1], scr1[:, CH - 1:CH])
                P.op("vector", f, reads=["scr1", "carry"], writes=["carry"])
                if debug and c == 2 and ncn == 0:
                    P.op("sync", lambda sy: sy.dma_start(dbg_sc1, scr1[:]), reads=["scr1"], dma=True)
                if debug and ncn == KC - 1:
                    def fd(sy, c=c):
                        return sy.dma_start(dbg_cc[:, c * KC:(c + 1) * KC], carry[:])
                    P.op("sync", fd, reads=["carry"], dma=True)
                if debug and c == 2:
                    def fd(sy, ncn=ncn):
                        return sy.dma_start(dbg_cc2[:, ncn * KC:(ncn + 1) * KC], carry[:])
                    P.op("sync", fd, reads=["carry"], dma=True)

        def f(v):
            r = None
            for ncn in range(KC):
                r = v.scalar_tensor_tensor(init2[:, ncn:ncn + 1], carry[:, ncn:ncn + 1],
                                           selv_sb[:, 0:1], h02_fm[:, ncn:ncn + 1], ALU.mult, ALU.add)
            return r
        if debug:
            P.op("sync", lambda sy: sy.dma_start(dbg_carry, carry[:]), reads=["carry"], dma=True)
        P.op("vector", f, reads=["carry", "selv", "vec"], writes=["init2"])

        def emit_load_cast_scan(c):
            ub = c % 2
            uo = "uop%d" % ub
            uf = "ufc%d" % ub

            def f(sy, c=c, ub=ub):
                return sy.dma_start(
                    ufc_s[:, ub * KC * CH:(ub + 1) * KC * CH].rearrange("p (k t) -> p k t", k=KC),
                    kview(u_own[:, c * CH:(c + 1) * CH]))
            P.op("sync", f, reads=[uf], writes=[uf], dma=True, stream=uf)

            def f(sc, ub=ub):
                r = None
                for kc in range(KC):
                    r = sc.activation(uop_s[:, ub * KC * CH + kc * CH: ub * KC * CH + (kc + 1) * CH],
                                      ufc_s[:, ub * KC * CH + kc * CH: ub * KC * CH + (kc + 1) * CH],
                                      AF.Copy)
                return r
            P.op("scalar", f, reads=[uf, uo], writes=[uo])

            for ncn in range(KC):
                bank = pbu[ncn % 2]
                pb = "pbu%d" % (ncn % 2)

                def f(t, ncn=ncn, ub=ub, bank=bank):
                    r = None
                    for kc in range(KC):
                        r = t.matmul(bank[:], wtile(bmt_s, kc, ncn),
                                     uop_s[:, ub * KC * CH + kc * CH: ub * KC * CH + (kc + 1) * CH],
                                     start=(kc == 0), stop=(kc == KC - 1))
                    return r
                P.op("tensor", f, reads=["bmt", uo, "init2"], writes=[pb])

                def f(v, c=c, ncn=ncn, ub=ub, bank=bank):
                    initial = init2[:, ncn:ncn + 1] if c == 0 else carry[:, ncn:ncn + 1]
                    dst = xs_s[:, ub * KC * CH + ncn * CH: ub * KC * CH + (ncn + 1) * CH]
                    return v.tensor_tensor_scan(dst, a_fm[:, ncn:ncn + 1].to_broadcast([128, CH]),
                                                bank[:], initial, ALU.mult, ALU.add)
                P.op("vector", f, reads=[pb, "vec_a", "init2", "carry", "xs%d_%d" % (ub, ncn)],
                     writes=[pb, "xs%d_%d" % (ub, ncn)], pre_drain=True)
                def f(v, ncn=ncn, ub=ub):
                    dst = xs_s[:, ub * KC * CH + ncn * CH: ub * KC * CH + (ncn + 1) * CH]
                    return v.tensor_copy(carry[:, ncn:ncn + 1], dst[:, CH - 1:CH])
                P.op("vector", f, reads=["xs%d_%d" % (ub, ncn), "carry"], writes=["carry"])

        def emit_y(c):
            ub = c % 2
            uo = "uop%d" % ub
            for oc in range(KC):
                pyb = "py%d" % (oc % 2)

                def f(t, oc=oc, ub=ub):
                    r = None
                    for ncn in range(KC):
                        r = t.matmul(py[oc % 2][:], wtile(cmt_s, ncn, oc),
                                     xs_s[:, ub * KC * CH + ncn * CH: ub * KC * CH + (ncn + 1) * CH],
                                     start=(ncn == 0), stop=False)
                    for kc in range(KC):
                        r = t.matmul(py[oc % 2][:], wtile(dmt_s, kc, oc),
                                     uop_s[:, ub * KC * CH + kc * CH: ub * KC * CH + (kc + 1) * CH],
                                     start=False, stop=(kc == KC - 1))
                    return r
                P.op("tensor", f,
                     reads=["cmt", "dmt", uo] + ["xs%d_%d" % (ub, n) for n in range(KC)],
                     writes=[pyb])

                def f(sc, oc=oc):
                    return sc.activation(zf[:, oc * CH:(oc + 1) * CH], py[oc % 2][:], AF.Gelu)
                P.op("scalar", f, reads=[pyb, "zf%d" % oc], writes=[pyb, "zf%d" % oc])
                def f(sc, oc=oc):
                    return sc.activation(asq8[:, oc * CH:(oc + 1) * CH],
                                         zf[:, oc * CH:(oc + 1) * CH], AF.Square)
                P.op("scalar", f, reads=["zf%d" % oc, "asq"], writes=["asq"])

        def emit_epilogue(c):
            ub = c % 2
            st = c % 2
            b0, b1 = pb4[2 * st], pb4[2 * st + 1]
            r0, r1 = "pbc%d0" % st, "pbc%d1" % st
            def f(t):
                for oc in range(KC):
                    t.matmul(b0[0:1, :], ones_p[:], zf[:, oc * CH:(oc + 1) * CH],
                             start=(oc == 0), stop=(oc == KC - 1))
                r = None
                for oc in range(KC):
                    r = t.matmul(b1[0:1, :], ones_b[:], asq8[:, oc * CH:(oc + 1) * CH],
                                 start=(oc == 0), stop=(oc == KC - 1))
                return r
            P.op("tensor", f, reads=["ones"] + ["zf%d" % o for o in range(KC)] + ["asq", r0, r1],
                 writes=[r0, r1])
            ln_smalls(st, [])

            zf_all = ["zf%d" % o for o in range(KC)]
            def f(v):
                r = None
                for oc in range(KC):
                    r = v.tensor_tensor(tm8[:, oc * CH:(oc + 1) * CH],
                                        zf[:, oc * CH:(oc + 1) * CH], b0[:], ALU.mult)
                return r
            P.op("vector", f, reads=zf_all + [r0, r1, "tm8"], writes=["tm8"])
            def f(v):
                r = None
                for oc in range(KC):
                    dst = tm8[:, oc * CH:(oc + 1) * CH]
                    r = v.tensor_tensor(dst, dst, b1[:], ALU.subtract)
                return r
            P.op("vector", f, reads=[r1, "tm8"], writes=["tm8"])
            def f(sc):
                r = None
                for oc in range(KC):
                    r = sc.activation(zf[:, oc * CH:(oc + 1) * CH],
                                      tm8[:, oc * CH:(oc + 1) * CH], AF.Identity,
                                      bias=b_fm[:, oc:oc + 1], scale=g_fm[:, oc:oc + 1])
                return r
            P.op("scalar", f, reads=["tm8", "vec"] + zf_all, writes=zf_all)
            def f(v):
                r = None
                for oc in range(KC):
                    blk = zf[:, oc * CH:(oc + 1) * CH]
                    r = v.tensor_scalar(blk, blk, -CLAMP, CLAMP, ALU.max, ALU.min)
                return r
            P.op("vector", f, reads=zf_all, writes=zf_all)
            def f(v, ub=ub):
                r = None
                for oc in range(KC):
                    blk = zf[:, oc * CH:(oc + 1) * CH]
                    r = v.tensor_tensor(blk, blk,
                                        ufc_s[:, ub * KC * CH + oc * CH: ub * KC * CH + (oc + 1) * CH],
                                        ALU.add)
                return r
            P.op("vector", f, reads=zf_all + ["ufc%d" % ub], writes=zf_all + ["ufc%d" % ub])
            def f(sc):
                r = None
                for oc in range(KC):
                    r = sc.activation(zq8[:, oc * CH:(oc + 1) * CH],
                                      zf[:, oc * CH:(oc + 1) * CH], AF.Square)
                return r
            P.op("scalar", f, reads=zf_all + ["zq"], writes=["zq"])

            def f(t):
                for oc in range(KC):
                    t.matmul(b0[0:1, :], ones_p[:], zf[:, oc * CH:(oc + 1) * CH],
                             start=(oc == 0), stop=(oc == KC - 1))
                r = None
                for oc in range(KC):
                    r = t.matmul(b1[0:1, :], ones_b[:], zq8[:, oc * CH:(oc + 1) * CH],
                                 start=(oc == 0), stop=(oc == KC - 1))
                return r
            P.op("tensor", f, reads=["ones"] + ["zf%d" % o for o in range(KC)] + ["zq", r0, r1],
                 writes=[r0, r1])
            ln_smalls(st, [])

            def f(v):
                r = None
                for oc in range(KC):
                    r = v.tensor_tensor(tm8[:, oc * CH:(oc + 1) * CH],
                                        zf[:, oc * CH:(oc + 1) * CH], b0[:], ALU.mult)
                return r
            P.op("vector", f, reads=[r0, "tm8"] + ["zf%d" % o for o in range(KC)],
                 writes=["tm8"])
            def f(v):
                r = None
                for oc in range(KC):
                    r = v.tensor_tensor(x1c[:, oc * CH:(oc + 1) * CH],
                                        tm8[:, oc * CH:(oc + 1) * CH], b1[:], ALU.subtract)
                return r
            P.op("vector", f, reads=[r1, "tm8", "x1c"], writes=["x1c"])
            def f(sy, c=c):
                return sy.dma_start(kview(x1_h[:, c * CH:(c + 1) * CH]),
                                    x1c[:].rearrange("p (k t) -> p k t", k=KC))
            P.op("sync", f, reads=["x1c"], writes=["x1h%d" % c], dma=True, stream="x1sp")

        emit_load_cast_scan(0)
        for c in range(NCH):
            emit_y(c)
            if c + 1 < NCH:
                emit_load_cast_scan(c + 1)
            emit_epilogue(c)
        P.flush()

    P.barrier()

    with ExitStack() as fctx:
        fsb = lambda n, s, d_: fctx.enter_context(nc.sbuf_tensor(n, s, d_))
        wq_s = fsb("wq_s", [128, KC * 512], BF16)
        x1b = fsb("x1b", [128, KC * CH], BF16)
        q_s = fsb("q_s", [128, 4 * CH], BF16)
        sims = fsb("sims", [128, 512], F32)
        emx = fsb("emx", [128, 64], BF16)
        mx8 = fsb("mx8", [128, 64], F32)
        r88 = fsb("r88", [128, 4 * 64], F32)
        r8 = fsb("r8", [128, 32], F32)
        den4 = fsb("den4", [128, 4], F32)
        rdt = fsb("rdt", [128, 16], F32)
        thr4 = fsb("thr4", [128, 16], F32)
        expuv = fsb("expuv", [128, 4 * 512], BF16)
        gall = fsb("gall", [128, 4 * E], BF16)
        p1 = [fsb("p1%d" % i, [128, 1024], F32) for i in range(2)]
        q2 = fsb("q2", [128, 1024], BF16)
        argc = [fsb("argc%d" % i, [128, 1024], BF16) for i in range(2)]
        wsil = [fsb("wsil%d" % i, [128, 1024], BF16) for i in range(2)]
        ssum = fsb("ssum", [128, 4 * E], BF16)
        big32 = fsb("big32", [128, 32 * CH], BF16)
        expert = fsb("expert", [128, KC * CH], BF16)
        zf2 = fsb("zf2", [128, KC * CH], BF16)
        zsq = fsb("zsq", [128, KC * CH], BF16)
        otmp = fsb("otmp", [128, 2 * CH], F32)
        stat2 = fsb("stat2", [1, 2 * CH], F32)
        wstr = fsb("wstr", [128, 2 * KC * 512], BF16)
        ustr = fsb("ustr", [128, 2 * 4 * 512], BF16)
        uestr = fsb("uestr", [128, 2 * 4 * 512], BF16)
        dstr = wstr

        def f(g):
            return g.dma_start(wq_s[:].rearrange("p (k c) -> p k c", k=KC), kview(wqt))
        P.op("gpsimd", f, writes=["wq"], dma=True, stream="wq")
        P.op("gpsimd", lambda g: g.dma_start(det_b, det), writes=["det_b"], dma=True, stream="pc_det")
        P.op("gpsimd", lambda g: g.dma_start(wupt_b, wupt), writes=["wupt_b"], dma=True, stream="pc_wup")
        P.op("gpsimd", lambda g: g.dma_start(wdnt_b, wdnt), writes=["wdnt_b"], dma=True, stream="pc_wdn")
        P.op("gpsimd", lambda g: g.dma_start(uem_b, uem), writes=["uem_b"], dma=True, stream="pc_ue")

        fps = lambda n, sh, d_: fctx.enter_context(nc.psum_tensor(n, sh, d_))
        pqs = fps("pqs", [128, 512], F32)
        pg = [fps("pg%d" % i, [128, 512], F32) for i in range(2)]
        ptr = fps("ptr", [128, 1024], BF16)
        pbig = fps("pbig", [128, 2048], F32)

        def emit_tail(s):
            def f(t):
                for oc in range(KC):
                    t.matmul(pg[0][0:1, :], ones_b[:], zf2[:, oc * CH:(oc + 1) * CH],
                             start=(oc == 0), stop=(oc == KC - 1))
                r = None
                for oc in range(KC):
                    r = t.matmul(pg[1][0:1, :], ones_b[:], zsq[:, oc * CH:(oc + 1) * CH],
                                 start=(oc == 0), stop=(oc == KC - 1))
                return r
            P.op("tensor", f, reads=["ones", "zf2", "zsq", "pg0", "pg1"], writes=["pg0", "pg1"])

            aux2 = otmp[0:1, 0:CH]
            def f(v):
                v.tensor_scalar(stat2[:, 0:CH], pg[0][0:1, :], 1.0 / D, None, ALU.mult)
                return v.tensor_scalar(stat2[:, CH:2 * CH], pg[1][0:1, :], 1.0 / D, None, ALU.mult)
            P.op("vector", f, reads=["pg0", "pg1", "stat2"], writes=["pg0", "pg1", "stat2"])
            P.op("vector", lambda v: v.tensor_tensor(aux2, stat2[:, 0:CH], stat2[:, 0:CH], ALU.mult),
                 reads=["stat2", "ot0"], writes=["ot0"])
            P.op("vector", lambda v: v.scalar_tensor_tensor(stat2[:, CH:2 * CH],
                                                            stat2[:, CH:2 * CH], EPS, aux2,
                                                            ALU.add, ALU.subtract),
                 reads=["stat2", "ot0"], writes=["stat2"])
            P.op("vector", lambda v: v.reciprocal(aux2, stat2[:, CH:2 * CH]),
                 reads=["stat2", "ot0"], writes=["ot0"])
            P.op("scalar", lambda sc: sc.activation(stat2[:, CH:2 * CH], otmp[0:1, 0:CH], AF.Sqrt),
                 reads=["stat2", "ot0"], writes=["stat2"])
            P.op("vector", lambda v: v.tensor_tensor(stat2[:, 0:CH], stat2[:, 0:CH],
                                                     stat2[:, CH:2 * CH], ALU.mult),
                 reads=["stat2"], writes=["stat2"])

            def f(t):
                t.matmul(pg[0][:], ones_r[:], stat2[:, CH:2 * CH], start=True, stop=True)
                return t.matmul(pg[1][:], ones_r[:], stat2[:, 0:CH], start=True, stop=True)
            P.op("tensor", f, reads=["stat2", "ones", "pg0", "pg1"], writes=["pg0", "pg1"])

            for oc in range(KC):
                ob = "ot%d" % (oc % 2)

                def f(v, oc=oc):
                    dst = otmp[:, (oc % 2) * CH:((oc % 2) + 1) * CH]
                    return v.tensor_tensor(dst, zf2[:, oc * CH:(oc + 1) * CH], pg[0][:], ALU.mult)
                P.op("vector", f, reads=["zf2", "pg0", ob], writes=[ob])
                def f(v, oc=oc):
                    dst = otmp[:, (oc % 2) * CH:((oc % 2) + 1) * CH]
                    return v.tensor_tensor(dst, dst, pg[1][:], ALU.subtract)
                P.op("vector", f, reads=["pg1", ob], writes=[ob])

                def f(sy, s=s, oc=oc):
                    return sy.dma_start(out_h[oc * 128:(oc + 1) * 128, s * CH:(s + 1) * CH],
                                        otmp[:, (oc % 2) * CH:((oc % 2) + 1) * CH])
                P.op("sync", f, reads=[ob], writes=["outw", "outw_" + ob], dma=True, stream="out_" + ob)

        for s in range(NCH):
            if True:

                def f(sy, s=s):
                    return sy.dma_start(x1b[:].rearrange("p (k t) -> p k t", k=KC),
                                        kview(x1_h[:, s * CH:(s + 1) * CH]))
                P.op("sync", f, reads=["x1h%d" % s], writes=["x1b"], dma=True, stream="x1b")

                for ec in range(4):
                    def f(t, ec=ec):
                        r = None
                        for kc in range(KC):
                            r = t.matmul(pqs[:], wq_s[:, kc * 512 + ec * 128: kc * 512 + (ec + 1) * 128],
                                         x1b[:, kc * CH:(kc + 1) * CH],
                                         start=(kc == 0), stop=(kc == KC - 1))
                        return r
                    P.op("tensor", f, reads=["wq", "x1b"], writes=["pqs"])
                    def f(sc, ec=ec):
                        return sc.activation(q_s[:, ec * CH:(ec + 1) * CH], pqs[:], AF.Copy)
                    P.op("scalar", f, reads=["pqs", "q"], writes=["pqs", "q"])

                for t4 in range(4):
                    def f(t, t4=t4):
                        r = None
                        for p in range(2):
                            for h in range(HEADS):
                                jb = p * 4 + h
                                ecq = 2 * p + h // 2
                                r = t.matmul(pqs[:, jb * 64:(jb + 1) * 64],
                                             q_s[:, ecq * CH + t4 * 128: ecq * CH + (t4 + 1) * 128],
                                             keys_sb[:, jb * 64:(jb + 1) * 64],
                                             start=True, stop=True)
                        return r
                    P.op("tensor", f, reads=["keys", "q", "pqs"], writes=["pqs"])
                    def f(sc, t4=t4):
                        return sc.activation(sims[:], pqs[:], AF.Copy)
                    P.op("scalar", f, reads=["pqs", "sims"], writes=["pqs", "sims"])
                    def f(sc, t4=t4):
                        return sc.activation(expuv[:, t4 * 512:(t4 + 1) * 512], sims[:], AF.Exp)
                    P.op("scalar", f, reads=["sims", "expuv%d" % t4], writes=["expuv%d" % t4])

                    def f(v):
                        r = None
                        for jb in range(8):
                            r = v.max(out=mx8[:, jb * 8:(jb + 1) * 8], in_=sims[:, jb * 64:(jb + 1) * 64])
                        return r
                    P.op("vector", f, reads=["sims", "mx8"], writes=["mx8"])
                    P.op("scalar", lambda sc: sc.activation(emx[:], mx8[:], AF.Exp),
                         reads=["mx8", "emx"], writes=["emx"])

                    def f(v):
                        r = None
                        for h in range(HEADS):
                            xb_ = emx[:, h * 8:(h + 1) * 8]
                            yb_ = emx[:, (4 + h) * 8:(4 + h) * 8 + 8]
                            r = v.tensor_tensor(r88[:, h * 64:(h + 1) * 64]
                                                .rearrange("p (a b) -> p a b", a=8),
                                                xb_[:, :, None].to_broadcast([128, 8, 8]),
                                                yb_[:, None, :].to_broadcast([128, 8, 8]), ALU.mult)
                        return r
                    P.op("vector", f, reads=["emx", "r88"], writes=["r88"])
                    def f(v):
                        r = None
                        for h in range(HEADS):
                            r = v.max(out=r8[:, h * 8:(h + 1) * 8], in_=r88[:, h * 64:(h + 1) * 64])
                        return r
                    P.op("vector", f, reads=["r88", "r8"], writes=["r8"])
                    def f(v):
                        r = None
                        for h in range(HEADS):
                            r = v.reduce_sum(den4[:, h:h + 1], r8[:, h * 8:(h + 1) * 8], axis=AX.X)
                        return r
                    P.op("vector", f, reads=["r8", "den4"], writes=["den4"])

                    def f(v, t4=t4):
                        v.reciprocal(rdt[:, t4 * 4:(t4 + 1) * 4], den4[:])
                        r = None
                        for h in range(HEADS):
                            r = v.tensor_scalar(thr4[:, t4 * 4 + h: t4 * 4 + h + 1],
                                                r8[:, h * 8 + 7: h * 8 + 8], MARGIN, None, ALU.mult)
                        return r
                    P.op("vector", f, reads=["den4", "r8", "rdt%d" % t4, "thr%d" % t4],
                         writes=["rdt%d" % t4, "thr%d" % t4])

                    def f(v, t4=t4):
                        r = None
                        for jb in range(8):
                            blk = expuv[:, t4 * 512 + jb * 64: t4 * 512 + (jb + 1) * 64]
                            r = v.scalar_tensor_tensor(
                                blk, sims[:, jb * 64:(jb + 1) * 64],
                                mx8[:, jb * 8 + 7: jb * 8 + 8],
                                blk, ALU.is_ge, ALU.mult)
                        return r
                    P.op("vector", f, reads=["sims", "mx8", "expuv%d" % t4],
                         writes=["expuv%d" % t4])

                if s > 0:
                    emit_tail(s - 1)

                plane_it = {"it": 0, "pend": None}

                def emit_planes(t4):
                    for h in range(HEADS):
                        for ecc in range(4):
                            pp = plane_it["it"] % 2
                            plane_it["it"] += 1

                            def f(e, t4=t4, h=h, ecc=ecc, pp=pp):
                                xb_ = expuv[:, t4 * 512 + h * 64 + ecc * 16: t4 * 512 + h * 64 + (ecc + 1) * 16]
                                yb_ = expuv[:, t4 * 512 + (4 + h) * 64: t4 * 512 + (4 + h) * 64 + 64]
                                return e.tensor_tensor(p1[pp][:].rearrange("p (a b) -> p a b", a=16),
                                                       xb_[:, :, None].to_broadcast([128, 16, 64]),
                                                       yb_[:, None, :].to_broadcast([128, 16, 64]), ALU.mult)
                            P.op("gpsimd", f,
                                 reads=["expuv%d" % t4, "p1%d" % pp], writes=["p1%d" % pp])
                            if plane_it["pend"] is not None:
                                plane_it["pend"]()
                                plane_it["pend"] = None

                            def f(v, t4=t4, h=h, pp=pp):
                                return v.scalar_tensor_tensor(q2[:], p1[pp][:],
                                                              thr4[:, t4 * 4 + h: t4 * 4 + h + 1],
                                                              p1[pp][:], ALU.is_ge, ALU.mult)
                            P.op("vector", f, reads=["p1%d" % pp, "thr%d" % t4, "q2"],
                                 writes=["q2"])
                            def f(v, t4=t4, ecc=ecc, pp=pp):
                                return v.tensor_tensor(argc[pp][:], q2[:],
                                                       gall[:, t4 * E + ecc * 1024: t4 * E + (ecc + 1) * 1024],
                                                       ALU.mult)
                            P.op("vector", f,
                                 reads=["q2", "gall%d_%d" % (t4, 2 * ecc), "gall%d_%d" % (t4, 2 * ecc + 1),
                                        "argc%d" % pp],
                                 writes=["argc%d" % pp])

                            sres = "ss%d_%d" % (t4, ecc)
                            if h == 0:
                                def f(sc, t4=t4, ecc=ecc, pp=pp):
                                    return sc.activation(ssum[:, t4 * E + ecc * 1024: t4 * E + (ecc + 1) * 1024],
                                                         argc[pp][:], AF.Silu, scale=rdt[:, t4 * 4: t4 * 4 + 1])
                                P.op("scalar", f, reads=["argc%d" % pp, "rdt%d" % t4, sres], writes=[sres])
                            else:
                                def f(sc, t4=t4, h=h, pp=pp):
                                    return sc.activation(wsil[pp][:], argc[pp][:], AF.Silu,
                                                         scale=rdt[:, t4 * 4 + h: t4 * 4 + h + 1])
                                P.op("scalar", f, reads=["argc%d" % pp, "rdt%d" % t4, "wsil%d" % pp],
                                     writes=["wsil%d" % pp])
                                def mk_add(t4=t4, ecc=ecc, pp=pp, sres=sres):
                                    def f(g):
                                        dst = ssum[:, t4 * E + ecc * 1024: t4 * E + (ecc + 1) * 1024]
                                        return g.tensor_tensor(dst, dst, wsil[pp][:], ALU.add)
                                    P.op("vector", f, reads=["wsil%d" % pp, sres], writes=[sres])
                                plane_it["pend"] = mk_add


                gbanks = [(pg[0][:], "pg0"), (pg[1][:], "pg1"),
                          (pbig[:, 0:512], "pbig0"), (pbig[:, 512:1024], "pbig1")]
                for pair in range(2):
                    for ec in range(8):
                        wb = "wstr%d" % (ec % 2)
                        off = (ec % 2) * KC * 512

                        def f(sy, ec=ec, off=off):
                            dst = wstr[:, off:off + KC * 512]
                            return sy.dma_start(dst.rearrange("p (k c) -> p k c", k=KC),
                                                kview(det_b[:, ec * 512:(ec + 1) * 512]))
                        P.op("sync", f, reads=["det_b", wb], writes=[wb], dma=True, stream=wb)
                        for tloc in range(2):
                            t4 = 2 * pair + tloc
                            bank, pb = gbanks[(ec * 2 + tloc) % 4]

                            def f(t, ec=ec, t4=t4, off=off, bank=bank):
                                r = None
                                for kc in range(KC):
                                    r = t.matmul(bank,
                                                 x1b[:, kc * CH + t4 * 128: kc * CH + (t4 + 1) * 128],
                                                 wstr[:, off + kc * 512: off + (kc + 1) * 512],
                                                 start=(kc == 0), stop=(kc == KC - 1))
                                return r
                            P.op("tensor", f, reads=[wb, "x1b"], writes=[pb])
                            def f(sc, ec=ec, t4=t4, bank=bank):
                                return sc.activation(gall[:, t4 * E + ec * 512: t4 * E + (ec + 1) * 512],
                                                     bank, AF.Copy)
                            P.op("scalar", f, reads=[pb, "gall%d_%d" % (t4, ec)],
                                 writes=[pb, "gall%d_%d" % (t4, ec)])
                    emit_planes(2 * pair)
                    emit_planes(2 * pair + 1)

                for icg in range(8):
                    db = "wstr%d" % (icg % 2)
                    doff = (icg % 2) * KC * 512

                    def f(sy, icg=icg, doff=doff):
                        dst = dstr[:, doff:doff + KC * 512]
                        return sy.dma_start(
                            dst.rearrange("p (k i c) -> p k i c", k=KC, i=4),
                            wupt_b[:, icg * 512:(icg + 1) * 512]
                            .rearrange("(k p) (i c) -> p k i c", p=128, i=4))
                    P.op("sync", f, reads=["wupt_b", db], writes=[db], dma=True, stream=db)

                    for i4 in range(4):
                        ic = icg * 4 + i4
                        bank, pb = gbanks[ic % 4]

                        def f(t, i4=i4, ic=ic, doff=doff, bank=bank):
                            r = None
                            for kc in range(KC):
                                r = t.matmul(bank,
                                             dstr[:, doff + kc * 512 + i4 * 128: doff + kc * 512 + (i4 + 1) * 128],
                                             x1b[:, kc * CH:(kc + 1) * CH],
                                             start=(kc == 0), stop=(kc == KC - 1))
                            return r
                        P.op("tensor", f, reads=[db, "x1b"], writes=[pb])

                        def f(sc, ic=ic, bank=bank):
                            return sc.activation(big32[:, ic * CH:(ic + 1) * CH], bank, AF.Silu)
                        P.op("scalar", f, reads=[pb, "big32"], writes=[pb, "big32"])

                dn1 = [(pqs, "pqs"), (pg[0], "pg0"), (pg[1], "pg1")]
                for dcg in range(2):
                    for icg in range(8):
                        ub_ = "ustr%d" % (icg % 2)
                        uoff = (icg % 2) * 4 * 512

                        def f(sy, dcg=dcg, icg=icg, uoff=uoff):
                            return sy.dma_start(
                                ustr[:, uoff:uoff + 4 * 512].rearrange("p (i c) -> p i c", i=4),
                                wdnt_b[icg * 512:(icg + 1) * 512, dcg * 512:(dcg + 1) * 512]
                                .rearrange("(i p) c -> p i c", p=128))
                        P.op("sync", f, reads=["wdnt_b", ub_], writes=[ub_], dma=True, stream=ub_)

                        def f(t, dcg=dcg, icg=icg, uoff=uoff):
                            r = None
                            for i4 in range(4):
                                ic = icg * 4 + i4
                                for dc4 in range(4 if dcg == 0 else 3):
                                    dst = pbig[:, dc4 * 512:(dc4 + 1) * 512] if dcg == 0 \
                                        else dn1[dc4][0][:]
                                    r = t.matmul(dst,
                                                 ustr[:, uoff + i4 * 512 + dc4 * 128: uoff + i4 * 512 + (dc4 + 1) * 128],
                                                 big32[:, ic * CH:(ic + 1) * CH],
                                                 start=(ic == 0), stop=(ic == 31))
                            return r
                        wl = ["pbig"] if dcg == 0 else [n for _, n in dn1]
                        P.op("tensor", f, reads=[ub_, "big32"] + wl, writes=wl)

                def f(sy):
                    return sy.dma_start(
                        ustr[:, 0:4096].rearrange("p (i c) -> p i c", i=32),
                        wdnt_b[:, 7 * 128:8 * 128].rearrange("(i p) c -> p i c", p=128))
                P.op("sync", f, reads=["ustr0", "ustr1"], writes=["ustr0", "ustr1"],
                     dma=True, stream="ustr0")

                for dc4 in range(4):
                    def f(v, dc4=dc4):
                        return v.tensor_tensor(zf2[:, dc4 * CH:(dc4 + 1) * CH],
                                               pbig[:, dc4 * 512:(dc4 + 1) * 512],
                                               x1b[:, dc4 * CH:(dc4 + 1) * CH], ALU.add)
                    P.op("vector", f, reads=["pbig", "x1b", "zf2"], writes=["pbig", "zf2"])
                for dc4 in range(3):
                    bank, bn = dn1[dc4]

                    def f(v, dc4=dc4, bank=bank):
                        oc = 4 + dc4
                        return v.tensor_tensor(zf2[:, oc * CH:(oc + 1) * CH],
                                               bank[:],
                                               x1b[:, oc * CH:(oc + 1) * CH], ALU.add)
                    P.op("vector", f, reads=[bn, "x1b", "zf2"], writes=[bn, "zf2"])

                def f(t):
                    r = None
                    for ic in range(32):
                        r = t.matmul(pbig[:, 0:512],
                                     ustr[:, ic * 128:(ic + 1) * 128],
                                     big32[:, ic * CH:(ic + 1) * CH],
                                     start=(ic == 0), stop=(ic == 31))
                    return r
                P.op("tensor", f, reads=["ustr0", "ustr1", "big32", "pbig"], writes=["pbig"])

                def f(v):
                    return v.tensor_tensor(zf2[:, 7 * CH:8 * CH], pbig[:, 0:512],
                                           x1b[:, 7 * CH:8 * CH], ALU.add)
                P.op("vector", f, reads=["pbig", "x1b", "zf2"], writes=["pbig", "zf2"])

                def emit_eup(dcg, ec):
                    ub_ = "uestr%d" % (ec % 4)
                    uoff = (ec % 4) * 512

                    def f(sy, dcg=dcg, ec=ec, uoff=uoff):
                        return sy.dma_start(uestr[:, uoff:uoff + 512],
                                            uem_b[ec * 128:(ec + 1) * 128, dcg * 512:(dcg + 1) * 512])
                    P.op("sync", f, reads=["uem_b", ub_], writes=[ub_], dma=True, stream=ub_)

                    def f(t, ec=ec, uoff=uoff):
                        r = None
                        for dc4 in range(4):
                            r = t.matmul(pbig[:, dc4 * 512:(dc4 + 1) * 512],
                                         uestr[:, uoff + dc4 * 128: uoff + (dc4 + 1) * 128],
                                         big32[:, ec * CH:(ec + 1) * CH],
                                         start=(ec == 0), stop=(ec == 31))
                        return r
                    P.op("tensor", f, reads=[ub_, "big32"], writes=["pbig"])

                for grp in range(4):
                    for t4 in range(4):
                        def f(t, t4=t4, grp=grp):
                            r = None
                            for j in range(8):
                                eb = grp * 8 + j
                                r = t.transpose(ptr[:, j * 128:(j + 1) * 128],
                                                ssum[:, t4 * E + eb * 128: t4 * E + (eb + 1) * 128],
                                                id_b[:])
                            return r
                        P.op("tensor", f, reads=["idb", "ptr"] + ["ss%d_%d" % (t4, e) for e in range(4)],
                             writes=["ptr"])

                        def f(sc, t4=t4, grp=grp):
                            dst = big32[:].rearrange("p (e t) -> p e t", e=32)[:, grp * 8:(grp + 1) * 8,
                                                                              t4 * 128:(t4 + 1) * 128]
                            return sc.activation(dst, ptr[:].rearrange("p (j c) -> p j c", j=8), AF.Copy)
                        P.op("scalar", f, reads=["ptr", "big32"], writes=["ptr", "big32"])
                    for ec in range(8 * grp, 8 * grp + 8):
                        emit_eup(0, ec)
                for dc4 in range(4):
                    def f(sc, dc4=dc4):
                        return sc.activation(expert[:, dc4 * CH:(dc4 + 1) * CH],
                                             pbig[:, dc4 * 512:(dc4 + 1) * 512], AF.Copy)
                    P.op("scalar", f, reads=["pbig", "expert"], writes=["pbig", "expert"])
                for ec in range(32):
                    emit_eup(1, ec)
                for dc4 in range(4):
                    def f(sc, dc4=dc4):
                        oc = 4 + dc4
                        return sc.activation(expert[:, oc * CH:(oc + 1) * CH],
                                             pbig[:, dc4 * 512:(dc4 + 1) * 512], AF.Copy)
                    P.op("scalar", f, reads=["pbig", "expert"], writes=["pbig", "expert"])

                def f(v):
                    r = None
                    for oc in range(KC):
                        r = v.tensor_tensor(zf2[:, oc * CH:(oc + 1) * CH],
                                            zf2[:, oc * CH:(oc + 1) * CH],
                                            expert[:, oc * CH:(oc + 1) * CH], ALU.add)
                    return r
                P.op("vector", f, reads=["zf2", "expert"], writes=["zf2"])
                def f(sc):
                    r = None
                    for oc in range(KC):
                        r = sc.activation(zsq[:, oc * CH:(oc + 1) * CH],
                                          zf2[:, oc * CH:(oc + 1) * CH], AF.Square)
                    return r
                P.op("scalar", f, reads=["zf2", "zsq"], writes=["zsq"])
                P.flush()
        emit_tail(NCH - 1)
        P.flush()

    P.barrier()
    return nc, ctx


def shard_inputs(inputs):
    f32 = lambda x: np.ascontiguousarray(np.asarray(x, dtype=np.float32))
    u = f32(inputs["L0_attn_u"])
    Bm = f32(inputs["L0_attn_Bmat"])
    Cm = f32(inputs["L0_attn_Cmat"])
    Dm = f32(inputs["L0_attn_Dmat"])
    A = f32(inputs["L0_attn_A_diag"])
    h0 = f32(inputs["L0_attn_h0"])
    gam = f32(inputs["L0_attn_ln_gamma"])
    bet = f32(inputs["L0_attn_ln_beta"])
    Wup = f32(inputs["L0_ffn_W_up"])
    Wdn = f32(inputs["L0_ffn_W_down"])
    Wq = f32(inputs["L0_ffn_W_q"])
    keys = f32(inputs["L0_ffn_keys"])
    de = f32(inputs["L0_ffn_down_embed"])
    ue = f32(inputs["L0_ffn_up_embed"])

    fm = lambda v: np.ascontiguousarray(v.reshape(KC, 128).T)
    kp = np.zeros((128, 512), np.float32)
    for p in range(2):
        for h in range(HEADS):
            jb = p * 4 + h
            base = (h % 2) * 64
            kp[base:base + 64, jb * 64:(jb + 1) * 64] = keys[h, :, p, :].T

    shared = {
        "bmt": np.ascontiguousarray(Bm.T), "cmt": np.ascontiguousarray(Cm.T),
        "dmt": np.ascontiguousarray(Dm.T), "wqt": np.ascontiguousarray(Wq.T),
        "keyspad": kp, "det": np.ascontiguousarray(de.T), "uem": ue,
        "wupt": np.ascontiguousarray(Wup.T), "wdnt": np.ascontiguousarray(Wdn.T),
        "identm": np.eye(128, dtype=np.float32),
        "afmv": fm(A), "h0fm": fm(h0), "gfm": fm(gam), "bfm": fm(bet),
    }
    zeros_dt = np.zeros((D, T), np.float32)
    zkc = np.zeros((128, KC), np.float32)
    in_maps = []
    for c in range(8):
        b, half = c // 2, c % 2
        m = dict(shared)
        m["u_own"] = np.ascontiguousarray(u[b, half * T:(half + 1) * T].T)
        m["u_pre"] = np.ascontiguousarray(u[b, :T].T) if half else zeros_dt
        m["selv"] = np.full((128, 1), float(half), np.float32)
        m["h0v2"] = zkc if half else fm(h0)
        in_maps.append(m)
    return in_maps


def kernel(**inputs):
    in_maps = shard_inputs(inputs)
    nc, ctx = build_nc(debug=False)
    ctx.close()
    res = run_bass_kernel_spmd(nc, in_maps, core_ids=list(range(8)))
    out = np.zeros((4, 4096, D), np.float32)
    for c in range(8):
        b, half = c // 2, c % 2
        out[b, half * T:(half + 1) * T] = np.asarray(res.results[c]["out"]).T
    return out


# version 29
# speedup vs baseline: 1.0489x; 1.0489x over previous
import numpy as np
from contextlib import ExitStack

import concourse.bass as bass
import concourse.mybir as mybir
from concourse.bass_utils import run_bass_kernel_spmd

F32 = mybir.dt.float32
BF16 = mybir.dt.bfloat16
AF = mybir.ActivationFunctionType
ALU = mybir.AluOpType
AX = mybir.AxisListType

D = 1024
T = 2048
CH = 512
NCH = T // CH
KC = D // 128
E = 4096
HEADS = 4
EPS = 1e-5
CLAMP = 1e6
MARGIN = 1.0

ENGS = ["tensor", "vector", "scalar", "gpsimd", "sync"]
ESEM = {"tensor": "pe", "vector": "dve", "scalar": "act", "gpsimd": "pool", "sync": "sp"}


class Prog:

    def __init__(self, nc, ctx):
        self.nc = nc
        self.ctx = ctx
        self.block = ctx.enter_context(nc.Block())
        self.sems = {}
        for s in ["pe", "dve", "act", "pool", "sp"]:
            self.sems[s] = ctx.enter_context(nc.semaphore(s))
        self.cnt = {s: 0 for s in self.sems}
        self.last_w = {}
        self.readers = {}
        self.waited = {e: {s: 0 for s in self.sems} for e in ENGS}
        self.pending = {e: [] for e in ENGS}

    def dma_sem(self, stream):
        name = "dq_" + stream
        if name not in self.sems:
            self.sems[name] = self.ctx.enter_context(self.nc.semaphore(name))
            self.cnt[name] = 0
            for e in ENGS:
                self.waited[e][name] = 0
        return name

    def flush(self, engs=None):
        for e in (ENGS if engs is None else engs):
            items = self.pending[e]
            if not items:
                continue
            self.pending[e] = []

            def section(handle, items=items):
                for waits, build, sem, k, drain, new_cnt in items:
                    for s_, c_ in waits:
                        handle.wait_ge(self.sems[s_], c_)
                    ins = build(handle)
                    assert ins is not None
                    if isinstance(ins, list):
                        assert 16 * len(ins) == k
                        for i_ in ins:
                            i_.then_inc(self.sems[sem], 16)
                    else:
                        ins.then_inc(self.sems[sem], k)
                    if drain:
                        handle.wait_ge(self.sems[sem], new_cnt)
            getattr(self.block, e)(section)

    def _need(self, eng, reads, writes):
        need = {}

        def add(tok):
            if tok is None:
                return
            s, c, weng = tok
            if weng == eng and not s.startswith("dq_"):
                return
            need[s] = max(need.get(s, 0), c)

        for r in reads:
            add(self.last_w.get(r))
        for w in writes:
            add(self.last_w.get(w))
            for s, c in self.readers.get(w, {}).items():
                need[s] = max(need.get(s, 0), c)
        need.pop(ESEM[eng], None)
        out = []
        for s, c in sorted(need.items()):
            if c > self.waited[eng][s]:
                out.append((s, c))
                self.waited[eng][s] = c
        return out

    def op(self, eng, build, reads=(), writes=(), dma=False, drain=False, stream=None,
           pre_drain=False):
        if dma:
            sem = self.dma_sem(stream or "misc")
        else:
            sem = ESEM[eng]
        waits = self._need(eng, reads, writes)
        if (pre_drain or eng in ("vector", "scalar", "gpsimd")) and not dma \
                and self.cnt[ESEM[eng]] > self.waited[eng][ESEM[eng]]:
            waits.insert(0, (ESEM[eng], self.cnt[ESEM[eng]]))
            self.waited[eng][ESEM[eng]] = self.cnt[ESEM[eng]]
        if dma:
            k = 16 * (build.n_dma if hasattr(build, "n_dma") else 1)
        else:
            k = 1
        new_cnt = self.cnt[sem] + k
        self.pending[eng].append((waits, build, sem, k, drain, new_cnt))
        self.cnt[sem] = new_cnt
        if drain:
            self.waited[eng][sem] = max(self.waited[eng][sem], new_cnt)
        tok = (sem, new_cnt, eng)
        for w in writes:
            self.last_w[w] = tok
            self.readers[w] = {}
        for r in reads:
            d = self.readers.setdefault(r, {})
            d[sem] = max(d.get(sem, 0), new_cnt)
        return tok

    def barrier(self):
        self.flush()
        snap = dict(self.cnt)
        for e in ENGS:
            todo = [(s, c) for s, c in sorted(snap.items()) if c > self.waited[e][s]]

            def f(eng, todo=todo):
                for s, c in todo:
                    eng.wait_ge(self.sems[s], c)
            getattr(self.block, e)(f)
            for s, c in snap.items():
                self.waited[e][s] = max(self.waited[e][s], c)


def build_nc(debug=False):
    nc = bass.Bass("TRN2", target_bir_lowering=False, dynamic_dma_scratch_size=16384)
    dt = lambda n, s, d_, k: nc.dram_tensor(n, s, d_, kind=k).ap()

    u_own = dt("u_own", [D, T], F32, "ExternalInput")
    u_pre = dt("u_pre", [D, T], F32, "ExternalInput")
    selv = dt("selv", [128, 1], F32, "ExternalInput")
    h0v2 = dt("h0v2", [128, KC], F32, "ExternalInput")
    afmv = dt("afmv", [128, KC], F32, "ExternalInput")
    h0fm = dt("h0fm", [128, KC], F32, "ExternalInput")
    gfm = dt("gfm", [128, KC], F32, "ExternalInput")
    bfm = dt("bfm", [128, KC], F32, "ExternalInput")
    bmt = dt("bmt", [D, D], F32, "ExternalInput")
    cmt = dt("cmt", [D, D], F32, "ExternalInput")
    dmt = dt("dmt", [D, D], F32, "ExternalInput")
    wqt = dt("wqt", [D, 512], F32, "ExternalInput")
    keyspad = dt("keyspad", [128, 512], F32, "ExternalInput")
    det = dt("det", [D, E], F32, "ExternalInput")
    uem = dt("uem", [E, D], F32, "ExternalInput")
    wupt = dt("wupt", [D, E], F32, "ExternalInput")
    wdnt = dt("wdnt", [E, D], F32, "ExternalInput")
    identm = dt("identm", [128, 128], F32, "ExternalInput")
    out_h = dt("out", [D, T], F32, "ExternalOutput")

    det_b = dt("det_b", [D, E], BF16, "Internal")
    uem_b = dt("uem_b", [E, D], BF16, "Internal")
    wupt_b = dt("wupt_b", [D, E], BF16, "Internal")
    wdnt_b = dt("wdnt_b", [E, D], BF16, "Internal")
    x1_h = dt("x1_h", [D, T], BF16, "ExternalOutput" if debug else "Internal")
    if debug:
        dbg_vec = dt("dbg_vec", [128, 6 * KC], F32, "ExternalOutput")
        dbg_bmt = dt("dbg_bmt", [128, KC * D], BF16, "ExternalOutput")
        dbg_uop = dt("dbg_uop", [128, KC * CH], BF16, "ExternalOutput")
        dbg_xs = dt("dbg_xs", [128, KC * CH], BF16, "ExternalOutput")
        dbg_act = dt("dbg_act", [128, KC * CH], F32, "ExternalOutput")
        dbg_carry = dt("dbg_carry", [128, KC], F32, "ExternalOutput")
        dbg_bu = dt("dbg_bu", [128, CH], F32, "ExternalOutput")
        dbg_cc = dt("dbg_cc", [128, 4 * KC], F32, "ExternalOutput")
        dbg_cc2 = dt("dbg_cc2", [128, KC * KC], F32, "ExternalOutput")
        dbg_sc1 = dt("dbg_sc1", [128, CH], BF16, "ExternalOutput")

    kview = lambda ap: ap.rearrange("(k p) t -> p k t", p=128)

    ctx = ExitStack()
    sb = lambda n, s, d_: ctx.enter_context(nc.sbuf_tensor(n, s, d_))
    P = Prog(nc, ctx)

    vec = sb("vec", [128, 6 * KC], F32)
    a_fm = vec[:, 0:KC]
    h0_fm = vec[:, KC:2 * KC]
    g_fm = vec[:, 2 * KC:3 * KC]
    b_fm = vec[:, 3 * KC:4 * KC]
    h02_fm = vec[:, 4 * KC:5 * KC]
    spa = vec[:, 5 * KC:6 * KC]
    selv_sb = sb("selv_sb", [128, 1], F32)
    ones_b = sb("ones_b", [128, 1], BF16)
    ones_p = sb("ones_p", [128, 1], F32)
    ones_r = sb("ones_r", [1, 128], F32)
    id_b = sb("id_b", [128, 128], BF16)
    carry = sb("carry", [128, KC], F32)
    init2 = sb("init2", [128, KC], F32)
    keys_sb = sb("keys_sb", [128, 512], BF16)

    def f(sy):
        return [sy.dma_start(vec[:, 0:KC], afmv),
                sy.dma_start(vec[:, KC:2 * KC], h0fm),
                sy.dma_start(vec[:, 2 * KC:3 * KC], gfm),
                sy.dma_start(vec[:, 3 * KC:4 * KC], bfm),
                sy.dma_start(vec[:, 4 * KC:5 * KC], h0v2),
                sy.dma_start(selv_sb[:], selv)]
    f.n_dma = 6
    P.op("sync", f, writes=["vec", "selv"], dma=True, stream="init")

    def f(g):
        return [g.dma_start(id_b[:], identm),
                g.dma_start(keys_sb[:], keyspad)]
    f.n_dma = 2
    P.op("gpsimd", f, writes=["idb", "keys"], dma=True, stream="consts")

    def f(v):
        v.memset(ones_b[:], 1.0)
        v.memset(ones_p[:], 1.0)
        return v.memset(ones_r[:], 1.0)
    P.op("vector", f, writes=["ones"])

    P.op("scalar", lambda sc: sc.activation(spa, vec[:, 0:KC], AF.Exp),
         reads=["vec"], writes=["spa"])
    P.op("vector", lambda v: v.tensor_scalar(spa, spa, 1.0, None, ALU.add),
         reads=["spa"], writes=["spa"])
    P.op("scalar", lambda sc: sc.activation(spa, spa, AF.Ln),
         reads=["spa"], writes=["spa"])
    P.op("vector", lambda v: v.tensor_scalar(a_fm, spa, -1.0, None, ALU.mult),
         reads=["spa"], writes=["vec_a"])
    if debug:
        P.op("sync", lambda sy: sy.dma_start(dbg_vec, vec[:]), reads=["vec", "vec_a"], dma=True)

    with ExitStack() as actx:
        asb = lambda n, s, d_: actx.enter_context(nc.sbuf_tensor(n, s, d_))
        aps = lambda n, s, d_: actx.enter_context(nc.psum_tensor(n, s, d_))
        bmt_s = asb("bmt_s", [128, KC * D], BF16)
        cmt_s = asb("cmt_s", [128, KC * D], BF16)
        dmt_s = asb("dmt_s", [128, KC * D], BF16)
        uop_s = asb("uop_s", [128, 2 * KC * CH], BF16)
        ufc_s = asb("ufc_s", [128, 2 * KC * CH], F32)
        xs_s = asb("xs_s", [128, 2 * KC * CH], BF16)
        scr1 = asb("scr1", [128, CH], BF16)
        asq8 = asb("asq8", [128, KC * CH], BF16)
        zf = asb("zf", [128, KC * CH], F32)
        zq8 = asb("zq8", [128, KC * CH], BF16)
        x1c = asb("x1c", [128, KC * CH], BF16)
        tmp = asb("tmp", [128, 2 * CH], F32)
        tm8 = asb("tm8", [128, KC * CH], F32)
        stat = asb("stat", [1, 6 * CH], F32)
        pbu = [aps("pbu%d" % i, [128, CH], F32) for i in range(2)]
        py = [aps("py%d" % i, [128, CH], F32) for i in range(2)]
        pb4 = [aps("pb4_%d" % i, [128, CH], F32) for i in range(4)]

        wtile = lambda buf, kc, j: buf[:, kc * D + j * 128: kc * D + (j + 1) * 128]

        def f(g):
            return [g.dma_start(bmt_s[:].rearrange("p (k c) -> p k c", k=KC), kview(bmt)),
                    g.dma_start(cmt_s[:].rearrange("p (k c) -> p k c", k=KC), kview(cmt)),
                    g.dma_start(dmt_s[:].rearrange("p (k c) -> p k c", k=KC), kview(dmt))]
        f.n_dma = 3
        P.op("gpsimd", f, writes=["bmt", "cmt", "dmt"], dma=True, stream="bcd")
        if debug:
            P.op("sync", lambda sy: sy.dma_start(dbg_bmt, bmt_s[:]), reads=["bmt"], dma=True)

        def ln_smalls(st, extra_reads):
            o = st * 3 * CH
            b0, b1 = pb4[2 * st], pb4[2 * st + 1]
            sres = "stat%d" % st
            r0, r1 = "pbc%d0" % st, "pbc%d1" % st

            def f(v):
                v.tensor_scalar(stat[:, o:o + CH], b0[0:1, :], 1.0 / D, None, ALU.mult)
                return v.tensor_scalar(stat[:, o + CH:o + 2 * CH], b1[0:1, :], 1.0 / D, None, ALU.mult)
            P.op("vector", f, reads=list(extra_reads) + [sres, r0, r1], writes=[sres])
            P.op("vector", lambda v: v.tensor_tensor(stat[:, o + 2 * CH:o + 3 * CH],
                                                     stat[:, o:o + CH], stat[:, o:o + CH], ALU.mult),
                 reads=[sres], writes=[sres])
            P.op("vector", lambda v: v.scalar_tensor_tensor(stat[:, o + CH:o + 2 * CH],
                                                            stat[:, o + CH:o + 2 * CH], EPS,
                                                            stat[:, o + 2 * CH:o + 3 * CH],
                                                            ALU.add, ALU.subtract),
                 reads=[sres], writes=[sres])
            P.op("vector", lambda v: v.reciprocal(stat[:, o + 2 * CH:o + 3 * CH],
                                                  stat[:, o + CH:o + 2 * CH]),
                 reads=[sres], writes=[sres])
            P.op("scalar", lambda sc: sc.activation(stat[:, o + CH:o + 2 * CH],
                                                    stat[:, o + 2 * CH:o + 3 * CH], AF.Sqrt),
                 reads=[sres], writes=[sres])
            P.op("vector", lambda v: v.tensor_tensor(stat[:, o:o + CH], stat[:, o:o + CH],
                                                     stat[:, o + CH:o + 2 * CH], ALU.mult),
                 reads=[sres], writes=[sres])

            def f(t):
                t.matmul(b0[:], ones_r[:], stat[:, o + CH:o + 2 * CH], start=True, stop=True)
                return t.matmul(b1[:], ones_r[:], stat[:, o:o + CH], start=True, stop=True)
            P.op("tensor", f, reads=[sres, "ones", r0, r1], writes=[r0, r1])

        for c in range(NCH):
            ub1 = c % 2
            uf1 = "ufc%d" % ub1
            up1 = "uop%d" % ub1

            def f(sy, c=c, ub1=ub1):
                return sy.dma_start(
                    ufc_s[:, ub1 * KC * CH:(ub1 + 1) * KC * CH].rearrange("p (k t) -> p k t", k=KC),
                    kview(u_pre[:, c * CH:(c + 1) * CH]))
            P.op("sync", f, reads=[uf1], writes=[uf1], dma=True, stream=uf1)
            def f(sc, ub1=ub1):
                r = None
                for kc in range(KC):
                    r = sc.activation(uop_s[:, ub1 * KC * CH + kc * CH: ub1 * KC * CH + (kc + 1) * CH],
                                      ufc_s[:, ub1 * KC * CH + kc * CH: ub1 * KC * CH + (kc + 1) * CH],
                                      AF.Copy)
                return r
            P.op("scalar", f, reads=[uf1, up1], writes=[up1])
            for ncn in range(KC):
                bank = [pbu[0], pbu[1], py[0], py[1]][ncn % 4]
                pb = ["pbu0", "pbu1", "py0", "py1"][ncn % 4]

                def f(t, ncn=ncn, bank=bank, ub1=ub1):
                    r = None
                    for kc in range(KC):
                        r = t.matmul(bank[:], wtile(bmt_s, kc, ncn),
                                     uop_s[:, ub1 * KC * CH + kc * CH: ub1 * KC * CH + (kc + 1) * CH],
                                     start=(kc == 0), stop=(kc == KC - 1))
                    return r
                P.op("tensor", f, reads=["bmt", up1], writes=[pb])

                if debug and c == 2 and ncn == 0:
                    def fd(v):
                        return v.tensor_copy(zf[:, 0:CH], pbu[0][:])
                    P.op("vector", fd, reads=["pbu0"], writes=["zf0"])
                    P.op("sync", lambda sy: sy.dma_start(dbg_bu, zf[:, 0:CH]), reads=["zf0"], dma=True)
                def f(v, c=c, ncn=ncn, bank=bank):
                    initial = h0_fm[:, ncn:ncn + 1] if c == 0 else carry[:, ncn:ncn + 1]
                    return v.tensor_tensor_scan(scr1[:], a_fm[:, ncn:ncn + 1].to_broadcast([128, CH]),
                                                bank[:], initial, ALU.mult, ALU.add)
                P.op("vector", f, reads=[pb, "vec_a", "vec", "carry", "scr1"],
                     writes=[pb, "scr1"], pre_drain=True)
                def f(v, ncn=ncn):
                    return v.tensor_copy(carry[:, ncn:ncn + 1], scr1[:, CH - 1:CH])
                P.op("vector", f, reads=["scr1", "carry"], writes=["carry"])
                if debug and c == 2 and ncn == 0:
                    P.op("sync", lambda sy: sy.dma_start(dbg_sc1, scr1[:]), reads=["scr1"], dma=True)
                if debug and ncn == KC - 1:
                    def fd(sy, c=c):
                        return sy.dma_start(dbg_cc[:, c * KC:(c + 1) * KC], carry[:])
                    P.op("sync", fd, reads=["carry"], dma=True)
                if debug and c == 2:
                    def fd(sy, ncn=ncn):
                        return sy.dma_start(dbg_cc2[:, ncn * KC:(ncn + 1) * KC], carry[:])
                    P.op("sync", fd, reads=["carry"], dma=True)

        def f(v):
            r = None
            for ncn in range(KC):
                r = v.scalar_tensor_tensor(init2[:, ncn:ncn + 1], carry[:, ncn:ncn + 1],
                                           selv_sb[:, 0:1], h02_fm[:, ncn:ncn + 1], ALU.mult, ALU.add)
            return r
        if debug:
            P.op("sync", lambda sy: sy.dma_start(dbg_carry, carry[:]), reads=["carry"], dma=True)
        P.op("vector", f, reads=["carry", "selv", "vec"], writes=["init2"])

        def emit_load_cast_scan(c):
            ub = c % 2
            uo = "uop%d" % ub
            uf = "ufc%d" % ub

            def f(sy, c=c, ub=ub):
                return sy.dma_start(
                    ufc_s[:, ub * KC * CH:(ub + 1) * KC * CH].rearrange("p (k t) -> p k t", k=KC),
                    kview(u_own[:, c * CH:(c + 1) * CH]))
            P.op("sync", f, reads=[uf], writes=[uf], dma=True, stream=uf)

            def f(sc, ub=ub):
                r = None
                for kc in range(KC):
                    r = sc.activation(uop_s[:, ub * KC * CH + kc * CH: ub * KC * CH + (kc + 1) * CH],
                                      ufc_s[:, ub * KC * CH + kc * CH: ub * KC * CH + (kc + 1) * CH],
                                      AF.Copy)
                return r
            P.op("scalar", f, reads=[uf, uo], writes=[uo])

            for ncn in range(KC):
                bank = pbu[ncn % 2]
                pb = "pbu%d" % (ncn % 2)

                def f(t, ncn=ncn, ub=ub, bank=bank):
                    r = None
                    for kc in range(KC):
                        r = t.matmul(bank[:], wtile(bmt_s, kc, ncn),
                                     uop_s[:, ub * KC * CH + kc * CH: ub * KC * CH + (kc + 1) * CH],
                                     start=(kc == 0), stop=(kc == KC - 1))
                    return r
                P.op("tensor", f, reads=["bmt", uo, "init2"], writes=[pb])

                def f(v, c=c, ncn=ncn, ub=ub, bank=bank):
                    initial = init2[:, ncn:ncn + 1] if c == 0 else carry[:, ncn:ncn + 1]
                    dst = xs_s[:, ub * KC * CH + ncn * CH: ub * KC * CH + (ncn + 1) * CH]
                    return v.tensor_tensor_scan(dst, a_fm[:, ncn:ncn + 1].to_broadcast([128, CH]),
                                                bank[:], initial, ALU.mult, ALU.add)
                P.op("vector", f, reads=[pb, "vec_a", "init2", "carry", "xs%d_%d" % (ub, ncn)],
                     writes=[pb, "xs%d_%d" % (ub, ncn)], pre_drain=True)
                def f(v, ncn=ncn, ub=ub):
                    dst = xs_s[:, ub * KC * CH + ncn * CH: ub * KC * CH + (ncn + 1) * CH]
                    return v.tensor_copy(carry[:, ncn:ncn + 1], dst[:, CH - 1:CH])
                P.op("vector", f, reads=["xs%d_%d" % (ub, ncn), "carry"], writes=["carry"])

        def emit_y(c):
            ub = c % 2
            uo = "uop%d" % ub
            for oc in range(KC):
                pyb = "py%d" % (oc % 2)

                def f(t, oc=oc, ub=ub):
                    r = None
                    for ncn in range(KC):
                        r = t.matmul(py[oc % 2][:], wtile(cmt_s, ncn, oc),
                                     xs_s[:, ub * KC * CH + ncn * CH: ub * KC * CH + (ncn + 1) * CH],
                                     start=(ncn == 0), stop=False)
                    for kc in range(KC):
                        r = t.matmul(py[oc % 2][:], wtile(dmt_s, kc, oc),
                                     uop_s[:, ub * KC * CH + kc * CH: ub * KC * CH + (kc + 1) * CH],
                                     start=False, stop=(kc == KC - 1))
                    return r
                P.op("tensor", f,
                     reads=["cmt", "dmt", uo] + ["xs%d_%d" % (ub, n) for n in range(KC)],
                     writes=[pyb])

                def f(sc, oc=oc):
                    return sc.activation(zf[:, oc * CH:(oc + 1) * CH], py[oc % 2][:], AF.Gelu)
                P.op("scalar", f, reads=[pyb, "zf%d" % oc], writes=[pyb, "zf%d" % oc])
                def f(sc, oc=oc):
                    return sc.activation(asq8[:, oc * CH:(oc + 1) * CH],
                                         zf[:, oc * CH:(oc + 1) * CH], AF.Square)
                P.op("scalar", f, reads=["zf%d" % oc, "asq"], writes=["asq"])

        def emit_epilogue(c):
            ub = c % 2
            st = c % 2
            b0, b1 = pb4[2 * st], pb4[2 * st + 1]
            r0, r1 = "pbc%d0" % st, "pbc%d1" % st
            def f(t):
                for oc in range(KC):
                    t.matmul(b0[0:1, :], ones_p[:], zf[:, oc * CH:(oc + 1) * CH],
                             start=(oc == 0), stop=(oc == KC - 1))
                r = None
                for oc in range(KC):
                    r = t.matmul(b1[0:1, :], ones_b[:], asq8[:, oc * CH:(oc + 1) * CH],
                                 start=(oc == 0), stop=(oc == KC - 1))
                return r
            P.op("tensor", f, reads=["ones"] + ["zf%d" % o for o in range(KC)] + ["asq", r0, r1],
                 writes=[r0, r1])
            ln_smalls(st, [])

            zf_all = ["zf%d" % o for o in range(KC)]
            def f(v):
                r = None
                for oc in range(KC):
                    r = v.tensor_tensor(tm8[:, oc * CH:(oc + 1) * CH],
                                        zf[:, oc * CH:(oc + 1) * CH], b0[:], ALU.mult)
                return r
            P.op("vector", f, reads=zf_all + [r0, r1, "tm8"], writes=["tm8"])
            def f(v):
                r = None
                for oc in range(KC):
                    dst = tm8[:, oc * CH:(oc + 1) * CH]
                    r = v.tensor_tensor(dst, dst, b1[:], ALU.subtract)
                return r
            P.op("vector", f, reads=[r1, "tm8"], writes=["tm8"])
            def f(sc):
                r = None
                for oc in range(KC):
                    r = sc.activation(zf[:, oc * CH:(oc + 1) * CH],
                                      tm8[:, oc * CH:(oc + 1) * CH], AF.Identity,
                                      bias=b_fm[:, oc:oc + 1], scale=g_fm[:, oc:oc + 1])
                return r
            P.op("scalar", f, reads=["tm8", "vec"] + zf_all, writes=zf_all)
            def f(v):
                r = None
                for oc in range(KC):
                    blk = zf[:, oc * CH:(oc + 1) * CH]
                    r = v.tensor_scalar(blk, blk, -CLAMP, CLAMP, ALU.max, ALU.min)
                return r
            P.op("vector", f, reads=zf_all, writes=zf_all)
            def f(v, ub=ub):
                r = None
                for oc in range(KC):
                    blk = zf[:, oc * CH:(oc + 1) * CH]
                    r = v.tensor_tensor(blk, blk,
                                        ufc_s[:, ub * KC * CH + oc * CH: ub * KC * CH + (oc + 1) * CH],
                                        ALU.add)
                return r
            P.op("vector", f, reads=zf_all + ["ufc%d" % ub], writes=zf_all + ["ufc%d" % ub])
            def f(sc):
                r = None
                for oc in range(KC):
                    r = sc.activation(zq8[:, oc * CH:(oc + 1) * CH],
                                      zf[:, oc * CH:(oc + 1) * CH], AF.Square)
                return r
            P.op("scalar", f, reads=zf_all + ["zq"], writes=["zq"])

            def f(t):
                for oc in range(KC):
                    t.matmul(b0[0:1, :], ones_p[:], zf[:, oc * CH:(oc + 1) * CH],
                             start=(oc == 0), stop=(oc == KC - 1))
                r = None
                for oc in range(KC):
                    r = t.matmul(b1[0:1, :], ones_b[:], zq8[:, oc * CH:(oc + 1) * CH],
                                 start=(oc == 0), stop=(oc == KC - 1))
                return r
            P.op("tensor", f, reads=["ones"] + ["zf%d" % o for o in range(KC)] + ["zq", r0, r1],
                 writes=[r0, r1])
            ln_smalls(st, [])

            def f(v):
                r = None
                for oc in range(KC):
                    r = v.tensor_tensor(tm8[:, oc * CH:(oc + 1) * CH],
                                        zf[:, oc * CH:(oc + 1) * CH], b0[:], ALU.mult)
                return r
            P.op("vector", f, reads=[r0, "tm8"] + ["zf%d" % o for o in range(KC)],
                 writes=["tm8"])
            def f(v):
                r = None
                for oc in range(KC):
                    r = v.tensor_tensor(x1c[:, oc * CH:(oc + 1) * CH],
                                        tm8[:, oc * CH:(oc + 1) * CH], b1[:], ALU.subtract)
                return r
            P.op("vector", f, reads=[r1, "tm8", "x1c"], writes=["x1c"])
            def f(sy, c=c):
                return sy.dma_start(kview(x1_h[:, c * CH:(c + 1) * CH]),
                                    x1c[:].rearrange("p (k t) -> p k t", k=KC))
            P.op("sync", f, reads=["x1c"], writes=["x1h%d" % c], dma=True, stream="x1sp")

        emit_load_cast_scan(0)
        for c in range(NCH):
            emit_y(c)
            if c + 1 < NCH:
                emit_load_cast_scan(c + 1)
            emit_epilogue(c)
        P.flush()

    P.barrier()

    with ExitStack() as fctx:
        fsb = lambda n, s, d_: fctx.enter_context(nc.sbuf_tensor(n, s, d_))
        wq_s = fsb("wq_s", [128, KC * 512], BF16)
        x1b = fsb("x1b", [128, KC * CH], BF16)
        q_s = fsb("q_s", [128, 4 * CH], BF16)
        sims = fsb("sims", [128, 512], F32)
        emx = fsb("emx", [128, 64], BF16)
        mx8 = fsb("mx8", [128, 64], F32)
        r88 = fsb("r88", [128, 4 * 64], F32)
        r8 = fsb("r8", [128, 32], F32)
        den4 = fsb("den4", [128, 4], F32)
        rdt = fsb("rdt", [128, 16], F32)
        thr4 = fsb("thr4", [128, 16], F32)
        expuv = fsb("expuv", [128, 4 * 512], BF16)
        gall = fsb("gall", [128, 4 * E], BF16)
        p1 = [fsb("p1%d" % i, [128, 1024], F32) for i in range(2)]
        q2 = fsb("q2", [128, 1024], BF16)
        argc = [fsb("argc%d" % i, [128, 1024], BF16) for i in range(2)]
        wsil = [fsb("wsil%d" % i, [128, 1024], BF16) for i in range(2)]
        ssum = fsb("ssum", [128, 4 * E], BF16)
        big32 = fsb("big32", [128, 32 * CH], BF16)
        expert = fsb("expert", [128, KC * CH], BF16)
        zf2 = fsb("zf2", [128, KC * CH], BF16)
        zsq = fsb("zsq", [128, KC * CH], BF16)
        otmp = fsb("otmp", [128, 2 * CH], F32)
        stat2 = fsb("stat2", [1, 2 * CH], F32)
        wstr = fsb("wstr", [128, 2 * KC * 512], BF16)
        ustr = fsb("ustr", [128, 2 * 4 * 512], BF16)
        uestr = fsb("uestr", [128, 2 * 4 * 512], BF16)
        dstr = wstr

        def f(g):
            return g.dma_start(wq_s[:].rearrange("p (k c) -> p k c", k=KC), kview(wqt))
        P.op("gpsimd", f, writes=["wq"], dma=True, stream="wq")
        P.op("gpsimd", lambda g: g.dma_start(det_b, det), writes=["det_b"], dma=True, stream="pc_det")
        P.op("gpsimd", lambda g: g.dma_start(wupt_b, wupt), writes=["wupt_b"], dma=True, stream="pc_wup")
        P.op("gpsimd", lambda g: g.dma_start(wdnt_b, wdnt), writes=["wdnt_b"], dma=True, stream="pc_wdn")
        P.op("gpsimd", lambda g: g.dma_start(uem_b, uem), writes=["uem_b"], dma=True, stream="pc_ue")

        fps = lambda n, sh, d_: fctx.enter_context(nc.psum_tensor(n, sh, d_))
        pqs = fps("pqs", [128, 512], F32)
        pg = [fps("pg%d" % i, [128, 512], F32) for i in range(2)]
        ptr = fps("ptr", [128, 1024], BF16)
        pbig = fps("pbig", [128, 2048], F32)

        def emit_tail(s):
            def f(t):
                for oc in range(KC):
                    t.matmul(pg[0][0:1, :], ones_b[:], zf2[:, oc * CH:(oc + 1) * CH],
                             start=(oc == 0), stop=(oc == KC - 1))
                r = None
                for oc in range(KC):
                    r = t.matmul(pg[1][0:1, :], ones_b[:], zsq[:, oc * CH:(oc + 1) * CH],
                                 start=(oc == 0), stop=(oc == KC - 1))
                return r
            P.op("tensor", f, reads=["ones", "zf2", "zsq", "pg0", "pg1"], writes=["pg0", "pg1"])

            aux2 = otmp[0:1, 0:CH]
            def f(v):
                v.tensor_scalar(stat2[:, 0:CH], pg[0][0:1, :], 1.0 / D, None, ALU.mult)
                return v.tensor_scalar(stat2[:, CH:2 * CH], pg[1][0:1, :], 1.0 / D, None, ALU.mult)
            P.op("vector", f, reads=["pg0", "pg1", "stat2"], writes=["pg0", "pg1", "stat2"])
            P.op("vector", lambda v: v.tensor_tensor(aux2, stat2[:, 0:CH], stat2[:, 0:CH], ALU.mult),
                 reads=["stat2", "ot0"], writes=["ot0"])
            P.op("vector", lambda v: v.scalar_tensor_tensor(stat2[:, CH:2 * CH],
                                                            stat2[:, CH:2 * CH], EPS, aux2,
                                                            ALU.add, ALU.subtract),
                 reads=["stat2", "ot0"], writes=["stat2"])
            P.op("vector", lambda v: v.reciprocal(aux2, stat2[:, CH:2 * CH]),
                 reads=["stat2", "ot0"], writes=["ot0"])
            P.op("scalar", lambda sc: sc.activation(stat2[:, CH:2 * CH], otmp[0:1, 0:CH], AF.Sqrt),
                 reads=["stat2", "ot0"], writes=["stat2"])
            P.op("vector", lambda v: v.tensor_tensor(stat2[:, 0:CH], stat2[:, 0:CH],
                                                     stat2[:, CH:2 * CH], ALU.mult),
                 reads=["stat2"], writes=["stat2"])

            def f(t):
                t.matmul(pg[0][:], ones_r[:], stat2[:, CH:2 * CH], start=True, stop=True)
                return t.matmul(pg[1][:], ones_r[:], stat2[:, 0:CH], start=True, stop=True)
            P.op("tensor", f, reads=["stat2", "ones", "pg0", "pg1"], writes=["pg0", "pg1"])

            for oc in range(KC):
                ob = "ot%d" % (oc % 2)

                def f(v, oc=oc):
                    dst = otmp[:, (oc % 2) * CH:((oc % 2) + 1) * CH]
                    return v.tensor_tensor(dst, zf2[:, oc * CH:(oc + 1) * CH], pg[0][:], ALU.mult)
                P.op("vector", f, reads=["zf2", "pg0", ob], writes=[ob])
                def f(v, oc=oc):
                    dst = otmp[:, (oc % 2) * CH:((oc % 2) + 1) * CH]
                    return v.tensor_tensor(dst, dst, pg[1][:], ALU.subtract)
                P.op("vector", f, reads=["pg1", ob], writes=[ob])

                def f(sy, s=s, oc=oc):
                    return sy.dma_start(out_h[oc * 128:(oc + 1) * 128, s * CH:(s + 1) * CH],
                                        otmp[:, (oc % 2) * CH:((oc % 2) + 1) * CH])
                P.op("sync", f, reads=[ob], writes=["outw", "outw_" + ob], dma=True, stream="out_" + ob)

        for s in range(NCH):
            if True:

                def f(sy, s=s):
                    return sy.dma_start(x1b[:].rearrange("p (k t) -> p k t", k=KC),
                                        kview(x1_h[:, s * CH:(s + 1) * CH]))
                P.op("sync", f, reads=["x1h%d" % s], writes=["x1b"], dma=True, stream="x1b")

                for ec in range(4):
                    def f(t, ec=ec):
                        r = None
                        for kc in range(KC):
                            r = t.matmul(pqs[:], wq_s[:, kc * 512 + ec * 128: kc * 512 + (ec + 1) * 128],
                                         x1b[:, kc * CH:(kc + 1) * CH],
                                         start=(kc == 0), stop=(kc == KC - 1))
                        return r
                    P.op("tensor", f, reads=["wq", "x1b"], writes=["pqs"])
                    def f(sc, ec=ec):
                        return sc.activation(q_s[:, ec * CH:(ec + 1) * CH], pqs[:], AF.Copy)
                    P.op("scalar", f, reads=["pqs", "q"], writes=["pqs", "q"])

                for t4 in range(4):
                    def f(t, t4=t4):
                        r = None
                        for p in range(2):
                            for h in range(HEADS):
                                jb = p * 4 + h
                                ecq = 2 * p + h // 2
                                r = t.matmul(pqs[:, jb * 64:(jb + 1) * 64],
                                             q_s[:, ecq * CH + t4 * 128: ecq * CH + (t4 + 1) * 128],
                                             keys_sb[:, jb * 64:(jb + 1) * 64],
                                             start=True, stop=True)
                        return r
                    P.op("tensor", f, reads=["keys", "q", "pqs"], writes=["pqs"])
                    def f(sc, t4=t4):
                        return sc.activation(sims[:], pqs[:], AF.Copy)
                    P.op("scalar", f, reads=["pqs", "sims"], writes=["pqs", "sims"])
                    def f(sc, t4=t4):
                        return sc.activation(expuv[:, t4 * 512:(t4 + 1) * 512], sims[:], AF.Exp)
                    P.op("scalar", f, reads=["sims", "expuv%d" % t4], writes=["expuv%d" % t4])

                    def f(v):
                        r = None
                        for jb in range(8):
                            r = v.max(out=mx8[:, jb * 8:(jb + 1) * 8], in_=sims[:, jb * 64:(jb + 1) * 64])
                        return r
                    P.op("vector", f, reads=["sims", "mx8"], writes=["mx8"])
                    P.op("scalar", lambda sc: sc.activation(emx[:], mx8[:], AF.Exp),
                         reads=["mx8", "emx"], writes=["emx"])

                    def f(v):
                        r = None
                        for h in range(HEADS):
                            xb_ = emx[:, h * 8:(h + 1) * 8]
                            yb_ = emx[:, (4 + h) * 8:(4 + h) * 8 + 8]
                            r = v.tensor_tensor(r88[:, h * 64:(h + 1) * 64]
                                                .rearrange("p (a b) -> p a b", a=8),
                                                xb_[:, :, None].to_broadcast([128, 8, 8]),
                                                yb_[:, None, :].to_broadcast([128, 8, 8]), ALU.mult)
                        return r
                    P.op("vector", f, reads=["emx", "r88"], writes=["r88"])
                    def f(v):
                        r = None
                        for h in range(HEADS):
                            r = v.max(out=r8[:, h * 8:(h + 1) * 8], in_=r88[:, h * 64:(h + 1) * 64])
                        return r
                    P.op("vector", f, reads=["r88", "r8"], writes=["r8"])
                    def f(v):
                        r = None
                        for h in range(HEADS):
                            r = v.reduce_sum(den4[:, h:h + 1], r8[:, h * 8:(h + 1) * 8], axis=AX.X)
                        return r
                    P.op("vector", f, reads=["r8", "den4"], writes=["den4"])

                    def f(v, t4=t4):
                        v.reciprocal(rdt[:, t4 * 4:(t4 + 1) * 4], den4[:])
                        r = None
                        for h in range(HEADS):
                            r = v.tensor_scalar(thr4[:, t4 * 4 + h: t4 * 4 + h + 1],
                                                r8[:, h * 8 + 7: h * 8 + 8], MARGIN, None, ALU.mult)
                        return r
                    P.op("vector", f, reads=["den4", "r8", "rdt%d" % t4, "thr%d" % t4],
                         writes=["rdt%d" % t4, "thr%d" % t4])

                    def f(v, t4=t4):
                        r = None
                        for jb in range(8):
                            blk = expuv[:, t4 * 512 + jb * 64: t4 * 512 + (jb + 1) * 64]
                            r = v.scalar_tensor_tensor(
                                blk, sims[:, jb * 64:(jb + 1) * 64],
                                mx8[:, jb * 8 + 7: jb * 8 + 8],
                                blk, ALU.is_ge, ALU.mult)
                        return r
                    P.op("vector", f, reads=["sims", "mx8", "expuv%d" % t4],
                         writes=["expuv%d" % t4])

                if s > 0:
                    emit_tail(s - 1)

                plane_it = {"it": 0, "pend": None, "hb": 0}
                up_q = list(range(8))

                def emit_planes(t4):
                    for h in range(HEADS):
                        for ecc in range(4):
                            pp = plane_it["it"] % 2
                            plane_it["it"] += 1

                            def f(e, t4=t4, h=h, ecc=ecc, pp=pp):
                                xb_ = expuv[:, t4 * 512 + h * 64 + ecc * 16: t4 * 512 + h * 64 + (ecc + 1) * 16]
                                yb_ = expuv[:, t4 * 512 + (4 + h) * 64: t4 * 512 + (4 + h) * 64 + 64]
                                return e.tensor_tensor(p1[pp][:].rearrange("p (a b) -> p a b", a=16),
                                                       xb_[:, :, None].to_broadcast([128, 16, 64]),
                                                       yb_[:, None, :].to_broadcast([128, 16, 64]), ALU.mult)
                            P.op("gpsimd", f,
                                 reads=["expuv%d" % t4, "p1%d" % pp], writes=["p1%d" % pp])
                            if plane_it["pend"] is not None:
                                plane_it["pend"]()
                                plane_it["pend"] = None

                            def f(v, t4=t4, h=h, pp=pp):
                                return v.scalar_tensor_tensor(q2[:], p1[pp][:],
                                                              thr4[:, t4 * 4 + h: t4 * 4 + h + 1],
                                                              p1[pp][:], ALU.is_ge, ALU.mult)
                            P.op("vector", f, reads=["p1%d" % pp, "thr%d" % t4, "q2"],
                                 writes=["q2"])
                            def f(v, t4=t4, ecc=ecc, pp=pp):
                                return v.tensor_tensor(argc[pp][:], q2[:],
                                                       gall[:, t4 * E + ecc * 1024: t4 * E + (ecc + 1) * 1024],
                                                       ALU.mult)
                            P.op("vector", f,
                                 reads=["q2", "gall%d_%d" % (t4, 2 * ecc), "gall%d_%d" % (t4, 2 * ecc + 1),
                                        "argc%d" % pp],
                                 writes=["argc%d" % pp])

                            sres = "ss%d_%d" % (t4, ecc)
                            if h == 0:
                                def f(sc, t4=t4, ecc=ecc, pp=pp):
                                    return sc.activation(ssum[:, t4 * E + ecc * 1024: t4 * E + (ecc + 1) * 1024],
                                                         argc[pp][:], AF.Silu, scale=rdt[:, t4 * 4: t4 * 4 + 1])
                                P.op("scalar", f, reads=["argc%d" % pp, "rdt%d" % t4, sres], writes=[sres])
                            else:
                                def f(sc, t4=t4, h=h, pp=pp):
                                    return sc.activation(wsil[pp][:], argc[pp][:], AF.Silu,
                                                         scale=rdt[:, t4 * 4 + h: t4 * 4 + h + 1])
                                P.op("scalar", f, reads=["argc%d" % pp, "rdt%d" % t4, "wsil%d" % pp],
                                     writes=["wsil%d" % pp])
                                def mk_add(t4=t4, ecc=ecc, pp=pp, sres=sres):
                                    def f(g):
                                        dst = ssum[:, t4 * E + ecc * 1024: t4 * E + (ecc + 1) * 1024]
                                        return g.tensor_tensor(dst, dst, wsil[pp][:], ALU.add)
                                    P.op("vector", f, reads=["wsil%d" % pp, sres], writes=[sres])
                                plane_it["pend"] = mk_add
                        plane_it["hb"] += 1
                        if plane_it["hb"] % 2 == 0 and up_q:
                            emit_up_group(up_q.pop(0))

                def emit_up_group(icg):
                    db = "wstr%d" % (icg % 2)
                    doff = (icg % 2) * KC * 512

                    def f(sy, icg=icg, doff=doff):
                        dst = dstr[:, doff:doff + KC * 512]
                        return sy.dma_start(
                            dst.rearrange("p (k i c) -> p k i c", k=KC, i=4),
                            wupt_b[:, icg * 512:(icg + 1) * 512]
                            .rearrange("(k p) (i c) -> p k i c", p=128, i=4))
                    P.op("sync", f, reads=["wupt_b", db], writes=[db], dma=True, stream=db)

                    for i4 in range(4):
                        ic = icg * 4 + i4
                        bank, pb = gbanks[ic % 4]

                        def f(t, i4=i4, ic=ic, doff=doff, bank=bank):
                            r = None
                            for kc in range(KC):
                                r = t.matmul(bank,
                                             dstr[:, doff + kc * 512 + i4 * 128: doff + kc * 512 + (i4 + 1) * 128],
                                             x1b[:, kc * CH:(kc + 1) * CH],
                                             start=(kc == 0), stop=(kc == KC - 1))
                            return r
                        P.op("tensor", f, reads=[db, "x1b"], writes=[pb])

                        def f(sc, ic=ic, bank=bank):
                            return sc.activation(big32[:, ic * CH:(ic + 1) * CH], bank, AF.Silu)
                        P.op("scalar", f, reads=[pb, "big32"], writes=[pb, "big32"])

                gbanks = [(pg[0][:], "pg0"), (pg[1][:], "pg1"),
                          (pbig[:, 0:512], "pbig0"), (pbig[:, 512:1024], "pbig1")]
                for ec in range(8):
                    wb = "wstr%d" % (ec % 2)
                    off = (ec % 2) * KC * 512

                    def f(sy, ec=ec, off=off):
                        dst = wstr[:, off:off + KC * 512]
                        return sy.dma_start(dst.rearrange("p (k c) -> p k c", k=KC),
                                            kview(det_b[:, ec * 512:(ec + 1) * 512]))
                    P.op("sync", f, reads=["det_b", wb], writes=[wb], dma=True, stream=wb)
                    for t4 in range(4):
                        bank, pb = gbanks[(ec * 4 + t4) % 4]

                        def f(t, ec=ec, t4=t4, off=off, bank=bank):
                            r = None
                            for kc in range(KC):
                                r = t.matmul(bank,
                                             x1b[:, kc * CH + t4 * 128: kc * CH + (t4 + 1) * 128],
                                             wstr[:, off + kc * 512: off + (kc + 1) * 512],
                                             start=(kc == 0), stop=(kc == KC - 1))
                            return r
                        P.op("tensor", f, reads=[wb, "x1b"], writes=[pb])
                        def f(sc, ec=ec, t4=t4, bank=bank):
                            return sc.activation(gall[:, t4 * E + ec * 512: t4 * E + (ec + 1) * 512],
                                                 bank, AF.Copy)
                        P.op("scalar", f, reads=[pb, "gall%d_%d" % (t4, ec)],
                             writes=[pb, "gall%d_%d" % (t4, ec)])
                for t4 in range(4):
                    emit_planes(t4)

                dn1 = [(pqs, "pqs"), (pg[0], "pg0"), (pg[1], "pg1")]
                for dcg in range(2):
                    for icg in range(8):
                        ub_ = "ustr%d" % (icg % 2)
                        uoff = (icg % 2) * 4 * 512

                        def f(sy, dcg=dcg, icg=icg, uoff=uoff):
                            return sy.dma_start(
                                ustr[:, uoff:uoff + 4 * 512].rearrange("p (i c) -> p i c", i=4),
                                wdnt_b[icg * 512:(icg + 1) * 512, dcg * 512:(dcg + 1) * 512]
                                .rearrange("(i p) c -> p i c", p=128))
                        P.op("sync", f, reads=["wdnt_b", ub_], writes=[ub_], dma=True, stream=ub_)

                        def f(t, dcg=dcg, icg=icg, uoff=uoff):
                            r = None
                            for i4 in range(4):
                                ic = icg * 4 + i4
                                for dc4 in range(4 if dcg == 0 else 3):
                                    dst = pbig[:, dc4 * 512:(dc4 + 1) * 512] if dcg == 0 \
                                        else dn1[dc4][0][:]
                                    r = t.matmul(dst,
                                                 ustr[:, uoff + i4 * 512 + dc4 * 128: uoff + i4 * 512 + (dc4 + 1) * 128],
                                                 big32[:, ic * CH:(ic + 1) * CH],
                                                 start=(ic == 0), stop=(ic == 31))
                            return r
                        wl = ["pbig"] if dcg == 0 else [n for _, n in dn1]
                        P.op("tensor", f, reads=[ub_, "big32"] + wl, writes=wl)

                def f(sy):
                    return sy.dma_start(
                        ustr[:, 0:4096].rearrange("p (i c) -> p i c", i=32),
                        wdnt_b[:, 7 * 128:8 * 128].rearrange("(i p) c -> p i c", p=128))
                P.op("sync", f, reads=["ustr0", "ustr1"], writes=["ustr0", "ustr1"],
                     dma=True, stream="ustr0")

                for dc4 in range(4):
                    def f(v, dc4=dc4):
                        return v.tensor_tensor(zf2[:, dc4 * CH:(dc4 + 1) * CH],
                                               pbig[:, dc4 * 512:(dc4 + 1) * 512],
                                               x1b[:, dc4 * CH:(dc4 + 1) * CH], ALU.add)
                    P.op("vector", f, reads=["pbig", "x1b", "zf2"], writes=["pbig", "zf2"])
                for dc4 in range(3):
                    bank, bn = dn1[dc4]

                    def f(v, dc4=dc4, bank=bank):
                        oc = 4 + dc4
                        return v.tensor_tensor(zf2[:, oc * CH:(oc + 1) * CH],
                                               bank[:],
                                               x1b[:, oc * CH:(oc + 1) * CH], ALU.add)
                    P.op("vector", f, reads=[bn, "x1b", "zf2"], writes=[bn, "zf2"])

                def f(t):
                    r = None
                    for ic in range(32):
                        r = t.matmul(pbig[:, 0:512],
                                     ustr[:, ic * 128:(ic + 1) * 128],
                                     big32[:, ic * CH:(ic + 1) * CH],
                                     start=(ic == 0), stop=(ic == 31))
                    return r
                P.op("tensor", f, reads=["ustr0", "ustr1", "big32", "pbig"], writes=["pbig"])

                def f(v):
                    return v.tensor_tensor(zf2[:, 7 * CH:8 * CH], pbig[:, 0:512],
                                           x1b[:, 7 * CH:8 * CH], ALU.add)
                P.op("vector", f, reads=["pbig", "x1b", "zf2"], writes=["pbig", "zf2"])

                def emit_eup(dcg, ec):
                    ub_ = "uestr%d" % (ec % 4)
                    uoff = (ec % 4) * 512

                    def f(sy, dcg=dcg, ec=ec, uoff=uoff):
                        return sy.dma_start(uestr[:, uoff:uoff + 512],
                                            uem_b[ec * 128:(ec + 1) * 128, dcg * 512:(dcg + 1) * 512])
                    P.op("sync", f, reads=["uem_b", ub_], writes=[ub_], dma=True, stream=ub_)

                    def f(t, ec=ec, uoff=uoff):
                        r = None
                        for dc4 in range(4):
                            r = t.matmul(pbig[:, dc4 * 512:(dc4 + 1) * 512],
                                         uestr[:, uoff + dc4 * 128: uoff + (dc4 + 1) * 128],
                                         big32[:, ec * CH:(ec + 1) * CH],
                                         start=(ec == 0), stop=(ec == 31))
                        return r
                    P.op("tensor", f, reads=[ub_, "big32"], writes=["pbig"])

                for grp in range(4):
                    for t4 in range(4):
                        def f(t, t4=t4, grp=grp):
                            r = None
                            for j in range(8):
                                eb = grp * 8 + j
                                r = t.transpose(ptr[:, j * 128:(j + 1) * 128],
                                                ssum[:, t4 * E + eb * 128: t4 * E + (eb + 1) * 128],
                                                id_b[:])
                            return r
                        P.op("tensor", f, reads=["idb", "ptr"] + ["ss%d_%d" % (t4, e) for e in range(4)],
                             writes=["ptr"])

                        def f(sc, t4=t4, grp=grp):
                            dst = big32[:].rearrange("p (e t) -> p e t", e=32)[:, grp * 8:(grp + 1) * 8,
                                                                              t4 * 128:(t4 + 1) * 128]
                            return sc.activation(dst, ptr[:].rearrange("p (j c) -> p j c", j=8), AF.Copy)
                        P.op("scalar", f, reads=["ptr", "big32"], writes=["ptr", "big32"])
                    for ec in range(8 * grp, 8 * grp + 8):
                        emit_eup(0, ec)
                for dc4 in range(4):
                    def f(sc, dc4=dc4):
                        return sc.activation(expert[:, dc4 * CH:(dc4 + 1) * CH],
                                             pbig[:, dc4 * 512:(dc4 + 1) * 512], AF.Copy)
                    P.op("scalar", f, reads=["pbig", "expert"], writes=["pbig", "expert"])
                for ec in range(32):
                    emit_eup(1, ec)
                for dc4 in range(4):
                    def f(sc, dc4=dc4):
                        oc = 4 + dc4
                        return sc.activation(expert[:, oc * CH:(oc + 1) * CH],
                                             pbig[:, dc4 * 512:(dc4 + 1) * 512], AF.Copy)
                    P.op("scalar", f, reads=["pbig", "expert"], writes=["pbig", "expert"])

                def f(v):
                    r = None
                    for oc in range(KC):
                        r = v.tensor_tensor(zf2[:, oc * CH:(oc + 1) * CH],
                                            zf2[:, oc * CH:(oc + 1) * CH],
                                            expert[:, oc * CH:(oc + 1) * CH], ALU.add)
                    return r
                P.op("vector", f, reads=["zf2", "expert"], writes=["zf2"])
                def f(sc):
                    r = None
                    for oc in range(KC):
                        r = sc.activation(zsq[:, oc * CH:(oc + 1) * CH],
                                          zf2[:, oc * CH:(oc + 1) * CH], AF.Square)
                    return r
                P.op("scalar", f, reads=["zf2", "zsq"], writes=["zsq"])
                P.flush()
        emit_tail(NCH - 1)
        P.flush()

    P.barrier()
    return nc, ctx


def shard_inputs(inputs):
    f32 = lambda x: np.ascontiguousarray(np.asarray(x, dtype=np.float32))
    u = f32(inputs["L0_attn_u"])
    Bm = f32(inputs["L0_attn_Bmat"])
    Cm = f32(inputs["L0_attn_Cmat"])
    Dm = f32(inputs["L0_attn_Dmat"])
    A = f32(inputs["L0_attn_A_diag"])
    h0 = f32(inputs["L0_attn_h0"])
    gam = f32(inputs["L0_attn_ln_gamma"])
    bet = f32(inputs["L0_attn_ln_beta"])
    Wup = f32(inputs["L0_ffn_W_up"])
    Wdn = f32(inputs["L0_ffn_W_down"])
    Wq = f32(inputs["L0_ffn_W_q"])
    keys = f32(inputs["L0_ffn_keys"])
    de = f32(inputs["L0_ffn_down_embed"])
    ue = f32(inputs["L0_ffn_up_embed"])

    fm = lambda v: np.ascontiguousarray(v.reshape(KC, 128).T)
    kp = np.zeros((128, 512), np.float32)
    for p in range(2):
        for h in range(HEADS):
            jb = p * 4 + h
            base = (h % 2) * 64
            kp[base:base + 64, jb * 64:(jb + 1) * 64] = keys[h, :, p, :].T

    shared = {
        "bmt": np.ascontiguousarray(Bm.T), "cmt": np.ascontiguousarray(Cm.T),
        "dmt": np.ascontiguousarray(Dm.T), "wqt": np.ascontiguousarray(Wq.T),
        "keyspad": kp, "det": np.ascontiguousarray(de.T), "uem": ue,
        "wupt": np.ascontiguousarray(Wup.T), "wdnt": np.ascontiguousarray(Wdn.T),
        "identm": np.eye(128, dtype=np.float32),
        "afmv": fm(A), "h0fm": fm(h0), "gfm": fm(gam), "bfm": fm(bet),
    }
    zeros_dt = np.zeros((D, T), np.float32)
    zkc = np.zeros((128, KC), np.float32)
    in_maps = []
    for c in range(8):
        b, half = c // 2, c % 2
        m = dict(shared)
        m["u_own"] = np.ascontiguousarray(u[b, half * T:(half + 1) * T].T)
        m["u_pre"] = np.ascontiguousarray(u[b, :T].T) if half else zeros_dt
        m["selv"] = np.full((128, 1), float(half), np.float32)
        m["h0v2"] = zkc if half else fm(h0)
        in_maps.append(m)
    return in_maps


def kernel(**inputs):
    in_maps = shard_inputs(inputs)
    nc, ctx = build_nc(debug=False)
    ctx.close()
    res = run_bass_kernel_spmd(nc, in_maps, core_ids=list(range(8)))
    out = np.zeros((4, 4096, D), np.float32)
    for c in range(8):
        b, half = c // 2, c % 2
        out[b, half * T:(half + 1) * T] = np.asarray(res.results[c]["out"]).T
    return out


# version 30
# speedup vs baseline: 1.0702x; 1.0203x over previous
import numpy as np
from contextlib import ExitStack

import concourse.bass as bass
import concourse.mybir as mybir
from concourse.bass_utils import run_bass_kernel_spmd

F32 = mybir.dt.float32
BF16 = mybir.dt.bfloat16
AF = mybir.ActivationFunctionType
ALU = mybir.AluOpType
AX = mybir.AxisListType

D = 1024
T = 2048
CH = 512
NCH = T // CH
KC = D // 128
E = 4096
HEADS = 4
EPS = 1e-5
CLAMP = 1e6
MARGIN = 1.0

ENGS = ["tensor", "vector", "scalar", "gpsimd", "sync"]
ESEM = {"tensor": "pe", "vector": "dve", "scalar": "act", "gpsimd": "pool", "sync": "sp"}


class Prog:

    def __init__(self, nc, ctx):
        self.nc = nc
        self.ctx = ctx
        self.block = ctx.enter_context(nc.Block())
        self.sems = {}
        for s in ["pe", "dve", "act", "pool", "sp"]:
            self.sems[s] = ctx.enter_context(nc.semaphore(s))
        self.cnt = {s: 0 for s in self.sems}
        self.last_w = {}
        self.readers = {}
        self.waited = {e: {s: 0 for s in self.sems} for e in ENGS}
        self.pending = {e: [] for e in ENGS}

    def dma_sem(self, stream):
        name = "dq_" + stream
        if name not in self.sems:
            self.sems[name] = self.ctx.enter_context(self.nc.semaphore(name))
            self.cnt[name] = 0
            for e in ENGS:
                self.waited[e][name] = 0
        return name

    def flush(self, engs=None):
        for e in (ENGS if engs is None else engs):
            items = self.pending[e]
            if not items:
                continue
            self.pending[e] = []

            def section(handle, items=items):
                for waits, build, sem, k, drain, new_cnt in items:
                    for s_, c_ in waits:
                        handle.wait_ge(self.sems[s_], c_)
                    ins = build(handle)
                    assert ins is not None
                    if isinstance(ins, list):
                        assert 16 * len(ins) == k
                        for i_ in ins:
                            i_.then_inc(self.sems[sem], 16)
                    else:
                        ins.then_inc(self.sems[sem], k)
                    if drain:
                        handle.wait_ge(self.sems[sem], new_cnt)
            getattr(self.block, e)(section)

    def _need(self, eng, reads, writes):
        need = {}

        def add(tok):
            if tok is None:
                return
            s, c, weng = tok
            if weng == eng and not s.startswith("dq_"):
                return
            need[s] = max(need.get(s, 0), c)

        for r in reads:
            add(self.last_w.get(r))
        for w in writes:
            add(self.last_w.get(w))
            for s, c in self.readers.get(w, {}).items():
                need[s] = max(need.get(s, 0), c)
        need.pop(ESEM[eng], None)
        out = []
        for s, c in sorted(need.items()):
            if c > self.waited[eng][s]:
                out.append((s, c))
                self.waited[eng][s] = c
        return out

    def op(self, eng, build, reads=(), writes=(), dma=False, drain=False, stream=None,
           pre_drain=False):
        if dma:
            sem = self.dma_sem(stream or "misc")
        else:
            sem = ESEM[eng]
        waits = self._need(eng, reads, writes)
        if (pre_drain or eng in ("vector", "scalar", "gpsimd")) and not dma \
                and self.cnt[ESEM[eng]] > self.waited[eng][ESEM[eng]]:
            waits.insert(0, (ESEM[eng], self.cnt[ESEM[eng]]))
            self.waited[eng][ESEM[eng]] = self.cnt[ESEM[eng]]
        if dma:
            k = 16 * (build.n_dma if hasattr(build, "n_dma") else 1)
        else:
            k = 1
        new_cnt = self.cnt[sem] + k
        self.pending[eng].append((waits, build, sem, k, drain, new_cnt))
        self.cnt[sem] = new_cnt
        if drain:
            self.waited[eng][sem] = max(self.waited[eng][sem], new_cnt)
        tok = (sem, new_cnt, eng)
        for w in writes:
            self.last_w[w] = tok
            self.readers[w] = {}
        for r in reads:
            d = self.readers.setdefault(r, {})
            d[sem] = max(d.get(sem, 0), new_cnt)
        return tok

    def barrier(self):
        self.flush()
        snap = dict(self.cnt)
        for e in ENGS:
            todo = [(s, c) for s, c in sorted(snap.items()) if c > self.waited[e][s]]

            def f(eng, todo=todo):
                for s, c in todo:
                    eng.wait_ge(self.sems[s], c)
            getattr(self.block, e)(f)
            for s, c in snap.items():
                self.waited[e][s] = max(self.waited[e][s], c)


def build_nc(debug=False):
    nc = bass.Bass("TRN2", target_bir_lowering=False, dynamic_dma_scratch_size=16384)
    dt = lambda n, s, d_, k: nc.dram_tensor(n, s, d_, kind=k).ap()

    u_own = dt("u_own", [D, T], F32, "ExternalInput")
    u_pre = dt("u_pre", [D, T], F32, "ExternalInput")
    selv = dt("selv", [128, 1], F32, "ExternalInput")
    h0v2 = dt("h0v2", [128, KC], F32, "ExternalInput")
    afmv = dt("afmv", [128, KC], F32, "ExternalInput")
    h0fm = dt("h0fm", [128, KC], F32, "ExternalInput")
    gfm = dt("gfm", [128, KC], F32, "ExternalInput")
    bfm = dt("bfm", [128, KC], F32, "ExternalInput")
    bmt = dt("bmt", [D, D], F32, "ExternalInput")
    cmt = dt("cmt", [D, D], F32, "ExternalInput")
    dmt = dt("dmt", [D, D], F32, "ExternalInput")
    wqt = dt("wqt", [D, 512], F32, "ExternalInput")
    keyspad = dt("keyspad", [128, 512], F32, "ExternalInput")
    det = dt("det", [D, E], F32, "ExternalInput")
    uem = dt("uem", [E, D], F32, "ExternalInput")
    wupt = dt("wupt", [D, E], F32, "ExternalInput")
    wdnt = dt("wdnt", [E, D], F32, "ExternalInput")
    identm = dt("identm", [128, 128], F32, "ExternalInput")
    out_h = dt("out", [D, T], F32, "ExternalOutput")

    det_b = dt("det_b", [D, E], BF16, "Internal")
    uem_b = dt("uem_b", [E, D], BF16, "Internal")
    wupt_b = dt("wupt_b", [D, E], BF16, "Internal")
    wdnt_b = dt("wdnt_b", [E, D], BF16, "Internal")
    x1_h = dt("x1_h", [D, T], BF16, "ExternalOutput" if debug else "Internal")
    if debug:
        dbg_vec = dt("dbg_vec", [128, 6 * KC], F32, "ExternalOutput")
        dbg_bmt = dt("dbg_bmt", [128, KC * D], BF16, "ExternalOutput")
        dbg_uop = dt("dbg_uop", [128, KC * CH], BF16, "ExternalOutput")
        dbg_xs = dt("dbg_xs", [128, KC * CH], BF16, "ExternalOutput")
        dbg_act = dt("dbg_act", [128, KC * CH], F32, "ExternalOutput")
        dbg_carry = dt("dbg_carry", [128, KC], F32, "ExternalOutput")
        dbg_bu = dt("dbg_bu", [128, CH], F32, "ExternalOutput")
        dbg_cc = dt("dbg_cc", [128, 4 * KC], F32, "ExternalOutput")
        dbg_cc2 = dt("dbg_cc2", [128, KC * KC], F32, "ExternalOutput")
        dbg_sc1 = dt("dbg_sc1", [128, CH], BF16, "ExternalOutput")

    kview = lambda ap: ap.rearrange("(k p) t -> p k t", p=128)

    ctx = ExitStack()
    sb = lambda n, s, d_: ctx.enter_context(nc.sbuf_tensor(n, s, d_))
    P = Prog(nc, ctx)

    vec = sb("vec", [128, 6 * KC], F32)
    a_fm = vec[:, 0:KC]
    h0_fm = vec[:, KC:2 * KC]
    g_fm = vec[:, 2 * KC:3 * KC]
    b_fm = vec[:, 3 * KC:4 * KC]
    h02_fm = vec[:, 4 * KC:5 * KC]
    spa = vec[:, 5 * KC:6 * KC]
    selv_sb = sb("selv_sb", [128, 1], F32)
    ones_b = sb("ones_b", [128, 1], BF16)
    ones_p = sb("ones_p", [128, 1], F32)
    ones_r = sb("ones_r", [1, 128], F32)
    id_b = sb("id_b", [128, 128], BF16)
    carry = sb("carry", [128, KC], F32)
    init2 = sb("init2", [128, KC], F32)
    keys_sb = sb("keys_sb", [128, 512], BF16)

    def f(sy):
        return [sy.dma_start(vec[:, 0:KC], afmv),
                sy.dma_start(vec[:, KC:2 * KC], h0fm),
                sy.dma_start(vec[:, 2 * KC:3 * KC], gfm),
                sy.dma_start(vec[:, 3 * KC:4 * KC], bfm),
                sy.dma_start(vec[:, 4 * KC:5 * KC], h0v2),
                sy.dma_start(selv_sb[:], selv)]
    f.n_dma = 6
    P.op("sync", f, writes=["vec", "selv"], dma=True, stream="init")

    def f(g):
        return [g.dma_start(id_b[:], identm),
                g.dma_start(keys_sb[:], keyspad)]
    f.n_dma = 2
    P.op("gpsimd", f, writes=["idb", "keys"], dma=True, stream="consts")

    def f(v):
        v.memset(ones_b[:], 1.0)
        v.memset(ones_p[:], 1.0)
        return v.memset(ones_r[:], 1.0)
    P.op("vector", f, writes=["ones"])

    P.op("scalar", lambda sc: sc.activation(spa, vec[:, 0:KC], AF.Exp),
         reads=["vec"], writes=["spa"])
    P.op("vector", lambda v: v.tensor_scalar(spa, spa, 1.0, None, ALU.add),
         reads=["spa"], writes=["spa"])
    P.op("scalar", lambda sc: sc.activation(spa, spa, AF.Ln),
         reads=["spa"], writes=["spa"])
    P.op("vector", lambda v: v.tensor_scalar(a_fm, spa, -1.0, None, ALU.mult),
         reads=["spa"], writes=["vec_a"])
    if debug:
        P.op("sync", lambda sy: sy.dma_start(dbg_vec, vec[:]), reads=["vec", "vec_a"], dma=True)

    with ExitStack() as actx:
        asb = lambda n, s, d_: actx.enter_context(nc.sbuf_tensor(n, s, d_))
        aps = lambda n, s, d_: actx.enter_context(nc.psum_tensor(n, s, d_))
        bmt_s = asb("bmt_s", [128, KC * D], BF16)
        cmt_s = asb("cmt_s", [128, KC * D], BF16)
        dmt_s = asb("dmt_s", [128, KC * D], BF16)
        uop_s = asb("uop_s", [128, 2 * KC * CH], BF16)
        ufc_s = asb("ufc_s", [128, 2 * KC * CH], F32)
        xs_s = asb("xs_s", [128, 2 * KC * CH], BF16)
        scr1 = asb("scr1", [128, CH], BF16)
        asq8 = asb("asq8", [128, KC * CH], BF16)
        zf = asb("zf", [128, KC * CH], F32)
        zq8 = asb("zq8", [128, KC * CH], BF16)
        x1c = asb("x1c", [128, KC * CH], BF16)
        tmp = asb("tmp", [128, 2 * CH], F32)
        tm8 = asb("tm8", [128, KC * CH], F32)
        stat = asb("stat", [1, 6 * CH], F32)
        pbu = [aps("pbu%d" % i, [128, CH], F32) for i in range(2)]
        py = [aps("py%d" % i, [128, CH], F32) for i in range(2)]
        pb4 = [aps("pb4_%d" % i, [128, CH], F32) for i in range(4)]

        wtile = lambda buf, kc, j: buf[:, kc * D + j * 128: kc * D + (j + 1) * 128]

        def f(g):
            return [g.dma_start(bmt_s[:].rearrange("p (k c) -> p k c", k=KC), kview(bmt)),
                    g.dma_start(cmt_s[:].rearrange("p (k c) -> p k c", k=KC), kview(cmt)),
                    g.dma_start(dmt_s[:].rearrange("p (k c) -> p k c", k=KC), kview(dmt))]
        f.n_dma = 3
        P.op("gpsimd", f, writes=["bmt", "cmt", "dmt"], dma=True, stream="bcd")
        if debug:
            P.op("sync", lambda sy: sy.dma_start(dbg_bmt, bmt_s[:]), reads=["bmt"], dma=True)

        def ln_smalls(st, extra_reads):
            o = st * 3 * CH
            b0, b1 = pb4[2 * st], pb4[2 * st + 1]
            sres = "stat%d" % st
            r0, r1 = "pbc%d0" % st, "pbc%d1" % st

            def f(v):
                v.tensor_scalar(stat[:, o:o + CH], b0[0:1, :], 1.0 / D, None, ALU.mult)
                return v.tensor_scalar(stat[:, o + CH:o + 2 * CH], b1[0:1, :], 1.0 / D, None, ALU.mult)
            P.op("vector", f, reads=list(extra_reads) + [sres, r0, r1], writes=[sres])
            P.op("vector", lambda v: v.tensor_tensor(stat[:, o + 2 * CH:o + 3 * CH],
                                                     stat[:, o:o + CH], stat[:, o:o + CH], ALU.mult),
                 reads=[sres], writes=[sres])
            P.op("vector", lambda v: v.scalar_tensor_tensor(stat[:, o + CH:o + 2 * CH],
                                                            stat[:, o + CH:o + 2 * CH], EPS,
                                                            stat[:, o + 2 * CH:o + 3 * CH],
                                                            ALU.add, ALU.subtract),
                 reads=[sres], writes=[sres])
            P.op("vector", lambda v: v.reciprocal(stat[:, o + 2 * CH:o + 3 * CH],
                                                  stat[:, o + CH:o + 2 * CH]),
                 reads=[sres], writes=[sres])
            P.op("scalar", lambda sc: sc.activation(stat[:, o + CH:o + 2 * CH],
                                                    stat[:, o + 2 * CH:o + 3 * CH], AF.Sqrt),
                 reads=[sres], writes=[sres])
            P.op("vector", lambda v: v.tensor_tensor(stat[:, o:o + CH], stat[:, o:o + CH],
                                                     stat[:, o + CH:o + 2 * CH], ALU.mult),
                 reads=[sres], writes=[sres])

            def f(t):
                t.matmul(b0[:], ones_r[:], stat[:, o + CH:o + 2 * CH], start=True, stop=True)
                return t.matmul(b1[:], ones_r[:], stat[:, o:o + CH], start=True, stop=True)
            P.op("tensor", f, reads=[sres, "ones", r0, r1], writes=[r0, r1])

        for c in range(NCH):
            ub1 = c % 2
            uf1 = "ufc%d" % ub1
            up1 = "uop%d" % ub1

            def f(sy, c=c, ub1=ub1):
                return sy.dma_start(
                    ufc_s[:, ub1 * KC * CH:(ub1 + 1) * KC * CH].rearrange("p (k t) -> p k t", k=KC),
                    kview(u_pre[:, c * CH:(c + 1) * CH]))
            P.op("sync", f, reads=[uf1], writes=[uf1], dma=True, stream=uf1)
            def f(sc, ub1=ub1):
                r = None
                for kc in range(KC):
                    r = sc.activation(uop_s[:, ub1 * KC * CH + kc * CH: ub1 * KC * CH + (kc + 1) * CH],
                                      ufc_s[:, ub1 * KC * CH + kc * CH: ub1 * KC * CH + (kc + 1) * CH],
                                      AF.Copy)
                return r
            P.op("scalar", f, reads=[uf1, up1], writes=[up1])
            for ncn in range(KC):
                bank = [pbu[0], pbu[1], py[0], py[1]][ncn % 4]
                pb = ["pbu0", "pbu1", "py0", "py1"][ncn % 4]

                def f(t, ncn=ncn, bank=bank, ub1=ub1):
                    r = None
                    for kc in range(KC):
                        r = t.matmul(bank[:], wtile(bmt_s, kc, ncn),
                                     uop_s[:, ub1 * KC * CH + kc * CH: ub1 * KC * CH + (kc + 1) * CH],
                                     start=(kc == 0), stop=(kc == KC - 1))
                    return r
                P.op("tensor", f, reads=["bmt", up1], writes=[pb])

                if debug and c == 2 and ncn == 0:
                    def fd(v):
                        return v.tensor_copy(zf[:, 0:CH], pbu[0][:])
                    P.op("vector", fd, reads=["pbu0"], writes=["zf0"])
                    P.op("sync", lambda sy: sy.dma_start(dbg_bu, zf[:, 0:CH]), reads=["zf0"], dma=True)
                def f(v, c=c, ncn=ncn, bank=bank):
                    initial = h0_fm[:, ncn:ncn + 1] if c == 0 else carry[:, ncn:ncn + 1]
                    return v.tensor_tensor_scan(scr1[:], a_fm[:, ncn:ncn + 1].to_broadcast([128, CH]),
                                                bank[:], initial, ALU.mult, ALU.add)
                P.op("vector", f, reads=[pb, "vec_a", "vec", "carry", "scr1"],
                     writes=[pb, "scr1"], pre_drain=True)
                def f(v, ncn=ncn):
                    return v.tensor_copy(carry[:, ncn:ncn + 1], scr1[:, CH - 1:CH])
                P.op("vector", f, reads=["scr1", "carry"], writes=["carry"])
                if debug and c == 2 and ncn == 0:
                    P.op("sync", lambda sy: sy.dma_start(dbg_sc1, scr1[:]), reads=["scr1"], dma=True)
                if debug and ncn == KC - 1:
                    def fd(sy, c=c):
                        return sy.dma_start(dbg_cc[:, c * KC:(c + 1) * KC], carry[:])
                    P.op("sync", fd, reads=["carry"], dma=True)
                if debug and c == 2:
                    def fd(sy, ncn=ncn):
                        return sy.dma_start(dbg_cc2[:, ncn * KC:(ncn + 1) * KC], carry[:])
                    P.op("sync", fd, reads=["carry"], dma=True)

        def f(v):
            r = None
            for ncn in range(KC):
                r = v.scalar_tensor_tensor(init2[:, ncn:ncn + 1], carry[:, ncn:ncn + 1],
                                           selv_sb[:, 0:1], h02_fm[:, ncn:ncn + 1], ALU.mult, ALU.add)
            return r
        if debug:
            P.op("sync", lambda sy: sy.dma_start(dbg_carry, carry[:]), reads=["carry"], dma=True)
        P.op("vector", f, reads=["carry", "selv", "vec"], writes=["init2"])

        def emit_load_cast_scan(c):
            ub = c % 2
            uo = "uop%d" % ub
            uf = "ufc%d" % ub

            def f(sy, c=c, ub=ub):
                return sy.dma_start(
                    ufc_s[:, ub * KC * CH:(ub + 1) * KC * CH].rearrange("p (k t) -> p k t", k=KC),
                    kview(u_own[:, c * CH:(c + 1) * CH]))
            P.op("sync", f, reads=[uf], writes=[uf], dma=True, stream=uf)

            def f(sc, ub=ub):
                r = None
                for kc in range(KC):
                    r = sc.activation(uop_s[:, ub * KC * CH + kc * CH: ub * KC * CH + (kc + 1) * CH],
                                      ufc_s[:, ub * KC * CH + kc * CH: ub * KC * CH + (kc + 1) * CH],
                                      AF.Copy)
                return r
            P.op("scalar", f, reads=[uf, uo], writes=[uo])

            for ncn in range(KC):
                bank = pbu[ncn % 2]
                pb = "pbu%d" % (ncn % 2)

                def f(t, ncn=ncn, ub=ub, bank=bank):
                    r = None
                    for kc in range(KC):
                        r = t.matmul(bank[:], wtile(bmt_s, kc, ncn),
                                     uop_s[:, ub * KC * CH + kc * CH: ub * KC * CH + (kc + 1) * CH],
                                     start=(kc == 0), stop=(kc == KC - 1))
                    return r
                P.op("tensor", f, reads=["bmt", uo, "init2"], writes=[pb])

                def f(v, c=c, ncn=ncn, ub=ub, bank=bank):
                    initial = init2[:, ncn:ncn + 1] if c == 0 else carry[:, ncn:ncn + 1]
                    dst = xs_s[:, ub * KC * CH + ncn * CH: ub * KC * CH + (ncn + 1) * CH]
                    return v.tensor_tensor_scan(dst, a_fm[:, ncn:ncn + 1].to_broadcast([128, CH]),
                                                bank[:], initial, ALU.mult, ALU.add)
                P.op("vector", f, reads=[pb, "vec_a", "init2", "carry", "xs%d_%d" % (ub, ncn)],
                     writes=[pb, "xs%d_%d" % (ub, ncn)], pre_drain=True)
                def f(v, ncn=ncn, ub=ub):
                    dst = xs_s[:, ub * KC * CH + ncn * CH: ub * KC * CH + (ncn + 1) * CH]
                    return v.tensor_copy(carry[:, ncn:ncn + 1], dst[:, CH - 1:CH])
                P.op("vector", f, reads=["xs%d_%d" % (ub, ncn), "carry"], writes=["carry"])

        def emit_y(c):
            ub = c % 2
            uo = "uop%d" % ub
            for oc in range(KC):
                pyb = "py%d" % (oc % 2)

                def f(t, oc=oc, ub=ub):
                    r = None
                    for ncn in range(KC):
                        r = t.matmul(py[oc % 2][:], wtile(cmt_s, ncn, oc),
                                     xs_s[:, ub * KC * CH + ncn * CH: ub * KC * CH + (ncn + 1) * CH],
                                     start=(ncn == 0), stop=False)
                    for kc in range(KC):
                        r = t.matmul(py[oc % 2][:], wtile(dmt_s, kc, oc),
                                     uop_s[:, ub * KC * CH + kc * CH: ub * KC * CH + (kc + 1) * CH],
                                     start=False, stop=(kc == KC - 1))
                    return r
                P.op("tensor", f,
                     reads=["cmt", "dmt", uo] + ["xs%d_%d" % (ub, n) for n in range(KC)],
                     writes=[pyb])

                def f(sc, oc=oc):
                    return sc.activation(zf[:, oc * CH:(oc + 1) * CH], py[oc % 2][:], AF.Gelu)
                P.op("scalar", f, reads=[pyb, "zf%d" % oc], writes=[pyb, "zf%d" % oc])
                def f(sc, oc=oc):
                    return sc.activation(asq8[:, oc * CH:(oc + 1) * CH],
                                         zf[:, oc * CH:(oc + 1) * CH], AF.Square)
                P.op("scalar", f, reads=["zf%d" % oc, "asq"], writes=["asq"])

        def emit_epilogue(c):
            ub = c % 2
            st = c % 2
            b0, b1 = pb4[2 * st], pb4[2 * st + 1]
            r0, r1 = "pbc%d0" % st, "pbc%d1" % st
            def f(t):
                for oc in range(KC):
                    t.matmul(b0[0:1, :], ones_p[:], zf[:, oc * CH:(oc + 1) * CH],
                             start=(oc == 0), stop=(oc == KC - 1))
                r = None
                for oc in range(KC):
                    r = t.matmul(b1[0:1, :], ones_b[:], asq8[:, oc * CH:(oc + 1) * CH],
                                 start=(oc == 0), stop=(oc == KC - 1))
                return r
            P.op("tensor", f, reads=["ones"] + ["zf%d" % o for o in range(KC)] + ["asq", r0, r1],
                 writes=[r0, r1])
            ln_smalls(st, [])

            zf_all = ["zf%d" % o for o in range(KC)]
            def f(v):
                r = None
                for oc in range(KC):
                    r = v.tensor_tensor(tm8[:, oc * CH:(oc + 1) * CH],
                                        zf[:, oc * CH:(oc + 1) * CH], b0[:], ALU.mult)
                return r
            P.op("vector", f, reads=zf_all + [r0, r1, "tm8"], writes=["tm8"])
            def f(v):
                r = None
                for oc in range(KC):
                    dst = tm8[:, oc * CH:(oc + 1) * CH]
                    r = v.tensor_tensor(dst, dst, b1[:], ALU.subtract)
                return r
            P.op("vector", f, reads=[r1, "tm8"], writes=["tm8"])
            def f(sc):
                r = None
                for oc in range(KC):
                    r = sc.activation(zf[:, oc * CH:(oc + 1) * CH],
                                      tm8[:, oc * CH:(oc + 1) * CH], AF.Identity,
                                      bias=b_fm[:, oc:oc + 1], scale=g_fm[:, oc:oc + 1])
                return r
            P.op("scalar", f, reads=["tm8", "vec"] + zf_all, writes=zf_all)
            def f(v):
                r = None
                for oc in range(KC):
                    blk = zf[:, oc * CH:(oc + 1) * CH]
                    r = v.tensor_scalar(blk, blk, -CLAMP, CLAMP, ALU.max, ALU.min)
                return r
            P.op("vector", f, reads=zf_all, writes=zf_all)
            def f(v, ub=ub):
                r = None
                for oc in range(KC):
                    blk = zf[:, oc * CH:(oc + 1) * CH]
                    r = v.tensor_tensor(blk, blk,
                                        ufc_s[:, ub * KC * CH + oc * CH: ub * KC * CH + (oc + 1) * CH],
                                        ALU.add)
                return r
            P.op("vector", f, reads=zf_all + ["ufc%d" % ub], writes=zf_all + ["ufc%d" % ub])
            def f(sc):
                r = None
                for oc in range(KC):
                    r = sc.activation(zq8[:, oc * CH:(oc + 1) * CH],
                                      zf[:, oc * CH:(oc + 1) * CH], AF.Square)
                return r
            P.op("scalar", f, reads=zf_all + ["zq"], writes=["zq"])

            def f(t):
                for oc in range(KC):
                    t.matmul(b0[0:1, :], ones_p[:], zf[:, oc * CH:(oc + 1) * CH],
                             start=(oc == 0), stop=(oc == KC - 1))
                r = None
                for oc in range(KC):
                    r = t.matmul(b1[0:1, :], ones_b[:], zq8[:, oc * CH:(oc + 1) * CH],
                                 start=(oc == 0), stop=(oc == KC - 1))
                return r
            P.op("tensor", f, reads=["ones"] + ["zf%d" % o for o in range(KC)] + ["zq", r0, r1],
                 writes=[r0, r1])
            ln_smalls(st, [])

            def f(v):
                r = None
                for oc in range(KC):
                    r = v.tensor_tensor(tm8[:, oc * CH:(oc + 1) * CH],
                                        zf[:, oc * CH:(oc + 1) * CH], b0[:], ALU.mult)
                return r
            P.op("vector", f, reads=[r0, "tm8"] + ["zf%d" % o for o in range(KC)],
                 writes=["tm8"])
            def f(v):
                r = None
                for oc in range(KC):
                    r = v.tensor_tensor(x1c[:, oc * CH:(oc + 1) * CH],
                                        tm8[:, oc * CH:(oc + 1) * CH], b1[:], ALU.subtract)
                return r
            P.op("vector", f, reads=[r1, "tm8", "x1c"], writes=["x1c"])
            def f(sy, c=c):
                return sy.dma_start(kview(x1_h[:, c * CH:(c + 1) * CH]),
                                    x1c[:].rearrange("p (k t) -> p k t", k=KC))
            P.op("sync", f, reads=["x1c"], writes=["x1h%d" % c], dma=True, stream="x1sp")

        emit_load_cast_scan(0)
        for c in range(NCH):
            emit_y(c)
            if c + 1 < NCH:
                emit_load_cast_scan(c + 1)
            emit_epilogue(c)
        P.flush()

    P.barrier()

    with ExitStack() as fctx:
        fsb = lambda n, s, d_: fctx.enter_context(nc.sbuf_tensor(n, s, d_))
        wq_s = fsb("wq_s", [128, KC * 512], BF16)
        x1b = fsb("x1b", [128, 2 * KC * CH], BF16)
        q_s = fsb("q_s", [128, 4 * CH], BF16)
        sims = fsb("sims", [128, 512], F32)
        emx = fsb("emx", [128, 64], BF16)
        mx8 = fsb("mx8", [128, 64], F32)
        r88 = fsb("r88", [128, 4 * 64], F32)
        r8 = fsb("r8", [128, 32], F32)
        den4 = fsb("den4", [128, 4], F32)
        rdt = fsb("rdt", [128, 16], F32)
        thr4 = fsb("thr4", [128, 16], F32)
        expuv = fsb("expuv", [128, 4 * 512], BF16)
        gall = fsb("gall", [128, 4 * E], BF16)
        p1 = [fsb("p1%d" % i, [128, 1024], F32) for i in range(2)]
        q2 = fsb("q2", [128, 1024], BF16)
        argc = [fsb("argc%d" % i, [128, 1024], BF16) for i in range(2)]
        wsil = [fsb("wsil%d" % i, [128, 1024], BF16) for i in range(2)]
        ssum = fsb("ssum", [128, 4 * E], BF16)
        big32 = fsb("big32", [128, 32 * CH], BF16)
        zf2 = fsb("zf2", [128, KC * CH], BF16)
        zsq = fsb("zsq", [128, KC * CH], BF16)
        otmp = fsb("otmp", [128, 2 * CH], F32)
        stat2 = fsb("stat2", [1, 2 * CH], F32)
        wstr = fsb("wstr", [128, 2 * KC * 512], BF16)
        ustr = fsb("ustr", [128, 2 * 4 * 512], BF16)
        uestr = fsb("uestr", [128, 4 * 512], BF16)
        dstr = wstr

        def f(g):
            return g.dma_start(wq_s[:].rearrange("p (k c) -> p k c", k=KC), kview(wqt))
        P.op("gpsimd", f, writes=["wq"], dma=True, stream="wq")
        P.op("gpsimd", lambda g: g.dma_start(det_b, det), writes=["det_b"], dma=True, stream="pc_det")
        P.op("gpsimd", lambda g: g.dma_start(wupt_b, wupt), writes=["wupt_b"], dma=True, stream="pc_wup")
        P.op("gpsimd", lambda g: g.dma_start(wdnt_b, wdnt), writes=["wdnt_b"], dma=True, stream="pc_wdn")
        P.op("gpsimd", lambda g: g.dma_start(uem_b, uem), writes=["uem_b"], dma=True, stream="pc_ue")

        fps = lambda n, sh, d_: fctx.enter_context(nc.psum_tensor(n, sh, d_))
        pqs = fps("pqs", [128, 512], F32)
        pg = [fps("pg%d" % i, [128, 512], F32) for i in range(2)]
        ptr = fps("ptr", [128, 1024], BF16)
        pbig = fps("pbig", [128, 2048], F32)

        def emit_tail(s):
            def f(t):
                for oc in range(KC):
                    t.matmul(pg[0][0:1, :], ones_b[:], zf2[:, oc * CH:(oc + 1) * CH],
                             start=(oc == 0), stop=(oc == KC - 1))
                r = None
                for oc in range(KC):
                    r = t.matmul(pg[1][0:1, :], ones_b[:], zsq[:, oc * CH:(oc + 1) * CH],
                                 start=(oc == 0), stop=(oc == KC - 1))
                return r
            P.op("tensor", f, reads=["ones", "zf2", "zsq", "pg0", "pg1"], writes=["pg0", "pg1"])

            aux2 = otmp[0:1, 0:CH]
            def f(v):
                v.tensor_scalar(stat2[:, 0:CH], pg[0][0:1, :], 1.0 / D, None, ALU.mult)
                return v.tensor_scalar(stat2[:, CH:2 * CH], pg[1][0:1, :], 1.0 / D, None, ALU.mult)
            P.op("vector", f, reads=["pg0", "pg1", "stat2"], writes=["pg0", "pg1", "stat2"])
            P.op("vector", lambda v: v.tensor_tensor(aux2, stat2[:, 0:CH], stat2[:, 0:CH], ALU.mult),
                 reads=["stat2", "ot0"], writes=["ot0"])
            P.op("vector", lambda v: v.scalar_tensor_tensor(stat2[:, CH:2 * CH],
                                                            stat2[:, CH:2 * CH], EPS, aux2,
                                                            ALU.add, ALU.subtract),
                 reads=["stat2", "ot0"], writes=["stat2"])
            P.op("vector", lambda v: v.reciprocal(aux2, stat2[:, CH:2 * CH]),
                 reads=["stat2", "ot0"], writes=["ot0"])
            P.op("scalar", lambda sc: sc.activation(stat2[:, CH:2 * CH], otmp[0:1, 0:CH], AF.Sqrt),
                 reads=["stat2", "ot0"], writes=["stat2"])
            P.op("vector", lambda v: v.tensor_tensor(stat2[:, 0:CH], stat2[:, 0:CH],
                                                     stat2[:, CH:2 * CH], ALU.mult),
                 reads=["stat2"], writes=["stat2"])

            def f(t):
                t.matmul(pg[0][:], ones_r[:], stat2[:, CH:2 * CH], start=True, stop=True)
                return t.matmul(pg[1][:], ones_r[:], stat2[:, 0:CH], start=True, stop=True)
            P.op("tensor", f, reads=["stat2", "ones", "pg0", "pg1"], writes=["pg0", "pg1"])

            for oc in range(KC):
                ob = "ot%d" % (oc % 2)

                def f(v, oc=oc):
                    dst = otmp[:, (oc % 2) * CH:((oc % 2) + 1) * CH]
                    return v.tensor_tensor(dst, zf2[:, oc * CH:(oc + 1) * CH], pg[0][:], ALU.mult)
                P.op("vector", f, reads=["zf2", "pg0", ob], writes=[ob])
                def f(v, oc=oc):
                    dst = otmp[:, (oc % 2) * CH:((oc % 2) + 1) * CH]
                    return v.tensor_tensor(dst, dst, pg[1][:], ALU.subtract)
                P.op("vector", f, reads=["pg1", ob], writes=[ob])

                def f(sy, s=s, oc=oc):
                    return sy.dma_start(out_h[oc * 128:(oc + 1) * 128, s * CH:(s + 1) * CH],
                                        otmp[:, (oc % 2) * CH:((oc % 2) + 1) * CH])
                P.op("sync", f, reads=[ob], writes=["outw", "outw_" + ob], dma=True, stream="out_" + ob)

        for s in range(NCH):
            if True:

                xo = (s % 2) * KC * CH
                xr = "x1b%d" % (s % 2)

                def f(sy, s=s, xo=xo):
                    return sy.dma_start(
                        x1b[:, xo:xo + KC * CH].rearrange("p (k t) -> p k t", k=KC),
                        kview(x1_h[:, s * CH:(s + 1) * CH]))
                P.op("sync", f, reads=[xr], writes=[xr], dma=True, stream="x1b")

                for ec in range(4):
                    def f(t, ec=ec):
                        r = None
                        for kc in range(KC):
                            r = t.matmul(pqs[:], wq_s[:, kc * 512 + ec * 128: kc * 512 + (ec + 1) * 128],
                                         x1b[:, xo + kc * CH: xo + (kc + 1) * CH],
                                         start=(kc == 0), stop=(kc == KC - 1))
                        return r
                    P.op("tensor", f, reads=["wq", xr], writes=["pqs"])
                    def f(sc, ec=ec):
                        return sc.activation(q_s[:, ec * CH:(ec + 1) * CH], pqs[:], AF.Copy)
                    P.op("scalar", f, reads=["pqs", "q"], writes=["pqs", "q"])

                for t4 in range(4):
                    def f(t, t4=t4):
                        r = None
                        for p in range(2):
                            for h in range(HEADS):
                                jb = p * 4 + h
                                ecq = 2 * p + h // 2
                                r = t.matmul(pqs[:, jb * 64:(jb + 1) * 64],
                                             q_s[:, ecq * CH + t4 * 128: ecq * CH + (t4 + 1) * 128],
                                             keys_sb[:, jb * 64:(jb + 1) * 64],
                                             start=True, stop=True)
                        return r
                    P.op("tensor", f, reads=["keys", "q", "pqs"], writes=["pqs"])
                    def f(sc, t4=t4):
                        return sc.activation(sims[:], pqs[:], AF.Copy)
                    P.op("scalar", f, reads=["pqs", "sims"], writes=["pqs", "sims"])
                    def f(sc, t4=t4):
                        return sc.activation(expuv[:, t4 * 512:(t4 + 1) * 512], sims[:], AF.Exp)
                    P.op("scalar", f, reads=["sims", "expuv%d" % t4], writes=["expuv%d" % t4])

                    def f(v):
                        r = None
                        for jb in range(8):
                            r = v.max(out=mx8[:, jb * 8:(jb + 1) * 8], in_=sims[:, jb * 64:(jb + 1) * 64])
                        return r
                    P.op("vector", f, reads=["sims", "mx8"], writes=["mx8"])
                    P.op("scalar", lambda sc: sc.activation(emx[:], mx8[:], AF.Exp),
                         reads=["mx8", "emx"], writes=["emx"])

                    def f(v):
                        r = None
                        for h in range(HEADS):
                            xb_ = emx[:, h * 8:(h + 1) * 8]
                            yb_ = emx[:, (4 + h) * 8:(4 + h) * 8 + 8]
                            r = v.tensor_tensor(r88[:, h * 64:(h + 1) * 64]
                                                .rearrange("p (a b) -> p a b", a=8),
                                                xb_[:, :, None].to_broadcast([128, 8, 8]),
                                                yb_[:, None, :].to_broadcast([128, 8, 8]), ALU.mult)
                        return r
                    P.op("vector", f, reads=["emx", "r88"], writes=["r88"])
                    def f(v):
                        r = None
                        for h in range(HEADS):
                            r = v.max(out=r8[:, h * 8:(h + 1) * 8], in_=r88[:, h * 64:(h + 1) * 64])
                        return r
                    P.op("vector", f, reads=["r88", "r8"], writes=["r8"])
                    def f(v):
                        r = None
                        for h in range(HEADS):
                            r = v.reduce_sum(den4[:, h:h + 1], r8[:, h * 8:(h + 1) * 8], axis=AX.X)
                        return r
                    P.op("vector", f, reads=["r8", "den4"], writes=["den4"])

                    def f(v, t4=t4):
                        v.reciprocal(rdt[:, t4 * 4:(t4 + 1) * 4], den4[:])
                        r = None
                        for h in range(HEADS):
                            r = v.tensor_scalar(thr4[:, t4 * 4 + h: t4 * 4 + h + 1],
                                                r8[:, h * 8 + 7: h * 8 + 8], MARGIN, None, ALU.mult)
                        return r
                    P.op("vector", f, reads=["den4", "r8", "rdt%d" % t4, "thr%d" % t4],
                         writes=["rdt%d" % t4, "thr%d" % t4])

                    def f(v, t4=t4):
                        r = None
                        for jb in range(8):
                            blk = expuv[:, t4 * 512 + jb * 64: t4 * 512 + (jb + 1) * 64]
                            r = v.scalar_tensor_tensor(
                                blk, sims[:, jb * 64:(jb + 1) * 64],
                                mx8[:, jb * 8 + 7: jb * 8 + 8],
                                blk, ALU.is_ge, ALU.mult)
                        return r
                    P.op("vector", f, reads=["sims", "mx8", "expuv%d" % t4],
                         writes=["expuv%d" % t4])

                if s > 0:
                    emit_tail(s - 1)

                gbanks = [(pg[0][:], "pg0"), (pg[1][:], "pg1"),
                          (pbig[:, 0:512], "pbig0"), (pbig[:, 512:1024], "pbig1")]
                for ec in range(8):
                    wb = "wstr%d" % (ec % 2)
                    off = (ec % 2) * KC * 512

                    def f(sy, ec=ec, off=off):
                        dst = wstr[:, off:off + KC * 512]
                        return sy.dma_start(dst.rearrange("p (k c) -> p k c", k=KC),
                                            kview(det_b[:, ec * 512:(ec + 1) * 512]))
                    P.op("sync", f, reads=["det_b", wb], writes=[wb], dma=True, stream=wb)
                    for t4 in range(4):
                        bank, pb = gbanks[(ec * 4 + t4) % 4]

                        def f(t, ec=ec, t4=t4, off=off, bank=bank):
                            r = None
                            for kc in range(KC):
                                r = t.matmul(bank,
                                             x1b[:, xo + kc * CH + t4 * 128: xo + kc * CH + (t4 + 1) * 128],
                                             wstr[:, off + kc * 512: off + (kc + 1) * 512],
                                             start=(kc == 0), stop=(kc == KC - 1))
                            return r
                        P.op("tensor", f, reads=[wb, xr], writes=[pb])
                        def f(sc, ec=ec, t4=t4, bank=bank):
                            return sc.activation(gall[:, t4 * E + ec * 512: t4 * E + (ec + 1) * 512],
                                                 bank, AF.Copy)
                        P.op("scalar", f, reads=[pb, "gall%d_%d" % (t4, ec)],
                             writes=[pb, "gall%d_%d" % (t4, ec)])

                for icg in range(8):
                    db = "wstr%d" % (icg % 2)
                    doff = (icg % 2) * KC * 512

                    def f(sy, icg=icg, doff=doff):
                        dst = dstr[:, doff:doff + KC * 512]
                        return sy.dma_start(
                            dst.rearrange("p (k i c) -> p k i c", k=KC, i=4),
                            wupt_b[:, icg * 512:(icg + 1) * 512]
                            .rearrange("(k p) (i c) -> p k i c", p=128, i=4))
                    P.op("sync", f, reads=["wupt_b", db], writes=[db], dma=True, stream=db)

                    for i4 in range(4):
                        ic = icg * 4 + i4
                        bank, pb = gbanks[ic % 4]

                        def f(t, i4=i4, ic=ic, doff=doff, bank=bank):
                            r = None
                            for kc in range(KC):
                                r = t.matmul(bank,
                                             dstr[:, doff + kc * 512 + i4 * 128: doff + kc * 512 + (i4 + 1) * 128],
                                             x1b[:, xo + kc * CH: xo + (kc + 1) * CH],
                                             start=(kc == 0), stop=(kc == KC - 1))
                            return r
                        P.op("tensor", f, reads=[db, xr], writes=[pb])

                        def f(sc, ic=ic, bank=bank):
                            return sc.activation(big32[:, ic * CH:(ic + 1) * CH], bank, AF.Silu)
                        P.op("scalar", f, reads=[pb, "big32"], writes=[pb, "big32"])

                dn1 = [(pqs, "pqs"), (pg[0], "pg0"), (pg[1], "pg1")]
                for dcg in range(2):
                    for icg in range(8):
                        ub_ = "ustr%d" % (icg % 2)
                        uoff = (icg % 2) * 4 * 512

                        def f(sy, dcg=dcg, icg=icg, uoff=uoff):
                            return sy.dma_start(
                                ustr[:, uoff:uoff + 4 * 512].rearrange("p (i c) -> p i c", i=4),
                                wdnt_b[icg * 512:(icg + 1) * 512, dcg * 512:(dcg + 1) * 512]
                                .rearrange("(i p) c -> p i c", p=128))
                        P.op("sync", f, reads=["wdnt_b", ub_], writes=[ub_], dma=True, stream=ub_)

                        def f(t, dcg=dcg, icg=icg, uoff=uoff):
                            r = None
                            for i4 in range(4):
                                ic = icg * 4 + i4
                                for dc4 in range(4 if dcg == 0 else 3):
                                    dst = pbig[:, dc4 * 512:(dc4 + 1) * 512] if dcg == 0 \
                                        else dn1[dc4][0][:]
                                    r = t.matmul(dst,
                                                 ustr[:, uoff + i4 * 512 + dc4 * 128: uoff + i4 * 512 + (dc4 + 1) * 128],
                                                 big32[:, ic * CH:(ic + 1) * CH],
                                                 start=(ic == 0), stop=(ic == 31))
                            return r
                        wl = ["pbig"] if dcg == 0 else [n for _, n in dn1]
                        P.op("tensor", f, reads=[ub_, "big32"] + wl, writes=wl)

                def f(sy):
                    return sy.dma_start(
                        ustr[:, 0:4096].rearrange("p (i c) -> p i c", i=32),
                        wdnt_b[:, 7 * 128:8 * 128].rearrange("(i p) c -> p i c", p=128))
                P.op("sync", f, reads=["ustr0", "ustr1"], writes=["ustr0", "ustr1"],
                     dma=True, stream="ustr0")

                it = 0
                pend_add = None
                for t4 in range(4):
                    for h in range(HEADS):
                        for ecc in range(4):
                            pp = it % 2
                            it += 1

                            def f(e, t4=t4, h=h, ecc=ecc, pp=pp):
                                xb_ = expuv[:, t4 * 512 + h * 64 + ecc * 16: t4 * 512 + h * 64 + (ecc + 1) * 16]
                                yb_ = expuv[:, t4 * 512 + (4 + h) * 64: t4 * 512 + (4 + h) * 64 + 64]
                                return e.tensor_tensor(p1[pp][:].rearrange("p (a b) -> p a b", a=16),
                                                       xb_[:, :, None].to_broadcast([128, 16, 64]),
                                                       yb_[:, None, :].to_broadcast([128, 16, 64]), ALU.mult)
                            P.op("gpsimd", f,
                                 reads=["expuv%d" % t4, "p1%d" % pp], writes=["p1%d" % pp])
                            if pend_add is not None:
                                pend_add()
                                pend_add = None

                            def f(v, t4=t4, h=h, pp=pp):
                                return v.scalar_tensor_tensor(q2[:], p1[pp][:],
                                                              thr4[:, t4 * 4 + h: t4 * 4 + h + 1],
                                                              p1[pp][:], ALU.is_ge, ALU.mult)
                            P.op("vector", f, reads=["p1%d" % pp, "thr%d" % t4, "q2"],
                                 writes=["q2"])
                            def f(v, t4=t4, ecc=ecc, pp=pp):
                                return v.tensor_tensor(argc[pp][:], q2[:],
                                                       gall[:, t4 * E + ecc * 1024: t4 * E + (ecc + 1) * 1024],
                                                       ALU.mult)
                            P.op("vector", f,
                                 reads=["q2", "gall%d_%d" % (t4, 2 * ecc), "gall%d_%d" % (t4, 2 * ecc + 1),
                                        "argc%d" % pp],
                                 writes=["argc%d" % pp])

                            sres = "ss%d_%d" % (t4, ecc)
                            if h == 0:
                                def f(sc, t4=t4, ecc=ecc, pp=pp):
                                    return sc.activation(ssum[:, t4 * E + ecc * 1024: t4 * E + (ecc + 1) * 1024],
                                                         argc[pp][:], AF.Silu, scale=rdt[:, t4 * 4: t4 * 4 + 1])
                                P.op("scalar", f, reads=["argc%d" % pp, "rdt%d" % t4, sres], writes=[sres])
                            else:
                                def f(sc, t4=t4, h=h, pp=pp):
                                    return sc.activation(wsil[pp][:], argc[pp][:], AF.Silu,
                                                         scale=rdt[:, t4 * 4 + h: t4 * 4 + h + 1])
                                P.op("scalar", f, reads=["argc%d" % pp, "rdt%d" % t4, "wsil%d" % pp],
                                     writes=["wsil%d" % pp])
                                def mk_add(t4=t4, ecc=ecc, pp=pp, sres=sres):
                                    def f(g):
                                        dst = ssum[:, t4 * E + ecc * 1024: t4 * E + (ecc + 1) * 1024]
                                        return g.tensor_tensor(dst, dst, wsil[pp][:], ALU.add)
                                    P.op("vector", f, reads=["wsil%d" % pp, sres], writes=[sres])
                                pend_add = mk_add
                if pend_add is not None:
                    pend_add()
                    pend_add = None

                for dc4 in range(4):
                    def f(v, dc4=dc4):
                        return v.tensor_tensor(zf2[:, dc4 * CH:(dc4 + 1) * CH],
                                               pbig[:, dc4 * 512:(dc4 + 1) * 512],
                                               x1b[:, xo + dc4 * CH: xo + (dc4 + 1) * CH], ALU.add)
                    P.op("vector", f, reads=["pbig", xr, "zf2"], writes=["pbig", "zf2"])
                for dc4 in range(3):
                    bank, bn = dn1[dc4]

                    def f(v, dc4=dc4, bank=bank):
                        oc = 4 + dc4
                        return v.tensor_tensor(zf2[:, oc * CH:(oc + 1) * CH],
                                               bank[:],
                                               x1b[:, xo + oc * CH: xo + (oc + 1) * CH], ALU.add)
                    P.op("vector", f, reads=[bn, xr, "zf2"], writes=[bn, "zf2"])

                def f(t):
                    r = None
                    for ic in range(32):
                        r = t.matmul(pbig[:, 0:512],
                                     ustr[:, ic * 128:(ic + 1) * 128],
                                     big32[:, ic * CH:(ic + 1) * CH],
                                     start=(ic == 0), stop=(ic == 31))
                    return r
                P.op("tensor", f, reads=["ustr0", "ustr1", "big32", "pbig"], writes=["pbig"])

                def f(v):
                    return v.tensor_tensor(zf2[:, 7 * CH:8 * CH], pbig[:, 0:512],
                                           x1b[:, xo + 7 * CH: xo + 8 * CH], ALU.add)
                P.op("vector", f, reads=["pbig", xr, "zf2"], writes=["pbig", "zf2"])

                def emit_eup(dcg, ec):
                    ub_ = "uestr%d" % (ec % 4)
                    uoff = (ec % 4) * 512

                    def f(sy, dcg=dcg, ec=ec, uoff=uoff):
                        return sy.dma_start(uestr[:, uoff:uoff + 512],
                                            uem_b[ec * 128:(ec + 1) * 128, dcg * 512:(dcg + 1) * 512])
                    P.op("sync", f, reads=["uem_b", ub_], writes=[ub_], dma=True, stream=ub_)

                    def f(t, ec=ec, uoff=uoff):
                        r = None
                        for dc4 in range(4):
                            r = t.matmul(pbig[:, dc4 * 512:(dc4 + 1) * 512],
                                         uestr[:, uoff + dc4 * 128: uoff + (dc4 + 1) * 128],
                                         big32[:, ec * CH:(ec + 1) * CH],
                                         start=(ec == 0), stop=(ec == 31))
                        return r
                    P.op("tensor", f, reads=[ub_, "big32"], writes=["pbig"])

                for grp in range(4):
                    for t4 in range(4):
                        def f(t, t4=t4, grp=grp):
                            r = None
                            for j in range(8):
                                eb = grp * 8 + j
                                r = t.transpose(ptr[:, j * 128:(j + 1) * 128],
                                                ssum[:, t4 * E + eb * 128: t4 * E + (eb + 1) * 128],
                                                id_b[:])
                            return r
                        P.op("tensor", f, reads=["idb", "ptr"] + ["ss%d_%d" % (t4, e) for e in range(4)],
                             writes=["ptr"])

                        def f(sc, t4=t4, grp=grp):
                            dst = big32[:].rearrange("p (e t) -> p e t", e=32)[:, grp * 8:(grp + 1) * 8,
                                                                              t4 * 128:(t4 + 1) * 128]
                            return sc.activation(dst, ptr[:].rearrange("p (j c) -> p j c", j=8), AF.Copy)
                        P.op("scalar", f, reads=["ptr", "big32"], writes=["ptr", "big32"])
                    for ec in range(8 * grp, 8 * grp + 8):
                        emit_eup(0, ec)
                for dc4 in range(4):
                    def f(v, dc4=dc4):
                        dst = zf2[:, dc4 * CH:(dc4 + 1) * CH]
                        return v.tensor_tensor(dst, dst, pbig[:, dc4 * 512:(dc4 + 1) * 512],
                                               ALU.add)
                    P.op("vector", f, reads=["pbig", "zf2"], writes=["pbig", "zf2"])
                for ec in range(32):
                    emit_eup(1, ec)
                for dc4 in range(4):
                    def f(v, dc4=dc4):
                        oc = 4 + dc4
                        dst = zf2[:, oc * CH:(oc + 1) * CH]
                        return v.tensor_tensor(dst, dst, pbig[:, dc4 * 512:(dc4 + 1) * 512],
                                               ALU.add)
                    P.op("vector", f, reads=["pbig", "zf2"], writes=["pbig", "zf2"])

                def f(sc):
                    r = None
                    for oc in range(KC):
                        r = sc.activation(zsq[:, oc * CH:(oc + 1) * CH],
                                          zf2[:, oc * CH:(oc + 1) * CH], AF.Square)
                    return r
                P.op("scalar", f, reads=["zf2", "zsq"], writes=["zsq"])
                P.flush()
        emit_tail(NCH - 1)
        P.flush()

    P.barrier()
    return nc, ctx


def shard_inputs(inputs):
    f32 = lambda x: np.ascontiguousarray(np.asarray(x, dtype=np.float32))
    u = f32(inputs["L0_attn_u"])
    Bm = f32(inputs["L0_attn_Bmat"])
    Cm = f32(inputs["L0_attn_Cmat"])
    Dm = f32(inputs["L0_attn_Dmat"])
    A = f32(inputs["L0_attn_A_diag"])
    h0 = f32(inputs["L0_attn_h0"])
    gam = f32(inputs["L0_attn_ln_gamma"])
    bet = f32(inputs["L0_attn_ln_beta"])
    Wup = f32(inputs["L0_ffn_W_up"])
    Wdn = f32(inputs["L0_ffn_W_down"])
    Wq = f32(inputs["L0_ffn_W_q"])
    keys = f32(inputs["L0_ffn_keys"])
    de = f32(inputs["L0_ffn_down_embed"])
    ue = f32(inputs["L0_ffn_up_embed"])

    fm = lambda v: np.ascontiguousarray(v.reshape(KC, 128).T)
    kp = np.zeros((128, 512), np.float32)
    for p in range(2):
        for h in range(HEADS):
            jb = p * 4 + h
            base = (h % 2) * 64
            kp[base:base + 64, jb * 64:(jb + 1) * 64] = keys[h, :, p, :].T

    shared = {
        "bmt": np.ascontiguousarray(Bm.T), "cmt": np.ascontiguousarray(Cm.T),
        "dmt": np.ascontiguousarray(Dm.T), "wqt": np.ascontiguousarray(Wq.T),
        "keyspad": kp, "det": np.ascontiguousarray(de.T), "uem": ue,
        "wupt": np.ascontiguousarray(Wup.T), "wdnt": np.ascontiguousarray(Wdn.T),
        "identm": np.eye(128, dtype=np.float32),
        "afmv": fm(A), "h0fm": fm(h0), "gfm": fm(gam), "bfm": fm(bet),
    }
    zeros_dt = np.zeros((D, T), np.float32)
    zkc = np.zeros((128, KC), np.float32)
    in_maps = []
    for c in range(8):
        b, half = c // 2, c % 2
        m = dict(shared)
        m["u_own"] = np.ascontiguousarray(u[b, half * T:(half + 1) * T].T)
        m["u_pre"] = np.ascontiguousarray(u[b, :T].T) if half else zeros_dt
        m["selv"] = np.full((128, 1), float(half), np.float32)
        m["h0v2"] = zkc if half else fm(h0)
        in_maps.append(m)
    return in_maps


def kernel(**inputs):
    in_maps = shard_inputs(inputs)
    nc, ctx = build_nc(debug=False)
    ctx.close()
    res = run_bass_kernel_spmd(nc, in_maps, core_ids=list(range(8)))
    out = np.zeros((4, 4096, D), np.float32)
    for c in range(8):
        b, half = c // 2, c % 2
        out[b, half * T:(half + 1) * T] = np.asarray(res.results[c]["out"]).T
    return out


# version 31 (final)
# speedup vs baseline: 1.0814x; 1.0105x over previous
import numpy as np
from contextlib import ExitStack

import concourse.bass as bass
import concourse.mybir as mybir
from concourse.bass_utils import run_bass_kernel_spmd

F32 = mybir.dt.float32
BF16 = mybir.dt.bfloat16
AF = mybir.ActivationFunctionType
ALU = mybir.AluOpType
AX = mybir.AxisListType

D = 1024
T = 2048
CH = 512
NCH = T // CH
KC = D // 128
E = 4096
HEADS = 4
EPS = 1e-5
CLAMP = 1e6
MARGIN = 1.0

ENGS = ["tensor", "vector", "scalar", "gpsimd", "sync"]
ESEM = {"tensor": "pe", "vector": "dve", "scalar": "act", "gpsimd": "pool", "sync": "sp"}


class Prog:

    def __init__(self, nc, ctx):
        self.nc = nc
        self.ctx = ctx
        self.block = ctx.enter_context(nc.Block())
        self.sems = {}
        for s in ["pe", "dve", "act", "pool", "sp"]:
            self.sems[s] = ctx.enter_context(nc.semaphore(s))
        self.cnt = {s: 0 for s in self.sems}
        self.last_w = {}
        self.readers = {}
        self.waited = {e: {s: 0 for s in self.sems} for e in ENGS}
        self.pending = {e: [] for e in ENGS}

    def dma_sem(self, stream):
        name = "dq_" + stream
        if name not in self.sems:
            self.sems[name] = self.ctx.enter_context(self.nc.semaphore(name))
            self.cnt[name] = 0
            for e in ENGS:
                self.waited[e][name] = 0
        return name

    def flush(self, engs=None):
        for e in (ENGS if engs is None else engs):
            items = self.pending[e]
            if not items:
                continue
            self.pending[e] = []

            def section(handle, items=items):
                for waits, build, sem, k, drain, new_cnt in items:
                    for s_, c_ in waits:
                        handle.wait_ge(self.sems[s_], c_)
                    ins = build(handle)
                    assert ins is not None
                    if isinstance(ins, list):
                        assert 16 * len(ins) == k
                        for i_ in ins:
                            i_.then_inc(self.sems[sem], 16)
                    else:
                        ins.then_inc(self.sems[sem], k)
                    if drain:
                        handle.wait_ge(self.sems[sem], new_cnt)
            getattr(self.block, e)(section)

    def _need(self, eng, reads, writes):
        need = {}

        def add(tok):
            if tok is None:
                return
            s, c, weng = tok
            if weng == eng and not s.startswith("dq_"):
                return
            need[s] = max(need.get(s, 0), c)

        for r in reads:
            add(self.last_w.get(r))
        for w in writes:
            add(self.last_w.get(w))
            for s, c in self.readers.get(w, {}).items():
                need[s] = max(need.get(s, 0), c)
        need.pop(ESEM[eng], None)
        out = []
        for s, c in sorted(need.items()):
            if c > self.waited[eng][s]:
                out.append((s, c))
                self.waited[eng][s] = c
        return out

    def op(self, eng, build, reads=(), writes=(), dma=False, drain=False, stream=None,
           pre_drain=False):
        if dma:
            sem = self.dma_sem(stream or "misc")
        else:
            sem = ESEM[eng]
        waits = self._need(eng, reads, writes)
        if (pre_drain or eng in ("vector", "scalar", "gpsimd")) and not dma \
                and self.cnt[ESEM[eng]] > self.waited[eng][ESEM[eng]]:
            waits.insert(0, (ESEM[eng], self.cnt[ESEM[eng]]))
            self.waited[eng][ESEM[eng]] = self.cnt[ESEM[eng]]
        if dma:
            k = 16 * (build.n_dma if hasattr(build, "n_dma") else 1)
        else:
            k = 1
        new_cnt = self.cnt[sem] + k
        self.pending[eng].append((waits, build, sem, k, drain, new_cnt))
        self.cnt[sem] = new_cnt
        if drain:
            self.waited[eng][sem] = max(self.waited[eng][sem], new_cnt)
        tok = (sem, new_cnt, eng)
        for w in writes:
            self.last_w[w] = tok
            self.readers[w] = {}
        for r in reads:
            d = self.readers.setdefault(r, {})
            d[sem] = max(d.get(sem, 0), new_cnt)
        return tok

    def barrier(self):
        self.flush()
        snap = dict(self.cnt)
        for e in ENGS:
            todo = [(s, c) for s, c in sorted(snap.items()) if c > self.waited[e][s]]

            def f(eng, todo=todo):
                for s, c in todo:
                    eng.wait_ge(self.sems[s], c)
            getattr(self.block, e)(f)
            for s, c in snap.items():
                self.waited[e][s] = max(self.waited[e][s], c)


def build_nc(debug=False):
    nc = bass.Bass("TRN2", target_bir_lowering=False, dynamic_dma_scratch_size=16384)
    dt = lambda n, s, d_, k: nc.dram_tensor(n, s, d_, kind=k).ap()

    u_own = dt("u_own", [D, T], F32, "ExternalInput")
    u_pre = dt("u_pre", [D, T], F32, "ExternalInput")
    selv = dt("selv", [128, 1], F32, "ExternalInput")
    h0v2 = dt("h0v2", [128, KC], F32, "ExternalInput")
    afmv = dt("afmv", [128, KC], F32, "ExternalInput")
    h0fm = dt("h0fm", [128, KC], F32, "ExternalInput")
    gfm = dt("gfm", [128, KC], F32, "ExternalInput")
    bfm = dt("bfm", [128, KC], F32, "ExternalInput")
    bmt = dt("bmt", [D, D], F32, "ExternalInput")
    cmt = dt("cmt", [D, D], F32, "ExternalInput")
    dmt = dt("dmt", [D, D], F32, "ExternalInput")
    wqt = dt("wqt", [D, 512], F32, "ExternalInput")
    keyspad = dt("keyspad", [128, 512], F32, "ExternalInput")
    det = dt("det", [D, E], F32, "ExternalInput")
    uem = dt("uem", [E, D], F32, "ExternalInput")
    wupt = dt("wupt", [D, E], F32, "ExternalInput")
    wdnt = dt("wdnt", [E, D], F32, "ExternalInput")
    identm = dt("identm", [128, 128], F32, "ExternalInput")
    out_h = dt("out", [D, T], F32, "ExternalOutput")

    det_b = dt("det_b", [D, E], BF16, "Internal")
    uem_b = dt("uem_b", [E, D], BF16, "Internal")
    wupt_b = dt("wupt_b", [D, E], BF16, "Internal")
    wdnt_b = dt("wdnt_b", [E, D], BF16, "Internal")
    x1_h = dt("x1_h", [D, T], BF16, "ExternalOutput" if debug else "Internal")
    if debug:
        dbg_vec = dt("dbg_vec", [128, 6 * KC], F32, "ExternalOutput")
        dbg_bmt = dt("dbg_bmt", [128, KC * D], BF16, "ExternalOutput")
        dbg_uop = dt("dbg_uop", [128, KC * CH], BF16, "ExternalOutput")
        dbg_xs = dt("dbg_xs", [128, KC * CH], BF16, "ExternalOutput")
        dbg_act = dt("dbg_act", [128, KC * CH], F32, "ExternalOutput")
        dbg_carry = dt("dbg_carry", [128, KC], F32, "ExternalOutput")
        dbg_bu = dt("dbg_bu", [128, CH], F32, "ExternalOutput")
        dbg_cc = dt("dbg_cc", [128, 4 * KC], F32, "ExternalOutput")
        dbg_cc2 = dt("dbg_cc2", [128, KC * KC], F32, "ExternalOutput")
        dbg_sc1 = dt("dbg_sc1", [128, CH], BF16, "ExternalOutput")

    kview = lambda ap: ap.rearrange("(k p) t -> p k t", p=128)

    ctx = ExitStack()
    sb = lambda n, s, d_: ctx.enter_context(nc.sbuf_tensor(n, s, d_))
    P = Prog(nc, ctx)

    vec = sb("vec", [128, 6 * KC], F32)
    a_fm = vec[:, 0:KC]
    h0_fm = vec[:, KC:2 * KC]
    g_fm = vec[:, 2 * KC:3 * KC]
    b_fm = vec[:, 3 * KC:4 * KC]
    h02_fm = vec[:, 4 * KC:5 * KC]
    spa = vec[:, 5 * KC:6 * KC]
    selv_sb = sb("selv_sb", [128, 1], F32)
    ones_b = sb("ones_b", [128, 1], BF16)
    ones_p = sb("ones_p", [128, 1], F32)
    ones_r = sb("ones_r", [1, 128], F32)
    id_b = sb("id_b", [128, 128], BF16)
    carry = sb("carry", [128, KC], F32)
    init2 = sb("init2", [128, KC], F32)
    keys_sb = sb("keys_sb", [128, 512], BF16)

    def f(sy):
        return [sy.dma_start(vec[:, 0:KC], afmv),
                sy.dma_start(vec[:, KC:2 * KC], h0fm),
                sy.dma_start(vec[:, 2 * KC:3 * KC], gfm),
                sy.dma_start(vec[:, 3 * KC:4 * KC], bfm),
                sy.dma_start(vec[:, 4 * KC:5 * KC], h0v2),
                sy.dma_start(selv_sb[:], selv)]
    f.n_dma = 6
    P.op("sync", f, writes=["vec", "selv"], dma=True, stream="init")

    def f(g):
        return [g.dma_start(id_b[:], identm),
                g.dma_start(keys_sb[:], keyspad)]
    f.n_dma = 2
    P.op("gpsimd", f, writes=["idb", "keys"], dma=True, stream="consts")

    def f(v):
        v.memset(ones_b[:], 1.0)
        v.memset(ones_p[:], 1.0)
        return v.memset(ones_r[:], 1.0)
    P.op("vector", f, writes=["ones"])

    P.op("scalar", lambda sc: sc.activation(spa, vec[:, 0:KC], AF.Exp),
         reads=["vec"], writes=["spa"])
    P.op("vector", lambda v: v.tensor_scalar(spa, spa, 1.0, None, ALU.add),
         reads=["spa"], writes=["spa"])
    P.op("scalar", lambda sc: sc.activation(spa, spa, AF.Ln),
         reads=["spa"], writes=["spa"])
    P.op("vector", lambda v: v.tensor_scalar(a_fm, spa, -1.0, None, ALU.mult),
         reads=["spa"], writes=["vec_a"])
    if debug:
        P.op("sync", lambda sy: sy.dma_start(dbg_vec, vec[:]), reads=["vec", "vec_a"], dma=True)

    with ExitStack() as actx:
        asb = lambda n, s, d_: actx.enter_context(nc.sbuf_tensor(n, s, d_))
        aps = lambda n, s, d_: actx.enter_context(nc.psum_tensor(n, s, d_))
        bmt_s = asb("bmt_s", [128, KC * D], BF16)
        cmt_s = asb("cmt_s", [128, KC * D], BF16)
        dmt_s = asb("dmt_s", [128, KC * D], BF16)
        uop_s = asb("uop_s", [128, 2 * KC * CH], BF16)
        ufc_s = asb("ufc_s", [128, 2 * KC * CH], F32)
        xs_s = asb("xs_s", [128, 2 * KC * CH], BF16)
        scr1 = asb("scr1", [128, CH], BF16)
        asq8 = asb("asq8", [128, KC * CH], BF16)
        zf = asb("zf", [128, KC * CH], F32)
        zq8 = asb("zq8", [128, KC * CH], BF16)
        x1c = asb("x1c", [128, KC * CH], BF16)
        tmp = asb("tmp", [128, 2 * CH], F32)
        tm8 = asb("tm8", [128, KC * CH], F32)
        stat = asb("stat", [1, 6 * CH], F32)
        pbu = [aps("pbu%d" % i, [128, CH], F32) for i in range(2)]
        py = [aps("py%d" % i, [128, CH], F32) for i in range(2)]
        pb4 = [aps("pb4_%d" % i, [128, CH], F32) for i in range(4)]

        wtile = lambda buf, kc, j: buf[:, kc * D + j * 128: kc * D + (j + 1) * 128]

        def f(g):
            return [g.dma_start(cmt_s[:].rearrange("p (k c) -> p k c", k=KC), kview(cmt)),
                    g.dma_start(dmt_s[:].rearrange("p (k c) -> p k c", k=KC), kview(dmt))]
        f.n_dma = 2
        P.op("gpsimd", f, writes=["cmt", "dmt"], dma=True, stream="bcd")

        def f(sy):
            return sy.dma_start(ufc_s[:].rearrange("p (k c) -> p k c", k=KC), kview(bmt))
        P.op("sync", f, reads=["ufc0", "ufc1"], writes=["ufc0", "ufc1"], dma=True, stream="ufc0")
        def f(sc):
            r = None
            for j in range(16):
                r = sc.activation(bmt_s[:, j * 512:(j + 1) * 512],
                                  ufc_s[:, j * 512:(j + 1) * 512], AF.Copy)
            return r
        P.op("scalar", f, reads=["ufc0", "ufc1", "bmt"], writes=["ufc0", "ufc1", "bmt"])
        if debug:
            P.op("sync", lambda sy: sy.dma_start(dbg_bmt, bmt_s[:]), reads=["bmt"], dma=True)

        def ln_smalls(st, extra_reads):
            o = st * 3 * CH
            b0, b1 = pb4[2 * st], pb4[2 * st + 1]
            sres = "stat%d" % st
            r0, r1 = "pbc%d0" % st, "pbc%d1" % st

            def f(v):
                v.tensor_scalar(stat[:, o:o + CH], b0[0:1, :], 1.0 / D, None, ALU.mult)
                return v.tensor_scalar(stat[:, o + CH:o + 2 * CH], b1[0:1, :], 1.0 / D, None, ALU.mult)
            P.op("vector", f, reads=list(extra_reads) + [sres, r0, r1], writes=[sres])
            P.op("vector", lambda v: v.tensor_tensor(stat[:, o + 2 * CH:o + 3 * CH],
                                                     stat[:, o:o + CH], stat[:, o:o + CH], ALU.mult),
                 reads=[sres], writes=[sres])
            P.op("vector", lambda v: v.scalar_tensor_tensor(stat[:, o + CH:o + 2 * CH],
                                                            stat[:, o + CH:o + 2 * CH], EPS,
                                                            stat[:, o + 2 * CH:o + 3 * CH],
                                                            ALU.add, ALU.subtract),
                 reads=[sres], writes=[sres])
            P.op("vector", lambda v: v.reciprocal(stat[:, o + 2 * CH:o + 3 * CH],
                                                  stat[:, o + CH:o + 2 * CH]),
                 reads=[sres], writes=[sres])
            P.op("scalar", lambda sc: sc.activation(stat[:, o + CH:o + 2 * CH],
                                                    stat[:, o + 2 * CH:o + 3 * CH], AF.Sqrt),
                 reads=[sres], writes=[sres])
            P.op("vector", lambda v: v.tensor_tensor(stat[:, o:o + CH], stat[:, o:o + CH],
                                                     stat[:, o + CH:o + 2 * CH], ALU.mult),
                 reads=[sres], writes=[sres])

            def f(t):
                t.matmul(b0[:], ones_r[:], stat[:, o + CH:o + 2 * CH], start=True, stop=True)
                return t.matmul(b1[:], ones_r[:], stat[:, o:o + CH], start=True, stop=True)
            P.op("tensor", f, reads=[sres, "ones", r0, r1], writes=[r0, r1])

        for c in range(NCH):
            ub1 = c % 2
            uf1 = "ufc%d" % ub1
            up1 = "uop%d" % ub1

            def f(sy, c=c, ub1=ub1):
                return sy.dma_start(
                    ufc_s[:, ub1 * KC * CH:(ub1 + 1) * KC * CH].rearrange("p (k t) -> p k t", k=KC),
                    kview(u_pre[:, c * CH:(c + 1) * CH]))
            P.op("sync", f, reads=[uf1], writes=[uf1], dma=True, stream=uf1)
            def f(sc, ub1=ub1):
                r = None
                for kc in range(KC):
                    r = sc.activation(uop_s[:, ub1 * KC * CH + kc * CH: ub1 * KC * CH + (kc + 1) * CH],
                                      ufc_s[:, ub1 * KC * CH + kc * CH: ub1 * KC * CH + (kc + 1) * CH],
                                      AF.Copy)
                return r
            P.op("scalar", f, reads=[uf1, up1], writes=[up1])
            for ncn in range(KC):
                bank = [pbu[0], pbu[1], py[0], py[1]][ncn % 4]
                pb = ["pbu0", "pbu1", "py0", "py1"][ncn % 4]

                def f(t, ncn=ncn, bank=bank, ub1=ub1):
                    r = None
                    for kc in range(KC):
                        r = t.matmul(bank[:], wtile(bmt_s, kc, ncn),
                                     uop_s[:, ub1 * KC * CH + kc * CH: ub1 * KC * CH + (kc + 1) * CH],
                                     start=(kc == 0), stop=(kc == KC - 1))
                    return r
                P.op("tensor", f, reads=["bmt", up1], writes=[pb])

                if debug and c == 2 and ncn == 0:
                    def fd(v):
                        return v.tensor_copy(zf[:, 0:CH], pbu[0][:])
                    P.op("vector", fd, reads=["pbu0"], writes=["zf0"])
                    P.op("sync", lambda sy: sy.dma_start(dbg_bu, zf[:, 0:CH]), reads=["zf0"], dma=True)
                def f(v, c=c, ncn=ncn, bank=bank):
                    initial = h0_fm[:, ncn:ncn + 1] if c == 0 else carry[:, ncn:ncn + 1]
                    return v.tensor_tensor_scan(scr1[:], a_fm[:, ncn:ncn + 1].to_broadcast([128, CH]),
                                                bank[:], initial, ALU.mult, ALU.add)
                P.op("vector", f, reads=[pb, "vec_a", "vec", "carry", "scr1"],
                     writes=[pb, "scr1"], pre_drain=True)
                def f(v, ncn=ncn):
                    return v.tensor_copy(carry[:, ncn:ncn + 1], scr1[:, CH - 1:CH])
                P.op("vector", f, reads=["scr1", "carry"], writes=["carry"])
                if debug and c == 2 and ncn == 0:
                    P.op("sync", lambda sy: sy.dma_start(dbg_sc1, scr1[:]), reads=["scr1"], dma=True)
                if debug and ncn == KC - 1:
                    def fd(sy, c=c):
                        return sy.dma_start(dbg_cc[:, c * KC:(c + 1) * KC], carry[:])
                    P.op("sync", fd, reads=["carry"], dma=True)
                if debug and c == 2:
                    def fd(sy, ncn=ncn):
                        return sy.dma_start(dbg_cc2[:, ncn * KC:(ncn + 1) * KC], carry[:])
                    P.op("sync", fd, reads=["carry"], dma=True)

        def f(v):
            r = None
            for ncn in range(KC):
                r = v.scalar_tensor_tensor(init2[:, ncn:ncn + 1], carry[:, ncn:ncn + 1],
                                           selv_sb[:, 0:1], h02_fm[:, ncn:ncn + 1], ALU.mult, ALU.add)
            return r
        if debug:
            P.op("sync", lambda sy: sy.dma_start(dbg_carry, carry[:]), reads=["carry"], dma=True)
        P.op("vector", f, reads=["carry", "selv", "vec"], writes=["init2"])

        def emit_load_cast_scan(c):
            ub = c % 2
            uo = "uop%d" % ub
            uf = "ufc%d" % ub

            def f(sy, c=c, ub=ub):
                return sy.dma_start(
                    ufc_s[:, ub * KC * CH:(ub + 1) * KC * CH].rearrange("p (k t) -> p k t", k=KC),
                    kview(u_own[:, c * CH:(c + 1) * CH]))
            P.op("sync", f, reads=[uf], writes=[uf], dma=True, stream=uf)

            def f(sc, ub=ub):
                r = None
                for kc in range(KC):
                    r = sc.activation(uop_s[:, ub * KC * CH + kc * CH: ub * KC * CH + (kc + 1) * CH],
                                      ufc_s[:, ub * KC * CH + kc * CH: ub * KC * CH + (kc + 1) * CH],
                                      AF.Copy)
                return r
            P.op("scalar", f, reads=[uf, uo], writes=[uo])

            for ncn in range(KC):
                bank = pbu[ncn % 2]
                pb = "pbu%d" % (ncn % 2)

                def f(t, ncn=ncn, ub=ub, bank=bank):
                    r = None
                    for kc in range(KC):
                        r = t.matmul(bank[:], wtile(bmt_s, kc, ncn),
                                     uop_s[:, ub * KC * CH + kc * CH: ub * KC * CH + (kc + 1) * CH],
                                     start=(kc == 0), stop=(kc == KC - 1))
                    return r
                P.op("tensor", f, reads=["bmt", uo, "init2"], writes=[pb])

                def f(v, c=c, ncn=ncn, ub=ub, bank=bank):
                    initial = init2[:, ncn:ncn + 1] if c == 0 else carry[:, ncn:ncn + 1]
                    dst = xs_s[:, ub * KC * CH + ncn * CH: ub * KC * CH + (ncn + 1) * CH]
                    return v.tensor_tensor_scan(dst, a_fm[:, ncn:ncn + 1].to_broadcast([128, CH]),
                                                bank[:], initial, ALU.mult, ALU.add)
                P.op("vector", f, reads=[pb, "vec_a", "init2", "carry", "xs%d_%d" % (ub, ncn)],
                     writes=[pb, "xs%d_%d" % (ub, ncn)], pre_drain=True)
                def f(v, ncn=ncn, ub=ub):
                    dst = xs_s[:, ub * KC * CH + ncn * CH: ub * KC * CH + (ncn + 1) * CH]
                    return v.tensor_copy(carry[:, ncn:ncn + 1], dst[:, CH - 1:CH])
                P.op("vector", f, reads=["xs%d_%d" % (ub, ncn), "carry"], writes=["carry"])

        def emit_y(c):
            ub = c % 2
            uo = "uop%d" % ub
            for oc in range(KC):
                pyb = "py%d" % (oc % 2)

                def f(t, oc=oc, ub=ub):
                    r = None
                    for ncn in range(KC):
                        r = t.matmul(py[oc % 2][:], wtile(cmt_s, ncn, oc),
                                     xs_s[:, ub * KC * CH + ncn * CH: ub * KC * CH + (ncn + 1) * CH],
                                     start=(ncn == 0), stop=False)
                    for kc in range(KC):
                        r = t.matmul(py[oc % 2][:], wtile(dmt_s, kc, oc),
                                     uop_s[:, ub * KC * CH + kc * CH: ub * KC * CH + (kc + 1) * CH],
                                     start=False, stop=(kc == KC - 1))
                    return r
                P.op("tensor", f,
                     reads=["cmt", "dmt", uo] + ["xs%d_%d" % (ub, n) for n in range(KC)],
                     writes=[pyb])

                def f(sc, oc=oc):
                    return sc.activation(zf[:, oc * CH:(oc + 1) * CH], py[oc % 2][:], AF.Gelu)
                P.op("scalar", f, reads=[pyb, "zf%d" % oc], writes=[pyb, "zf%d" % oc])
                def f(sc, oc=oc):
                    return sc.activation(asq8[:, oc * CH:(oc + 1) * CH],
                                         zf[:, oc * CH:(oc + 1) * CH], AF.Square)
                P.op("scalar", f, reads=["zf%d" % oc, "asq"], writes=["asq"])

        def emit_epilogue(c):
            ub = c % 2
            st = c % 2
            b0, b1 = pb4[2 * st], pb4[2 * st + 1]
            r0, r1 = "pbc%d0" % st, "pbc%d1" % st
            def f(t):
                for oc in range(KC):
                    t.matmul(b0[0:1, :], ones_p[:], zf[:, oc * CH:(oc + 1) * CH],
                             start=(oc == 0), stop=(oc == KC - 1))
                r = None
                for oc in range(KC):
                    r = t.matmul(b1[0:1, :], ones_b[:], asq8[:, oc * CH:(oc + 1) * CH],
                                 start=(oc == 0), stop=(oc == KC - 1))
                return r
            P.op("tensor", f, reads=["ones"] + ["zf%d" % o for o in range(KC)] + ["asq", r0, r1],
                 writes=[r0, r1])
            ln_smalls(st, [])

            zf_all = ["zf%d" % o for o in range(KC)]
            def f(v):
                r = None
                for oc in range(KC):
                    r = v.tensor_tensor(tm8[:, oc * CH:(oc + 1) * CH],
                                        zf[:, oc * CH:(oc + 1) * CH], b0[:], ALU.mult)
                return r
            P.op("vector", f, reads=zf_all + [r0, r1, "tm8"], writes=["tm8"])
            def f(v):
                r = None
                for oc in range(KC):
                    dst = tm8[:, oc * CH:(oc + 1) * CH]
                    r = v.tensor_tensor(dst, dst, b1[:], ALU.subtract)
                return r
            P.op("vector", f, reads=[r1, "tm8"], writes=["tm8"])
            def f(sc):
                r = None
                for oc in range(KC):
                    r = sc.activation(zf[:, oc * CH:(oc + 1) * CH],
                                      tm8[:, oc * CH:(oc + 1) * CH], AF.Identity,
                                      bias=b_fm[:, oc:oc + 1], scale=g_fm[:, oc:oc + 1])
                return r
            P.op("scalar", f, reads=["tm8", "vec"] + zf_all, writes=zf_all)
            def f(v):
                r = None
                for oc in range(KC):
                    blk = zf[:, oc * CH:(oc + 1) * CH]
                    r = v.tensor_scalar(blk, blk, -CLAMP, CLAMP, ALU.max, ALU.min)
                return r
            P.op("vector", f, reads=zf_all, writes=zf_all)
            def f(v, ub=ub):
                r = None
                for oc in range(KC):
                    blk = zf[:, oc * CH:(oc + 1) * CH]
                    r = v.tensor_tensor(blk, blk,
                                        ufc_s[:, ub * KC * CH + oc * CH: ub * KC * CH + (oc + 1) * CH],
                                        ALU.add)
                return r
            P.op("vector", f, reads=zf_all + ["ufc%d" % ub], writes=zf_all + ["ufc%d" % ub])
            def f(sc):
                r = None
                for oc in range(KC):
                    r = sc.activation(zq8[:, oc * CH:(oc + 1) * CH],
                                      zf[:, oc * CH:(oc + 1) * CH], AF.Square)
                return r
            P.op("scalar", f, reads=zf_all + ["zq"], writes=["zq"])

            def f(t):
                for oc in range(KC):
                    t.matmul(b0[0:1, :], ones_p[:], zf[:, oc * CH:(oc + 1) * CH],
                             start=(oc == 0), stop=(oc == KC - 1))
                r = None
                for oc in range(KC):
                    r = t.matmul(b1[0:1, :], ones_b[:], zq8[:, oc * CH:(oc + 1) * CH],
                                 start=(oc == 0), stop=(oc == KC - 1))
                return r
            P.op("tensor", f, reads=["ones"] + ["zf%d" % o for o in range(KC)] + ["zq", r0, r1],
                 writes=[r0, r1])
            ln_smalls(st, [])

            def f(v):
                r = None
                for oc in range(KC):
                    r = v.tensor_tensor(tm8[:, oc * CH:(oc + 1) * CH],
                                        zf[:, oc * CH:(oc + 1) * CH], b0[:], ALU.mult)
                return r
            P.op("vector", f, reads=[r0, "tm8"] + ["zf%d" % o for o in range(KC)],
                 writes=["tm8"])
            def f(v):
                r = None
                for oc in range(KC):
                    r = v.tensor_tensor(x1c[:, oc * CH:(oc + 1) * CH],
                                        tm8[:, oc * CH:(oc + 1) * CH], b1[:], ALU.subtract)
                return r
            P.op("vector", f, reads=[r1, "tm8", "x1c"], writes=["x1c"])
            def f(sy, c=c):
                return sy.dma_start(kview(x1_h[:, c * CH:(c + 1) * CH]),
                                    x1c[:].rearrange("p (k t) -> p k t", k=KC))
            P.op("sync", f, reads=["x1c"], writes=["x1h%d" % c], dma=True, stream="x1sp")

        emit_load_cast_scan(0)
        for c in range(NCH):
            emit_y(c)
            if c + 1 < NCH:
                emit_load_cast_scan(c + 1)
            emit_epilogue(c)
        P.flush()

    P.barrier()

    with ExitStack() as fctx:
        fsb = lambda n, s, d_: fctx.enter_context(nc.sbuf_tensor(n, s, d_))
        wq_s = fsb("wq_s", [128, KC * 512], BF16)
        x1b = fsb("x1b", [128, 2 * KC * CH], BF16)
        q_s = fsb("q_s", [128, 4 * CH], BF16)
        sims = fsb("sims", [128, 512], F32)
        emx = fsb("emx", [128, 64], BF16)
        mx8 = fsb("mx8", [128, 64], F32)
        r88 = fsb("r88", [128, 4 * 64], F32)
        r8 = fsb("r8", [128, 32], F32)
        den4 = fsb("den4", [128, 4], F32)
        rdt = fsb("rdt", [128, 16], F32)
        thr4 = fsb("thr4", [128, 16], F32)
        expuv = fsb("expuv", [128, 4 * 512], BF16)
        gall = fsb("gall", [128, 4 * E], BF16)
        p1 = [fsb("p1%d" % i, [128, 1024], F32) for i in range(2)]
        q2 = fsb("q2", [128, 1024], BF16)
        argc = [fsb("argc%d" % i, [128, 1024], BF16) for i in range(2)]
        wsil = [fsb("wsil%d" % i, [128, 1024], BF16) for i in range(2)]
        ssum = fsb("ssum", [128, 4 * E], BF16)
        big32 = fsb("big32", [128, 32 * CH], BF16)
        zf2 = fsb("zf2", [128, KC * CH], BF16)
        zsq = fsb("zsq", [128, KC * CH], BF16)
        otmp = fsb("otmp", [128, 2 * CH], F32)
        stat2 = fsb("stat2", [1, 2 * CH], F32)
        wstr = fsb("wstr", [128, 2 * KC * 512], BF16)
        ustr = fsb("ustr", [128, 2 * 4 * 512], BF16)
        uestr = fsb("uestr", [128, 4 * 512], BF16)
        dstr = wstr

        def f(g):
            return g.dma_start(wq_s[:].rearrange("p (k c) -> p k c", k=KC), kview(wqt))
        P.op("gpsimd", f, writes=["wq"], dma=True, stream="wq")
        P.op("gpsimd", lambda g: g.dma_start(det_b, det), writes=["det_b"], dma=True, stream="pc_det")
        P.op("gpsimd", lambda g: g.dma_start(wupt_b, wupt), writes=["wupt_b"], dma=True, stream="pc_wup")
        P.op("gpsimd", lambda g: g.dma_start(wdnt_b, wdnt), writes=["wdnt_b"], dma=True, stream="pc_wdn")
        P.op("gpsimd", lambda g: g.dma_start(uem_b, uem), writes=["uem_b"], dma=True, stream="pc_ue")

        fps = lambda n, sh, d_: fctx.enter_context(nc.psum_tensor(n, sh, d_))
        pqs = fps("pqs", [128, 512], F32)
        pg = [fps("pg%d" % i, [128, 512], F32) for i in range(2)]
        ptr = fps("ptr", [128, 1024], BF16)
        pbig = fps("pbig", [128, 2048], F32)

        def emit_tail(s):
            def f(t):
                for oc in range(KC):
                    t.matmul(pg[0][0:1, :], ones_b[:], zf2[:, oc * CH:(oc + 1) * CH],
                             start=(oc == 0), stop=(oc == KC - 1))
                r = None
                for oc in range(KC):
                    r = t.matmul(pg[1][0:1, :], ones_b[:], zsq[:, oc * CH:(oc + 1) * CH],
                                 start=(oc == 0), stop=(oc == KC - 1))
                return r
            P.op("tensor", f, reads=["ones", "zf2", "zsq", "pg0", "pg1"], writes=["pg0", "pg1"])

            aux2 = otmp[0:1, 0:CH]
            def f(v):
                v.tensor_scalar(stat2[:, 0:CH], pg[0][0:1, :], 1.0 / D, None, ALU.mult)
                return v.tensor_scalar(stat2[:, CH:2 * CH], pg[1][0:1, :], 1.0 / D, None, ALU.mult)
            P.op("vector", f, reads=["pg0", "pg1", "stat2"], writes=["pg0", "pg1", "stat2"])
            P.op("vector", lambda v: v.tensor_tensor(aux2, stat2[:, 0:CH], stat2[:, 0:CH], ALU.mult),
                 reads=["stat2", "ot0"], writes=["ot0"])
            P.op("vector", lambda v: v.scalar_tensor_tensor(stat2[:, CH:2 * CH],
                                                            stat2[:, CH:2 * CH], EPS, aux2,
                                                            ALU.add, ALU.subtract),
                 reads=["stat2", "ot0"], writes=["stat2"])
            P.op("vector", lambda v: v.reciprocal(aux2, stat2[:, CH:2 * CH]),
                 reads=["stat2", "ot0"], writes=["ot0"])
            P.op("scalar", lambda sc: sc.activation(stat2[:, CH:2 * CH], otmp[0:1, 0:CH], AF.Sqrt),
                 reads=["stat2", "ot0"], writes=["stat2"])
            P.op("vector", lambda v: v.tensor_tensor(stat2[:, 0:CH], stat2[:, 0:CH],
                                                     stat2[:, CH:2 * CH], ALU.mult),
                 reads=["stat2"], writes=["stat2"])

            def f(t):
                t.matmul(pg[0][:], ones_r[:], stat2[:, CH:2 * CH], start=True, stop=True)
                return t.matmul(pg[1][:], ones_r[:], stat2[:, 0:CH], start=True, stop=True)
            P.op("tensor", f, reads=["stat2", "ones", "pg0", "pg1"], writes=["pg0", "pg1"])

            for oc in range(KC):
                ob = "ot%d" % (oc % 2)

                def f(v, oc=oc):
                    dst = otmp[:, (oc % 2) * CH:((oc % 2) + 1) * CH]
                    return v.tensor_tensor(dst, zf2[:, oc * CH:(oc + 1) * CH], pg[0][:], ALU.mult)
                P.op("vector", f, reads=["zf2", "pg0", ob], writes=[ob])
                def f(v, oc=oc):
                    dst = otmp[:, (oc % 2) * CH:((oc % 2) + 1) * CH]
                    return v.tensor_tensor(dst, dst, pg[1][:], ALU.subtract)
                P.op("vector", f, reads=["pg1", ob], writes=[ob])

                def f(sy, s=s, oc=oc):
                    return sy.dma_start(out_h[oc * 128:(oc + 1) * 128, s * CH:(s + 1) * CH],
                                        otmp[:, (oc % 2) * CH:((oc % 2) + 1) * CH])
                P.op("sync", f, reads=[ob], writes=["outw", "outw_" + ob], dma=True, stream="out_" + ob)

        for s in range(NCH):
            if True:

                xo = (s % 2) * KC * CH
                xr = "x1b%d" % (s % 2)

                def f(sy, s=s, xo=xo):
                    return sy.dma_start(
                        x1b[:, xo:xo + KC * CH].rearrange("p (k t) -> p k t", k=KC),
                        kview(x1_h[:, s * CH:(s + 1) * CH]))
                P.op("sync", f, reads=[xr], writes=[xr], dma=True, stream="x1b")

                for ec in range(4):
                    def f(t, ec=ec):
                        r = None
                        for kc in range(KC):
                            r = t.matmul(pqs[:], wq_s[:, kc * 512 + ec * 128: kc * 512 + (ec + 1) * 128],
                                         x1b[:, xo + kc * CH: xo + (kc + 1) * CH],
                                         start=(kc == 0), stop=(kc == KC - 1))
                        return r
                    P.op("tensor", f, reads=["wq", xr], writes=["pqs"])
                    def f(sc, ec=ec):
                        return sc.activation(q_s[:, ec * CH:(ec + 1) * CH], pqs[:], AF.Copy)
                    P.op("scalar", f, reads=["pqs", "q"], writes=["pqs", "q"])

                for t4 in range(4):
                    def f(t, t4=t4):
                        r = None
                        for p in range(2):
                            for h in range(HEADS):
                                jb = p * 4 + h
                                ecq = 2 * p + h // 2
                                r = t.matmul(pqs[:, jb * 64:(jb + 1) * 64],
                                             q_s[:, ecq * CH + t4 * 128: ecq * CH + (t4 + 1) * 128],
                                             keys_sb[:, jb * 64:(jb + 1) * 64],
                                             start=True, stop=True)
                        return r
                    P.op("tensor", f, reads=["keys", "q", "pqs"], writes=["pqs"])
                    def f(sc, t4=t4):
                        return sc.activation(sims[:], pqs[:], AF.Copy)
                    P.op("scalar", f, reads=["pqs", "sims"], writes=["pqs", "sims"])
                    def f(sc, t4=t4):
                        return sc.activation(expuv[:, t4 * 512:(t4 + 1) * 512], sims[:], AF.Exp)
                    P.op("scalar", f, reads=["sims", "expuv%d" % t4], writes=["expuv%d" % t4])

                    def f(v):
                        r = None
                        for jb in range(8):
                            r = v.max(out=mx8[:, jb * 8:(jb + 1) * 8], in_=sims[:, jb * 64:(jb + 1) * 64])
                        return r
                    P.op("vector", f, reads=["sims", "mx8"], writes=["mx8"])
                    P.op("scalar", lambda sc: sc.activation(emx[:], mx8[:], AF.Exp),
                         reads=["mx8", "emx"], writes=["emx"])

                    def f(v):
                        r = None
                        for h in range(HEADS):
                            xb_ = emx[:, h * 8:(h + 1) * 8]
                            yb_ = emx[:, (4 + h) * 8:(4 + h) * 8 + 8]
                            r = v.tensor_tensor(r88[:, h * 64:(h + 1) * 64]
                                                .rearrange("p (a b) -> p a b", a=8),
                                                xb_[:, :, None].to_broadcast([128, 8, 8]),
                                                yb_[:, None, :].to_broadcast([128, 8, 8]), ALU.mult)
                        return r
                    P.op("vector", f, reads=["emx", "r88"], writes=["r88"])
                    def f(v):
                        r = None
                        for h in range(HEADS):
                            r = v.max(out=r8[:, h * 8:(h + 1) * 8], in_=r88[:, h * 64:(h + 1) * 64])
                        return r
                    P.op("vector", f, reads=["r88", "r8"], writes=["r8"])
                    def f(v):
                        r = None
                        for h in range(HEADS):
                            r = v.reduce_sum(den4[:, h:h + 1], r8[:, h * 8:(h + 1) * 8], axis=AX.X)
                        return r
                    P.op("vector", f, reads=["r8", "den4"], writes=["den4"])

                    def f(v, t4=t4):
                        v.reciprocal(rdt[:, t4 * 4:(t4 + 1) * 4], den4[:])
                        r = None
                        for h in range(HEADS):
                            r = v.tensor_scalar(thr4[:, t4 * 4 + h: t4 * 4 + h + 1],
                                                r8[:, h * 8 + 7: h * 8 + 8], MARGIN, None, ALU.mult)
                        return r
                    P.op("vector", f, reads=["den4", "r8", "rdt%d" % t4, "thr%d" % t4],
                         writes=["rdt%d" % t4, "thr%d" % t4])

                    def f(v, t4=t4):
                        r = None
                        for jb in range(8):
                            blk = expuv[:, t4 * 512 + jb * 64: t4 * 512 + (jb + 1) * 64]
                            r = v.scalar_tensor_tensor(
                                blk, sims[:, jb * 64:(jb + 1) * 64],
                                mx8[:, jb * 8 + 7: jb * 8 + 8],
                                blk, ALU.is_ge, ALU.mult)
                        return r
                    P.op("vector", f, reads=["sims", "mx8", "expuv%d" % t4],
                         writes=["expuv%d" % t4])

                if s > 0:
                    emit_tail(s - 1)

                gbanks = [(pg[0][:], "pg0"), (pg[1][:], "pg1"),
                          (pbig[:, 0:512], "pbig0"), (pbig[:, 512:1024], "pbig1")]
                for ec in range(8):
                    wb = "wstr%d" % (ec % 2)
                    off = (ec % 2) * KC * 512

                    def f(sy, ec=ec, off=off):
                        dst = wstr[:, off:off + KC * 512]
                        return sy.dma_start(dst.rearrange("p (k c) -> p k c", k=KC),
                                            kview(det_b[:, ec * 512:(ec + 1) * 512]))
                    P.op("sync", f, reads=["det_b", wb], writes=[wb], dma=True, stream=wb)
                    for t4 in range(4):
                        bank, pb = gbanks[(ec * 4 + t4) % 4]

                        def f(t, ec=ec, t4=t4, off=off, bank=bank):
                            r = None
                            for kc in range(KC):
                                r = t.matmul(bank,
                                             x1b[:, xo + kc * CH + t4 * 128: xo + kc * CH + (t4 + 1) * 128],
                                             wstr[:, off + kc * 512: off + (kc + 1) * 512],
                                             start=(kc == 0), stop=(kc == KC - 1))
                            return r
                        P.op("tensor", f, reads=[wb, xr], writes=[pb])
                        def f(sc, ec=ec, t4=t4, bank=bank):
                            return sc.activation(gall[:, t4 * E + ec * 512: t4 * E + (ec + 1) * 512],
                                                 bank, AF.Copy)
                        P.op("scalar", f, reads=[pb, "gall%d_%d" % (t4, ec)],
                             writes=[pb, "gall%d_%d" % (t4, ec)])

                for icg in range(8):
                    db = "wstr%d" % (icg % 2)
                    doff = (icg % 2) * KC * 512

                    def f(sy, icg=icg, doff=doff):
                        dst = dstr[:, doff:doff + KC * 512]
                        return sy.dma_start(
                            dst.rearrange("p (k i c) -> p k i c", k=KC, i=4),
                            wupt_b[:, icg * 512:(icg + 1) * 512]
                            .rearrange("(k p) (i c) -> p k i c", p=128, i=4))
                    P.op("sync", f, reads=["wupt_b", db], writes=[db], dma=True, stream=db)

                    for i4 in range(4):
                        ic = icg * 4 + i4
                        bank, pb = gbanks[ic % 4]

                        def f(t, i4=i4, ic=ic, doff=doff, bank=bank):
                            r = None
                            for kc in range(KC):
                                r = t.matmul(bank,
                                             dstr[:, doff + kc * 512 + i4 * 128: doff + kc * 512 + (i4 + 1) * 128],
                                             x1b[:, xo + kc * CH: xo + (kc + 1) * CH],
                                             start=(kc == 0), stop=(kc == KC - 1))
                            return r
                        P.op("tensor", f, reads=[db, xr], writes=[pb])

                        def f(sc, ic=ic, bank=bank):
                            return sc.activation(big32[:, ic * CH:(ic + 1) * CH], bank, AF.Silu)
                        P.op("scalar", f, reads=[pb, "big32"], writes=[pb, "big32"])

                dn1 = [(pqs, "pqs"), (pg[0], "pg0"), (pg[1], "pg1")]
                for dcg in range(2):
                    for icg in range(8):
                        ub_ = "ustr%d" % (icg % 2)
                        uoff = (icg % 2) * 4 * 512

                        def f(sy, dcg=dcg, icg=icg, uoff=uoff):
                            return sy.dma_start(
                                ustr[:, uoff:uoff + 4 * 512].rearrange("p (i c) -> p i c", i=4),
                                wdnt_b[icg * 512:(icg + 1) * 512, dcg * 512:(dcg + 1) * 512]
                                .rearrange("(i p) c -> p i c", p=128))
                        P.op("sync", f, reads=["wdnt_b", ub_], writes=[ub_], dma=True, stream=ub_)

                        def f(t, dcg=dcg, icg=icg, uoff=uoff):
                            r = None
                            for i4 in range(4):
                                ic = icg * 4 + i4
                                for dc4 in range(4 if dcg == 0 else 3):
                                    dst = pbig[:, dc4 * 512:(dc4 + 1) * 512] if dcg == 0 \
                                        else dn1[dc4][0][:]
                                    r = t.matmul(dst,
                                                 ustr[:, uoff + i4 * 512 + dc4 * 128: uoff + i4 * 512 + (dc4 + 1) * 128],
                                                 big32[:, ic * CH:(ic + 1) * CH],
                                                 start=(ic == 0), stop=(ic == 31))
                            return r
                        wl = ["pbig"] if dcg == 0 else [n for _, n in dn1]
                        P.op("tensor", f, reads=[ub_, "big32"] + wl, writes=wl)

                def f(sy):
                    return sy.dma_start(
                        ustr[:, 0:4096].rearrange("p (i c) -> p i c", i=32),
                        wdnt_b[:, 7 * 128:8 * 128].rearrange("(i p) c -> p i c", p=128))
                P.op("sync", f, reads=["ustr0", "ustr1"], writes=["ustr0", "ustr1"],
                     dma=True, stream="ustr0")

                it = 0
                pend_add = None
                for t4 in range(4):
                    for h in range(HEADS):
                        for ecc in range(4):
                            pp = it % 2
                            it += 1

                            def f(e, t4=t4, h=h, ecc=ecc, pp=pp):
                                xb_ = expuv[:, t4 * 512 + h * 64 + ecc * 16: t4 * 512 + h * 64 + (ecc + 1) * 16]
                                yb_ = expuv[:, t4 * 512 + (4 + h) * 64: t4 * 512 + (4 + h) * 64 + 64]
                                return e.tensor_tensor(p1[pp][:].rearrange("p (a b) -> p a b", a=16),
                                                       xb_[:, :, None].to_broadcast([128, 16, 64]),
                                                       yb_[:, None, :].to_broadcast([128, 16, 64]), ALU.mult)
                            P.op("gpsimd", f,
                                 reads=["expuv%d" % t4, "p1%d" % pp], writes=["p1%d" % pp])
                            if pend_add is not None:
                                pend_add()
                                pend_add = None

                            def f(v, t4=t4, h=h, pp=pp):
                                return v.scalar_tensor_tensor(q2[:], p1[pp][:],
                                                              thr4[:, t4 * 4 + h: t4 * 4 + h + 1],
                                                              p1[pp][:], ALU.is_ge, ALU.mult)
                            P.op("vector", f, reads=["p1%d" % pp, "thr%d" % t4, "q2"],
                                 writes=["q2"])
                            def f(v, t4=t4, ecc=ecc, pp=pp):
                                return v.tensor_tensor(argc[pp][:], q2[:],
                                                       gall[:, t4 * E + ecc * 1024: t4 * E + (ecc + 1) * 1024],
                                                       ALU.mult)
                            P.op("vector", f,
                                 reads=["q2", "gall%d_%d" % (t4, 2 * ecc), "gall%d_%d" % (t4, 2 * ecc + 1),
                                        "argc%d" % pp],
                                 writes=["argc%d" % pp])

                            sres = "ss%d_%d" % (t4, ecc)
                            if h == 0:
                                def f(sc, t4=t4, ecc=ecc, pp=pp):
                                    return sc.activation(ssum[:, t4 * E + ecc * 1024: t4 * E + (ecc + 1) * 1024],
                                                         argc[pp][:], AF.Silu, scale=rdt[:, t4 * 4: t4 * 4 + 1])
                                P.op("scalar", f, reads=["argc%d" % pp, "rdt%d" % t4, sres], writes=[sres])
                            else:
                                def f(sc, t4=t4, h=h, pp=pp):
                                    return sc.activation(wsil[pp][:], argc[pp][:], AF.Silu,
                                                         scale=rdt[:, t4 * 4 + h: t4 * 4 + h + 1])
                                P.op("scalar", f, reads=["argc%d" % pp, "rdt%d" % t4, "wsil%d" % pp],
                                     writes=["wsil%d" % pp])
                                def mk_add(t4=t4, ecc=ecc, pp=pp, sres=sres):
                                    def f(g):
                                        dst = ssum[:, t4 * E + ecc * 1024: t4 * E + (ecc + 1) * 1024]
                                        return g.tensor_tensor(dst, dst, wsil[pp][:], ALU.add)
                                    P.op("vector", f, reads=["wsil%d" % pp, sres], writes=[sres])
                                pend_add = mk_add
                if pend_add is not None:
                    pend_add()
                    pend_add = None

                for dc4 in range(4):
                    def f(v, dc4=dc4):
                        return v.tensor_tensor(zf2[:, dc4 * CH:(dc4 + 1) * CH],
                                               pbig[:, dc4 * 512:(dc4 + 1) * 512],
                                               x1b[:, xo + dc4 * CH: xo + (dc4 + 1) * CH], ALU.add)
                    P.op("vector", f, reads=["pbig", xr, "zf2"], writes=["pbig", "zf2"])
                for dc4 in range(3):
                    bank, bn = dn1[dc4]

                    def f(v, dc4=dc4, bank=bank):
                        oc = 4 + dc4
                        return v.tensor_tensor(zf2[:, oc * CH:(oc + 1) * CH],
                                               bank[:],
                                               x1b[:, xo + oc * CH: xo + (oc + 1) * CH], ALU.add)
                    P.op("vector", f, reads=[bn, xr, "zf2"], writes=[bn, "zf2"])

                def f(t):
                    r = None
                    for ic in range(32):
                        r = t.matmul(pbig[:, 0:512],
                                     ustr[:, ic * 128:(ic + 1) * 128],
                                     big32[:, ic * CH:(ic + 1) * CH],
                                     start=(ic == 0), stop=(ic == 31))
                    return r
                P.op("tensor", f, reads=["ustr0", "ustr1", "big32", "pbig"], writes=["pbig"])

                def f(v):
                    return v.tensor_tensor(zf2[:, 7 * CH:8 * CH], pbig[:, 0:512],
                                           x1b[:, xo + 7 * CH: xo + 8 * CH], ALU.add)
                P.op("vector", f, reads=["pbig", xr, "zf2"], writes=["pbig", "zf2"])

                def emit_eup(dcg, ec):
                    ub_ = "uestr%d" % (ec % 4)
                    uoff = (ec % 4) * 512

                    def f(sy, dcg=dcg, ec=ec, uoff=uoff):
                        return sy.dma_start(uestr[:, uoff:uoff + 512],
                                            uem_b[ec * 128:(ec + 1) * 128, dcg * 512:(dcg + 1) * 512])
                    P.op("sync", f, reads=["uem_b", ub_], writes=[ub_], dma=True, stream=ub_)

                    def f(t, ec=ec, uoff=uoff):
                        r = None
                        for dc4 in range(4):
                            r = t.matmul(pbig[:, dc4 * 512:(dc4 + 1) * 512],
                                         uestr[:, uoff + dc4 * 128: uoff + (dc4 + 1) * 128],
                                         big32[:, ec * CH:(ec + 1) * CH],
                                         start=(ec == 0), stop=(ec == 31))
                        return r
                    P.op("tensor", f, reads=[ub_, "big32"], writes=["pbig"])

                for grp in range(4):
                    for t4 in range(4):
                        def f(t, t4=t4, grp=grp):
                            r = None
                            for j in range(8):
                                eb = grp * 8 + j
                                r = t.transpose(ptr[:, j * 128:(j + 1) * 128],
                                                ssum[:, t4 * E + eb * 128: t4 * E + (eb + 1) * 128],
                                                id_b[:])
                            return r
                        P.op("tensor", f, reads=["idb", "ptr"] + ["ss%d_%d" % (t4, e) for e in range(4)],
                             writes=["ptr"])

                        def f(sc, t4=t4, grp=grp):
                            dst = big32[:].rearrange("p (e t) -> p e t", e=32)[:, grp * 8:(grp + 1) * 8,
                                                                              t4 * 128:(t4 + 1) * 128]
                            return sc.activation(dst, ptr[:].rearrange("p (j c) -> p j c", j=8), AF.Copy)
                        P.op("scalar", f, reads=["ptr", "big32"], writes=["ptr", "big32"])
                    for ec in range(8 * grp, 8 * grp + 8):
                        emit_eup(0, ec)
                for dc4 in range(4):
                    def f(v, dc4=dc4):
                        dst = zf2[:, dc4 * CH:(dc4 + 1) * CH]
                        return v.tensor_tensor(dst, dst, pbig[:, dc4 * 512:(dc4 + 1) * 512],
                                               ALU.add)
                    P.op("vector", f, reads=["pbig", "zf2"], writes=["pbig", "zf2"])
                for ec in range(32):
                    emit_eup(1, ec)
                for dc4 in range(4):
                    def f(v, dc4=dc4):
                        oc = 4 + dc4
                        dst = zf2[:, oc * CH:(oc + 1) * CH]
                        return v.tensor_tensor(dst, dst, pbig[:, dc4 * 512:(dc4 + 1) * 512],
                                               ALU.add)
                    P.op("vector", f, reads=["pbig", "zf2"], writes=["pbig", "zf2"])

                def f(sc):
                    r = None
                    for oc in range(KC):
                        r = sc.activation(zsq[:, oc * CH:(oc + 1) * CH],
                                          zf2[:, oc * CH:(oc + 1) * CH], AF.Square)
                    return r
                P.op("scalar", f, reads=["zf2", "zsq"], writes=["zsq"])
                P.flush()
        emit_tail(NCH - 1)
        P.flush()

    P.barrier()
    return nc, ctx


def shard_inputs(inputs):
    f32 = lambda x: np.ascontiguousarray(np.asarray(x, dtype=np.float32))
    u = f32(inputs["L0_attn_u"])
    Bm = f32(inputs["L0_attn_Bmat"])
    Cm = f32(inputs["L0_attn_Cmat"])
    Dm = f32(inputs["L0_attn_Dmat"])
    A = f32(inputs["L0_attn_A_diag"])
    h0 = f32(inputs["L0_attn_h0"])
    gam = f32(inputs["L0_attn_ln_gamma"])
    bet = f32(inputs["L0_attn_ln_beta"])
    Wup = f32(inputs["L0_ffn_W_up"])
    Wdn = f32(inputs["L0_ffn_W_down"])
    Wq = f32(inputs["L0_ffn_W_q"])
    keys = f32(inputs["L0_ffn_keys"])
    de = f32(inputs["L0_ffn_down_embed"])
    ue = f32(inputs["L0_ffn_up_embed"])

    fm = lambda v: np.ascontiguousarray(v.reshape(KC, 128).T)
    kp = np.zeros((128, 512), np.float32)
    for p in range(2):
        for h in range(HEADS):
            jb = p * 4 + h
            base = (h % 2) * 64
            kp[base:base + 64, jb * 64:(jb + 1) * 64] = keys[h, :, p, :].T

    shared = {
        "bmt": np.ascontiguousarray(Bm.T), "cmt": np.ascontiguousarray(Cm.T),
        "dmt": np.ascontiguousarray(Dm.T), "wqt": np.ascontiguousarray(Wq.T),
        "keyspad": kp, "det": np.ascontiguousarray(de.T), "uem": ue,
        "wupt": np.ascontiguousarray(Wup.T), "wdnt": np.ascontiguousarray(Wdn.T),
        "identm": np.eye(128, dtype=np.float32),
        "afmv": fm(A), "h0fm": fm(h0), "gfm": fm(gam), "bfm": fm(bet),
    }
    zeros_dt = np.zeros((D, T), np.float32)
    zkc = np.zeros((128, KC), np.float32)
    in_maps = []
    for c in range(8):
        b, half = c // 2, c % 2
        m = dict(shared)
        m["u_own"] = np.ascontiguousarray(u[b, half * T:(half + 1) * T].T)
        m["u_pre"] = np.ascontiguousarray(u[b, :T].T) if half else zeros_dt
        m["selv"] = np.full((128, 1), float(half), np.float32)
        m["h0v2"] = zkc if half else fm(h0)
        in_maps.append(m)
    return in_maps


def kernel(**inputs):
    in_maps = shard_inputs(inputs)
    nc, ctx = build_nc(debug=False)
    ctx.close()
    res = run_bass_kernel_spmd(nc, in_maps, core_ids=list(range(8)))
    out = np.zeros((4, 4096, D), np.float32)
    for c in range(8):
        b, half = c // 2, c % 2
        out[b, half * T:(half + 1) * T] = np.asarray(res.results[c]["out"]).T
    return out
